# version 04
# speedup vs baseline: 1.1407x; 1.1294x over previous
import numpy as np
from contextlib import ExitStack

import concourse.bass as bass
import concourse.mybir as mybir
from concourse.bass_utils import run_bass_kernel_spmd

F32 = mybir.dt.float32
BF16 = mybir.dt.bfloat16
AF = mybir.ActivationFunctionType
ALU = mybir.AluOpType
AX = mybir.AxisListType

NCORES = 8
D = 1024
TOK = 2048
NT = TOK // 128


class Plan:

    ENGINES = ("sync", "gpsimd", "vector", "scalar", "tensor")

    def __init__(self, sem):
        self.ops = {k: [] for k in self.ENGINES}
        self.counts = {}
        self.marks = {}
        self.sem = sem

    def emit(self, eng, fn):
        self.ops[eng].append(fn)

    def inc(self, s, n=1):
        c = self.counts.get(s, 0) + n
        self.counts[s] = c
        return c

    def cur(self, s):
        return self.counts.get(s, 0)

    def vop(self, waits, fn):
        n_prev = self.cur("v")
        sem = self.sem

        def closure(v, waits=tuple(waits), fn=fn, n_prev=n_prev):
            if n_prev:
                v.wait_ge(sem["v"], n_prev)
            for s_, val in waits:
                v.wait_ge(sem[s_], val)
            fn(v).then_inc(sem["v"], 1)
        self.emit("vector", closure)
        return self.inc("v", 1)

    def sop(self, waits, fn):
        n_prev = self.cur("s")
        sem = self.sem

        def closure(s, waits=tuple(waits), fn=fn, n_prev=n_prev):
            if n_prev:
                s.wait_ge(sem["s"], n_prev)
            for s_, val in waits:
                s.wait_ge(sem[s_], val)
            fn(s).then_inc(sem["s"], 1)
        self.emit("scalar", closure)
        return self.inc("s", 1)

    def replay(self, eng, handle):
        for fn in self.ops[eng]:
            fn(handle)



class SubPlan:

    def __init__(self, parent, prefix, localsem):
        self.parent = parent
        self.prefix = prefix
        self.sem = localsem
        self.marks = {}

    def _t(self, n):
        return n if n in ("v", "s", "t", "g") else self.prefix + n

    def emit(self, eng, fn):
        self.parent.emit(eng, fn)

    def inc(self, n, k=1):
        return self.parent.inc(self._t(n), k)

    def cur(self, n):
        return self.parent.cur(self._t(n))

    def vop(self, waits, fn):
        n_prev = self.cur("v")
        sem = self.sem

        def closure(v, waits=tuple(waits), fn=fn, n_prev=n_prev):
            if n_prev:
                v.wait_ge(sem["v"], n_prev)
            for s_, val in waits:
                v.wait_ge(sem[s_], val)
            fn(v).then_inc(sem["v"], 1)
        self.parent.emit("vector", closure)
        return self.inc("v", 1)

    def sop(self, waits, fn):
        n_prev = self.cur("s")
        sem = self.sem

        def closure(s, waits=tuple(waits), fn=fn, n_prev=n_prev):
            if n_prev:
                s.wait_ge(sem["s"], n_prev)
            for s_, val in waits:
                s.wait_ge(sem[s_], val)
            fn(s).then_inc(sem["s"], 1)
        self.parent.emit("scalar", closure)
        return self.inc("s", 1)


def _mk_ps(nc, es):
    return {"tb0": es.enter_context(nc.psum_tensor("tb0", [128, 1024], BF16)),
            "tb1": es.enter_context(nc.psum_tensor("tb1", [128, 1024], BF16)),
            **{f"pf{i}": es.enter_context(nc.psum_tensor(f"pf{i}", [128, 512], F32))
               for i in range(6)}}


def _replay(nc, p):
    with nc.Block() as block:
        @block.sync
        def _(sy):
            p.replay("sync", sy)

        @block.gpsimd
        def _(g):
            p.replay("gpsimd", g)

        @block.vector
        def _(v):
            p.replay("vector", v)

        @block.scalar
        def _(s):
            p.replay("scalar", s)

        @block.tensor
        def _(t):
            p.replay("tensor", t)


def _mk_subplan(parent, prefix, names, es, nc, shared):
    localsem = dict(shared)
    for n in names:
        localsem[n] = es.enter_context(nc.semaphore(prefix + n))
    return SubPlan(parent, prefix, localsem), localsem



def emit_b(nc, p, sem, dram, ps, nt=NT, e_used=8):

    TOKb = nt * 128
    x2 = dram["x2"]
    wr = dram["wr"]
    br = dram["br"]
    lng = dram["lng"]
    lnb = dram["lnb"]
    w1 = dram["w1"]
    b1pl = dram["b1pl"]
    w2 = dram["w2"]
    b2 = dram["b2"]
    eye = dram["eye"]
    x3 = dram["x3"]
    xnd = dram["xnd"]

    E, FC, ET = e_used, 32, TOKb // 512

    es = ExitStack()
    with es:
        w1_sb = es.enter_context(nc.sbuf_tensor("w1_sb", [128, 8 * 4096], BF16))
        w2_sb = es.enter_context(nc.sbuf_tensor("w2_sb", [128, 32 * 1024], BF16))
        aB = es.enter_context(nc.sbuf_tensor("aB", [128, 25600], BF16))
        h_sb = aB[:, 0:16384]
        ybf = aB[:, 0:1024]
        xnbf = aB[:, 1024:2048]
        yfm = aB[:, 2048:3072]
        xnfm = aB[:, 3072:4096]
        xn_sb = [aB[:, 16384:20480], aB[:, 20480:24576]]
        b2row = aB[0:1, 24576:25600]
        aF = es.enter_context(nc.sbuf_tensor("aF", [128, 5120], F32))
        xt = aF[:, 0:1024]
        yt = aF[:, 1024:2048]
        t2 = aF[:, 2048:3072]
        gb = aF[:, 3072:4096]
        bb = aF[:, 4096:5120]
        lngrow = aF[0:1, 2048:3072]
        lnbrow = aF[0:1, 3072:4096]
        otile = [aF[:, 0:1024], aF[:, 1024:2048]]

        small = es.enter_context(nc.sbuf_tensor("small", [128, 16], F32))
        sum1 = small[:, 0:1]
        mean = small[:, 1:2]
        ex2 = small[:, 2:3]
        m2 = small[:, 3:4]
        var = small[:, 4:5]
        lnv = small[:, 5:6]
        rstd = small[:, 6:7]
        sumsq = small[:, 7:8]
        negm = small[:, 8:9]
        m1p = small[:, 9:10]
        ssum = small[:, 10:11]
        gts = small[:, 11:12]
        epsv = small[:, 12:13]
        nhalf = small[:, 13:14]
        geff = es.enter_context(nc.sbuf_tensor("geff", [128, nt * 8], F32))
        lgr = es.enter_context(nc.sbuf_tensor("lgr", [128, 8], F32))
        oh = es.enter_context(nc.sbuf_tensor("oh", [128, 8], F32))
        lge = oh
        eyeb = es.enter_context(nc.sbuf_tensor("eyeb", [128, 128], BF16))
        b1pl_sb = es.enter_context(nc.sbuf_tensor("b1pl_sb", [128, 256], F32))
        wr_sb = es.enter_context(nc.sbuf_tensor("wr_sb", [128, 64], BF16))
        brrow = es.enter_context(nc.sbuf_tensor("brrow", [1, 8], BF16))
        onesb = es.enter_context(nc.sbuf_tensor("onesb", [1, 512], BF16))
        onesf = es.enter_context(nc.sbuf_tensor("onesf", [1, 128], F32))

        tps = ps["tb0"]
        rps = ps["pf5"]
        hp = [ps["pf0"], ps["pf1"]]
        op = [ps["pf2"], ps["pf3"], ps["pf4"], ps["pf5"]]


        def g_init(g):
            g.dma_start(out=eyeb[:, :], in_=eye[:, :]).then_inc(sem["ldi"], 16)
            g.dma_start(out=wr_sb[:, :].rearrange("p (c e) -> p c e", c=8),
                        in_=wr.rearrange("(c p) e -> p c e", p=128)).then_inc(sem["ldi"], 16)
            g.dma_start(out=brrow[:, :], in_=br[:, :]).then_inc(sem["ldi"], 16)
            g.memset(onesb[:, :], 1.0)
            g.memset(onesf[:, :], 1.0)
            g.memset(epsv, 1e-5)
            g.memset(nhalf, -0.5).then_inc(sem["g"], 1)
            g.dma_start(out=lngrow, in_=lng[:, :]).then_inc(sem["ldi"], 16)
            g.dma_start(out=lnbrow, in_=lnb[:, :]).then_inc(sem["ldi"], 16)
            g.dma_start(out=b1pl_sb[:, :], in_=b1pl[:, :]).then_inc(sem["ldi"], 16)
        p.emit("gpsimd", g_init)
        ldi_all = p.inc("ldi", 16 * 6)
        g_init_done = p.inc("g", 1)

        def t_bcast_fn(t):
            t.wait_ge(sem["ldi"], ldi_all)
            t.wait_ge(sem["g"], g_init_done)
            t.matmul(rps[:, 0:512], onesf[:, :], lngrow[:, 0:512], start=True, stop=True)
            t.matmul(hp[0][:, 0:512], onesf[:, :], lngrow[:, 512:1024], start=True, stop=True)
            t.matmul(hp[1][:, 0:512], onesf[:, :], lnbrow[:, 0:512], start=True, stop=True)
            t.matmul(op[0][:, 0:512], onesf[:, :], lnbrow[:, 512:1024],
                     start=True, stop=True).then_inc(sem["t"], 1)
        p.emit("tensor", t_bcast_fn)
        t_bcast = p.inc("t", 1)

        p.vop([("t", t_bcast)], lambda v: v.tensor_copy(gb[:, 0:512], rps[:, 0:512]))
        p.vop([], lambda v: v.tensor_copy(gb[:, 512:1024], hp[0][:, 0:512]))
        p.vop([], lambda v: v.tensor_copy(bb[:, 0:512], hp[1][:, 0:512]))
        v_bcast = p.vop([], lambda v: v.tensor_copy(bb[:, 512:1024], op[0][:, 0:512]))

        y_done = [0] * nt
        sq2_done = [0] * nt
        var2_done = [0] * nt
        ybf_done = [0] * nt
        xnbf_done = [0] * nt
        ytr_done = [0] * nt
        yfm_done = [0] * nt
        router_mm_done = [0] * nt
        router_done = [0] * nt
        xntr_done = [0] * nt
        xnfm_done = [0] * nt
        sti_done = [0] * nt
        stxn_done = [0] * nt
        ldx_done = [0] * nt

        for i in range(nt):
            r0 = i * 128

            def f(sy, i=i, r0=r0):
                if i > 0:
                    sy.wait_ge(sem["s"], sq2_done[i - 1])
                    sy.wait_ge(sem["v"], y_done[i - 1])
                    sy.wait_ge(sem["sti"], sti_done[i - 1])
                sy.dma_start(out=xt, in_=x2[r0:r0 + 128, :]).then_inc(sem["ldx"], 16)
            p.emit("sync", f)
            ldx_done[i] = p.inc("ldx", 16)

            def f(sy, r0=r0, i=i):
                sy.wait_ge(sem["ldx"], ldx_done[i])
                sy.dma_start(out=x3[r0:r0 + 128, :], in_=xt).then_inc(sem["sti"], 16)
            p.emit("sync", f)
            sti_done[i] = p.inc("sti", 16)

            w0 = [("ldx", ldx_done[i])] + ([("v", v_bcast)] if i == 0 else [])
            p.vop(w0, lambda v: v.reduce_sum(sum1, xt, axis=AX.X))
            p.vop([], lambda v: v.tensor_scalar_mul(mean, sum1, 1.0 / D))

            sw = [("ldx", ldx_done[i])] + ([("ldi", ldi_all), ("g", g_init_done)] if i == 0 else
                                           [("v", var2_done[i - 1])])
            sq1 = p.sop(sw, lambda s: s.activation(yt, xt, AF.Square, accum_out=sumsq))
            p.vop([("s", sq1)], lambda v: v.tensor_scalar_mul(ex2, sumsq, 1.0 / D))
            p.vop([], lambda v: v.tensor_mul(m2, mean, mean))
            var_d = p.vop([], lambda v: v.tensor_sub(var, ex2, m2))
            p.sop([("v", var_d)], lambda s: s.activation(lnv, var, AF.Ln, bias=epsv))
            rstd1 = p.sop([], lambda s: s.activation(rstd, lnv, AF.Exp, scale=nhalf))
            y_done[i] = p.vop([("s", rstd1)],
                              lambda v: v.tensor_scalar(yt, xt, mean, rstd,
                                                        ALU.subtract, ALU.mult))
            ybf_done[i] = p.vop([], lambda v: v.tensor_copy(ybf, yt))

            p.vop([], lambda v: v.reduce_sum(sum1, yt, axis=AX.X))
            p.vop([], lambda v: v.tensor_scalar_mul(mean, sum1, 1.0 / D))
            sq2_done[i] = p.sop([("v", y_done[i])],
                                lambda s: s.activation(t2, yt, AF.Square, accum_out=sumsq))
            p.vop([("s", sq2_done[i])], lambda v: v.tensor_scalar_mul(ex2, sumsq, 1.0 / D))
            p.vop([], lambda v: v.tensor_mul(m2, mean, mean))
            var2_done[i] = p.vop([], lambda v: v.tensor_sub(var, ex2, m2))
            p.sop([("v", var2_done[i])], lambda s: s.activation(lnv, var, AF.Ln, bias=epsv))
            rstd2 = p.sop([], lambda s: s.activation(rstd, lnv, AF.Exp, scale=nhalf))
            p.vop([("s", rstd2)],
                  lambda v: v.tensor_scalar(t2, yt, mean, rstd, ALU.subtract, ALU.mult))
            p.vop([], lambda v: v.tensor_mul(t2, t2, gb))
            xnbf_done[i] = p.vop([], lambda v: v.tensor_add(xnbf, t2, bb))

            def f(t, i=i):
                t.wait_ge(sem["v"], ybf_done[i])
                if i > 0:
                    t.wait_ge(sem["v"], xnfm_done[i - 1])
                for k in range(8):
                    ins = t.transpose(tps[:, k * 128:(k + 1) * 128],
                                      ybf[:, k * 128:(k + 1) * 128], eyeb[:, :])
                ins.then_inc(sem["t"], 1)
            p.emit("tensor", f)
            ytr_done[i] = p.inc("t", 1)

            yfm_done[i] = p.vop([("t", ytr_done[i])],
                                lambda v: v.tensor_copy(yfm, tps[:, :]))

            def f(t, i=i):
                t.wait_ge(sem["v"], yfm_done[i])
                if i > 0:
                    t.wait_ge(sem["v"], router_done[i - 1])
                t.matmul(rps[:, 0:8], onesb[:, 0:128], brrow[:, :], start=True, stop=False)
                for c in range(8):
                    ins = t.matmul(rps[:, 0:8], yfm[:, c * 128:(c + 1) * 128],
                                   wr_sb[:, c * 8:(c + 1) * 8],
                                   start=False, stop=(c == 7))
                ins.then_inc(sem["t"], 1)
            p.emit("tensor", f)
            router_mm_done[i] = p.inc("t", 1)

            p.vop([("t", router_mm_done[i])], lambda v: v.tensor_copy(lgr[:, :], rps[:, 0:8]))
            p.vop([], lambda v: v.reduce_max(negm, lgr[:, :], axis=AX.X, negate=True))
            negm_d = p.vop([], lambda v: v.reduce_max(m1p, lgr[:, :], axis=AX.X))
            exp_d = p.sop([("v", negm_d)],
                          lambda s: s.activation(lge[:, :], lgr[:, :], AF.Exp, bias=negm,
                                                 accum_out=ssum))
            p.vop([("s", exp_d)], lambda v: v.reciprocal(gts, ssum))
            p.vop([], lambda v: v.tensor_scalar(oh[:, :], lgr[:, :], m1p, None, ALU.is_equal))
            router_done[i] = p.vop(
                [], lambda v, i=i: v.tensor_scalar(geff[:, i * 8:(i + 1) * 8], oh[:, :],
                                                   gts, None, ALU.mult))

            def f(t, i=i):
                t.wait_ge(sem["v"], xnbf_done[i])
                t.wait_ge(sem["v"], yfm_done[i])
                for k in range(8):
                    ins = t.transpose(tps[:, k * 128:(k + 1) * 128],
                                      xnbf[:, k * 128:(k + 1) * 128], eyeb[:, :])
                ins.then_inc(sem["t"], 1)
            p.emit("tensor", f)
            xntr_done[i] = p.inc("t", 1)

            wv = [("t", xntr_done[i])] + ([("stxn", stxn_done[i - 1])] if i > 0 else [])
            xnfm_done[i] = p.vop(wv, lambda v: v.tensor_copy(xnfm, tps[:, :]))

            def f(sy, i=i):
                sy.wait_ge(sem["v"], xnfm_done[i])
                sy.dma_start(
                    out=xnd.rearrange("(c p) n -> p c n", p=128)[:, :, i * 128:(i + 1) * 128],
                    in_=xnfm.rearrange("p (c n) -> p c n", c=8)).then_inc(sem["stxn"], 16)
            p.emit("sync", f)
            stxn_done[i] = p.inc("stxn", 16)

        def emit_w1_load(e):
            def f(g, e=e):
                for c in range(8):
                    g.dma_start(out=w1_sb[:, c * 4096:(c + 1) * 4096],
                                in_=w1[e, c * 128:(c + 1) * 128, :]).then_inc(sem["ldw1"], 16)
            p.emit("gpsimd", f)
            return p.inc("ldw1", 16 * 8)

        def emit_w2_load(e):
            def f(g, e=e):
                for j4 in range(8):
                    g.dma_start(
                        out=w2_sb[:, j4 * 4096:(j4 + 1) * 4096].rearrange("p (gg d) -> p gg d", gg=4),
                        in_=w2[e, j4 * 512:(j4 + 1) * 512, :].rearrange("(gg p) d -> p gg d", p=128),
                    ).then_inc(sem["ldw2"], 16)
                g.dma_start(out=b2row, in_=b2[e:e + 1, :]).then_inc(sem["ldw2"], 16)
            p.emit("gpsimd", f)
            return p.inc("ldw2", 16 * 9)

        w1rdy = [0] * E
        w2rdy = [0] * E
        w1rdy[0] = emit_w1_load(0)
        w2rdy[0] = emit_w2_load(0)

        th_after = {}
        sh_after = {}
        to_after = {}
        vo_after = {}
        acc_after = {}

        for e in range(E):
            for t_ in range(ET):
                j = e * ET + t_

                lsem = "ldxn" if j % 2 == 0 else "ldxn1"

                def f(sy, e=e, t_=t_, j=j, lsem=lsem):
                    if j == 0:
                        sy.wait_ge(sem["stxn"], stxn_done[nt - 1])
                    elif j >= 2:
                        sy.wait_ge(sem["th"], th_after[(j - 2) * FC + FC - 1])
                    sy.dma_start(
                        out=xn_sb[j % 2].rearrange("p (c n) -> p c n", c=8),
                        in_=xnd.rearrange("(c p) n -> p c n", p=128)[:, :, t_ * 512:(t_ + 1) * 512],
                    ).then_inc(sem[lsem], 16)
                p.emit("sync", f)
                ldxn_j = (lsem, p.inc(lsem, 16))

                for fc in range(FC):
                    g_ = j * FC + fc

                    def f(t, e=e, fc=fc, g_=g_, ldxn_j=ldxn_j):
                        if fc == 0:
                            t.wait_ge(sem[ldxn_j[0]], ldxn_j[1])
                            t.wait_ge(sem["ldw1"], w1rdy[e])
                            if e == 0 and t_ == 0:
                                t.wait_ge(sem["v"], v_bcast)
                        if g_ >= 2:
                            t.wait_ge(sem["s"], sh_after[g_ - 2])
                        bank = hp[g_ % 2]
                        for c in range(8):
                            ins = t.matmul(bank[:, 0:512],
                                           w1_sb[:, c * 4096 + fc * 128: c * 4096 + (fc + 1) * 128],
                                           xn_sb[(g_ // FC) % 2][:, c * 512:(c + 1) * 512],
                                           start=(c == 0), stop=(c == 7))
                        ins.then_inc(sem["th"], 1)
                    p.emit("tensor", f)
                    th_after[g_] = p.inc("th", 1)

                    sh_after[g_] = p.sop(
                        [("th", th_after[g_])],
                        lambda s, e=e, fc=fc, g_=g_: s.activation(
                            h_sb[:, fc * 512:(fc + 1) * 512], hp[g_ % 2][:, 0:512],
                            AF.Relu, bias=b1pl_sb[:, e * 32 + fc:e * 32 + fc + 1]))

                for s4 in range(4):
                    for half in range(2):
                        og = (j * 4 + s4) * 2 + half

                        def f(t, e=e, s4=s4, half=half, og=og, j=j):
                            if s4 == 0 and half == 0:
                                t.wait_ge(sem["s"], sh_after[j * FC + FC - 1])
                                t.wait_ge(sem["ldw2"], w2rdy[e])
                            if og >= 4:
                                t.wait_ge(sem["v"], vo_after[og - 4])
                            bank = op[og % 4]
                            t.matmul(bank[:, 0:512], onesb[:, 0:128],
                                     b2row[:, half * 512:(half + 1) * 512],
                                     start=True, stop=False)
                            for fc in range(FC):
                                ins = t.matmul(
                                    bank[:, 0:512],
                                    h_sb[:, fc * 512 + s4 * 128: fc * 512 + (s4 + 1) * 128],
                                    w2_sb[:, fc * 1024 + half * 512: fc * 1024 + (half + 1) * 512],
                                    start=False, stop=(fc == FC - 1))
                            ins.then_inc(sem["to"], 1)
                        p.emit("tensor", f)
                        to_after[og] = p.inc("to", 1)

                        def mkev(e, t_, s4, half, og):
                            def fv(v):
                                jj = t_ * 4 + s4
                                return v.tensor_scalar(
                                    otile[s4 % 2][:, half * 512:(half + 1) * 512],
                                    op[og % 4][:, 0:512],
                                    geff[:, jj * 8 + e: jj * 8 + e + 1], None, ALU.mult)
                            return fv
                        ai = og // 2
                        wv = [("to", to_after[og])]
                        if og == 0:
                            wv.append(("sti", sti_done[nt - 1]))
                        if ai >= 2 and half == 0:
                            wv.append(("acc", 16 * (ai - 2 + 1)))
                        vo_v = p.vop(wv, mkev(e, t_, s4, half, og))
                        vo_after[og] = vo_v

                if t_ == ET - 1 and e + 1 < E:
                    def f(g, j=j):
                        g.wait_ge(sem["th"], th_after[j * FC + FC - 1])
                    p.emit("gpsimd", f)
                    w1rdy[e + 1] = emit_w1_load(e + 1)

                for s4 in range(4):
                    ai = j * 4 + s4

                    def f(g, t_=t_, s4=s4, ai=ai):
                        g.wait_ge(sem["v"], vo_after[ai * 2 + 1])
                        if ai == 0:
                            g.wait_ge(sem["sti"], sti_done[nt - 1])
                        else:
                            g.wait_ge(sem["acc"], 16 * ai)
                        r = t_ * 512 + s4 * 128
                        g.dma_start(out=x3[r:r + 128, :], in_=otile[s4 % 2][:, :],
                                    accum_op=ALU.add).then_inc(sem["acc"], 16)
                    p.emit("gpsimd", f)
                    acc_after[ai] = p.inc("acc", 16)

            if e + 1 < E:
                def f(g, e=e):
                    g.wait_ge(sem["to"], to_after[(e * ET + ET - 1) * 8 + 7])
                p.emit("gpsimd", f)
                w2rdy[e + 1] = emit_w2_load(e + 1)

        total_acc = p.cur("acc")

        def g_final(g):
            g.wait_ge(sem["acc"], total_acc)
            g.wait_ge(sem["sti"], sti_done[nt - 1])
        p.emit("gpsimd", g_final)




def emit_a(nc, p, sem, dram, ps, nq=NT):

    NE = nq + 4
    TOKQ = nq * 128
    TOKE = NE * 128
    NG = NE // 4
    W = 640

    xe_d = dram["xe"]
    wq_d = dram["wq"]
    wk_d = dram["wk"]
    wv_d = dram["wv"]
    wo_d = dram["wo"]
    mask_d = dram["mask"]
    eye = dram["eye"]
    x1 = dram["x1"]

    es = ExitStack()
    with es:
        nqc = nq * 128
        nec = NE * 128
        aB = es.enter_context(nc.sbuf_tensor("aBa", [128, 8 * nqc + 8 * nec + NE * 1024 +
                                                     24576 + 1024 + 4096], BF16))
        o = 0
        q_sb = aB[:, o:o + 8 * nqc]; o += 8 * nqc
        k_sb = aB[:, o:o + 8 * nec]; o += 8 * nec
        v_sb = aB[:, o:o + NE * 1024]; o += NE * 1024
        wreg = o
        wq_sb = aB[:, o:o + 8192]
        wk_sb = aB[:, o + 8192:o + 16384]
        wv_sb = aB[:, o + 16384:o + 24576]
        wo_sb = aB[:, o:o + 8192]
        probs = [aB[:, o + 8192:o + 8832], aB[:, o + 8832:o + 9472]]
        probsT = [aB[:, o + 9472:o + 10112], aB[:, o + 10112:o + 10752]]
        avt = aB[:, o + 10752:o + 11776]
        avfm = aB[:, o + 11776:o + 12800]
        o += 24576
        ybf = aB[:, o:o + 1024]; o += 1024
        yfmroll = aB[:, o:o + 4096]; o += 4096

        aF = es.enter_context(nc.sbuf_tensor("aFa", [128, 7424], F32))
        xet = aF[:, 0:1024]
        yt = aF[:, 1024:2048]
        score = [aF[:, 0:640], aF[:, 640:1280]]
        xo = [aF[:, 2048:3072], aF[:, 3072:4096]]
        outs = [aF[:, 4096:5120], aF[:, 5120:6144]]
        maskt = [aF[:, 6144:6784], aF[:, 6784:7424]]

        small = es.enter_context(nc.sbuf_tensor("smalla", [128, 32], F32))
        rr_all = small[:, 16:32]
        sum1 = small[:, 0:1]
        mean = small[:, 1:2]
        ex2 = small[:, 2:3]
        m2 = small[:, 3:4]
        var = small[:, 4:5]
        lnv = small[:, 5:6]
        rstd = small[:, 6:7]
        sumsq = small[:, 7:8]
        negm = small[:, 8:9]
        ssum = small[:, 9:10]
        rr = small[:, 10:11]
        epsv = small[:, 11:12]
        nhalf = small[:, 12:13]
        eyeb = es.enter_context(nc.sbuf_tensor("eyeba", [128, 128], BF16))

        qkp = [ps["pf0"], ps["pf1"]]
        vvp = [ps["pf2"], ps["pf3"]]
        tps = ps["tb0"]
        tps2 = ps["tb1"]
        tpsx = [tps, tps2]
        oav = [ps["pf4"], ps["pf5"]]


        def g_init(g):
            g.dma_start(out=eyeb[:, :], in_=eye[:, :]).then_inc(sem["ldwa"], 16)
            g.memset(epsv, 1e-5)
            g.memset(nhalf, -0.5).then_inc(sem["g"], 1)
            for c in range(8):
                g.dma_start(out=wq_sb[:, c * 1024:(c + 1) * 1024],
                            in_=wq_d[c * 128:(c + 1) * 128, :]).then_inc(sem["ldwa"], 16)
            for c in range(8):
                g.dma_start(out=wk_sb[:, c * 1024:(c + 1) * 1024],
                            in_=wk_d[c * 128:(c + 1) * 128, :]).then_inc(sem["ldwa"], 16)
            for c in range(8):
                g.dma_start(out=wv_sb[:, c * 1024:(c + 1) * 1024],
                            in_=wv_d[c * 128:(c + 1) * 128, :]).then_inc(sem["ldwa"], 16)
        p.emit("gpsimd", g_init)
        g_done = p.inc("g", 1)
        ldwa_all = p.inc("ldwa", 16 * 25)

        ldx_done = [0] * NE
        ytr_done = [0] * NE
        yfm_done = [0] * NE
        ln_done = [0] * NE
        sq_done = [0] * NE

        qproj_done = [0] * NG
        kproj_done = [0] * NG
        vproj_done = [0] * NG
        qev_done = [0] * NG
        kev_done = [0] * NG
        vev_done = [0] * NG

        for i in range(NE):
            gidx = i // 4

            def f(sy, i=i):
                if i > 0:
                    sy.wait_ge(sem["v"], ln_done[i - 1])
                    sy.wait_ge(sem["s"], sq_done[i - 1])
                sy.dma_start(out=xet, in_=xe_d[i * 128:(i + 1) * 128, :]).then_inc(sem["ldx"], 16)
            p.emit("sync", f)
            ldx_done[i] = p.inc("ldx", 16)

            w0 = [("ldx", ldx_done[i])] + ([("g", g_done)] if i == 0 else [])
            p.vop(w0, lambda v: v.reduce_sum(sum1, xet, axis=AX.X))
            p.vop([], lambda v: v.tensor_scalar_mul(mean, sum1, 1.0 / D))
            sq_done[i] = p.sop(w0, lambda s: s.activation(yt, xet, AF.Square, accum_out=sumsq))
            p.vop([("s", sq_done[i])], lambda v: v.tensor_scalar_mul(ex2, sumsq, 1.0 / D))
            p.vop([], lambda v: v.tensor_mul(m2, mean, mean))
            var_d = p.vop([], lambda v: v.tensor_sub(var, ex2, m2))
            p.sop([("v", var_d)], lambda s: s.activation(lnv, var, AF.Ln, bias=epsv))
            rstd_d = p.sop([], lambda s: s.activation(rstd, lnv, AF.Exp, scale=nhalf))
            p.vop([("s", rstd_d)],
                  lambda v: v.tensor_scalar(yt, xet, mean, rstd, ALU.subtract, ALU.mult))
            ln_done[i] = p.vop([], lambda v: v.tensor_copy(ybf, yt))

            def f(t, i=i):
                t.wait_ge(sem["v"], ln_done[i])
                if i == 0:
                    t.wait_ge(sem["ldwa"], ldwa_all)
                if i > 0:
                    t.wait_ge(sem["v"], yfm_done[i - 1])
                for k in range(8):
                    ins = t.transpose(tps[:, k * 128:(k + 1) * 128],
                                      ybf[:, k * 128:(k + 1) * 128], eyeb[:, :])
                ins.then_inc(sem["t"], 1)
            p.emit("tensor", f)
            ytr_done[i] = p.inc("t", 1)

            def fv(v, i=i):
                return v.tensor_copy(
                    yfmroll.rearrange("p (c n) -> p c n", c=8)[:, :, (i % 4) * 128:(i % 4 + 1) * 128],
                    tps[:, :].rearrange("p (c n) -> p c n", c=8))
            wv_ = [("t", ytr_done[i])]
            if i >= 4:
                wv_ += [("t", vproj_done[gidx - 1])]
            yfm_done[i] = p.vop(wv_, fv)

            if i % 4 != 3:
                continue

            g0 = gidx * 512
            qlo = max(g0, 256)
            qhi = min(g0 + 512, 256 + TOKQ)
            if qhi > qlo:
                for dc in range(8):
                    def f(t, gidx=gidx, dc=dc, qlo=qlo, qhi=qhi, g0=g0, i=i):
                        if dc == 0:
                            t.wait_ge(sem["v"], yfm_done[i])
                            t.wait_ge(sem["ldwa"], ldwa_all)
                            if gidx >= 1:
                                t.wait_ge(sem["v"], qev_done[gidx - 1])
                        bank = qkp[dc % 2]
                        if dc >= 2:
                            t.wait_ge(sem["v"], p.marks[("qev", gidx, dc - 2)])
                        for c in range(8):
                            ins = t.matmul(bank[:, 0:qhi - qlo],
                                           wq_sb[:, c * 1024 + dc * 128: c * 1024 + (dc + 1) * 128],
                                           yfmroll[:, c * 512 + qlo - g0: c * 512 + qhi - g0],
                                           start=(c == 0), stop=(c == 7))
                        ins.then_inc(sem["t"], 1)
                    p.emit("tensor", f)
                    p.marks[("qproj", gidx, dc)] = p.inc("t", 1)

                    def fv(v, gidx=gidx, dc=dc, qlo=qlo, qhi=qhi):
                        return v.tensor_copy(
                            q_sb[:, dc * nqc + qlo - 256: dc * nqc + qhi - 256],
                            qkp[dc % 2][:, 0:qhi - qlo])
                    p.marks[("qev", gidx, dc)] = p.vop([("t", p.marks[("qproj", gidx, dc)])], fv)
                qproj_done[gidx] = p.marks[("qproj", gidx, 7)]
                qev_done[gidx] = p.marks[("qev", gidx, 7)]
            else:
                qproj_done[gidx] = p.cur("t")
                qev_done[gidx] = p.cur("v")

            for dc in range(8):
                def f(t, gidx=gidx, dc=dc, g0=g0, i=i):
                    if dc == 0:
                        t.wait_ge(sem["v"], qev_done[gidx])
                    if dc >= 2:
                        t.wait_ge(sem["v"], p.marks[("kev", gidx, dc - 2)])
                    bank = qkp[dc % 2]
                    for c in range(8):
                        ins = t.matmul(bank[:, 0:512],
                                       wk_sb[:, c * 1024 + dc * 128: c * 1024 + (dc + 1) * 128],
                                       yfmroll[:, c * 512:(c + 1) * 512],
                                       start=(c == 0), stop=(c == 7))
                    ins.then_inc(sem["t"], 1)
                p.emit("tensor", f)
                p.marks[("kproj", gidx, dc)] = p.inc("t", 1)

                def fv(v, gidx=gidx, dc=dc, g0=g0):
                    return v.tensor_copy(k_sb[:, dc * nec + g0: dc * nec + g0 + 512],
                                         qkp[dc % 2][:, 0:512])
                p.marks[("kev", gidx, dc)] = p.vop([("t", p.marks[("kproj", gidx, dc)])], fv)
            kproj_done[gidx] = p.marks[("kproj", gidx, 7)]
            kev_done[gidx] = p.marks[("kev", gidx, 7)]

            for tt in range(4):
                ti = gidx * 4 + tt
                for half in range(2):
                    def f(t, gidx=gidx, tt=tt, half=half):
                        idx = tt * 2 + half
                        if idx >= 2:
                            t.wait_ge(sem["v"], p.marks[("vev", gidx, idx - 2)])
                        elif gidx >= 1:
                            t.wait_ge(sem["v"], p.marks[("vev", gidx - 1, 6 + idx)])
                        bank = vvp[idx % 2]
                        for c in range(8):
                            ins = t.matmul(bank[:, 0:512],
                                           yfmroll[:, c * 512 + tt * 128: c * 512 + (tt + 1) * 128],
                                           wv_sb[:, c * 1024 + half * 512: c * 1024 + (half + 1) * 512],
                                           start=(c == 0), stop=(c == 7))
                        ins.then_inc(sem["t"], 1)
                    p.emit("tensor", f)
                    p.marks[("vproj", gidx, tt * 2 + half)] = p.inc("t", 1)

                    def fv(v, ti=ti, half=half, gidx=gidx, tt=tt):
                        return v.tensor_copy(
                            v_sb[:, ti * 1024 + half * 512: ti * 1024 + (half + 1) * 512],
                            vvp[(tt * 2 + half) % 2][:, 0:512])
                    p.marks[("vev", gidx, tt * 2 + half)] = p.vop(
                        [("t", p.marks[("vproj", gidx, tt * 2 + half)])], fv)
            vproj_done[gidx] = p.marks[("vproj", gidx, 7)]
            vev_done[gidx] = p.marks[("vev", gidx, 7)]

        phase1_v = p.cur("v")
        phase1_t = p.cur("t")

        def f(g):
            g.wait_ge(sem["t"], phase1_t)
            for c in range(8):
                g.dma_start(out=wo_sb[:, c * 1024:(c + 1) * 1024],
                            in_=wo_d[c * 128:(c + 1) * 128, :]).then_inc(sem["ldwo"], 16)
        p.emit("gpsimd", f)
        ldwo_all = p.inc("ldwo", 16 * 8)

        ldm_done = [0] * nq
        ldo_done = [0] * nq
        stx_done = [0] * nq

        for qb in range(nq):
            def f(sy, qb=qb):
                if qb > 0:
                    sy.wait_ge(sem["ldm"], 16 * qb)
                if qb >= 2:
                    sy.wait_ge(sem["v"], p.marks[("mask_used", qb - 2)])
                sy.dma_start(out=maskt[qb % 2], in_=mask_d[qb, :, :]).then_inc(sem["ldm"], 16)
            p.emit("sync", f)
            ldm_done[qb] = p.inc("ldm", 16)

            def f(sy, qb=qb):
                if qb > 0:
                    sy.wait_ge(sem["ldo"], 16 * qb)
                if qb >= 2:
                    sy.wait_ge(sem["v"], p.marks[("res", qb - 2)])
                sy.dma_start(out=xo[qb % 2],
                             in_=xe_d[256 + qb * 128: 256 + (qb + 1) * 128, :]).then_inc(sem["ldo"], 16)
            p.emit("sync", f)
            ldo_done[qb] = p.inc("ldo", 16)

            for h in range(16):
                c, base = h // 2, 64 * (h % 2)
                par = h % 2
                def f(t, qb=qb, h=h, c=c, base=base, par=par):
                    if h == 0:
                        t.wait_ge(sem["v"], vev_done[NG - 1])
                        t.wait_ge(sem["v"], qev_done[NG - 1])
                        t.wait_ge(sem["v"], kev_done[NG - 1])
                    if h >= 2:
                        t.wait_ge(sem["v"], p.marks[("sev", qb, h - 2)])
                    elif qb >= 1:
                        t.wait_ge(sem["v"], p.marks[("res", qb - 1)])
                    lq = q_sb[base:base + 64, c * nqc + qb * 128: c * nqc + (qb + 1) * 128]
                    t.matmul(qkp[par][:, 0:512],
                             lq, k_sb[base:base + 64, c * nec + qb * 128: c * nec + qb * 128 + 512],
                             start=True, stop=True)
                    t.matmul(vvp[par][:, 0:128],
                             lq, k_sb[base:base + 64, c * nec + qb * 128 + 512: c * nec + qb * 128 + 640],
                             start=True, stop=True).then_inc(sem["t"], 1)
                p.emit("tensor", f)
                p.marks[("smm", qb, h)] = p.inc("t", 1)

                wv_ = [("t", p.marks[("smm", qb, h)]), ("ldm", ldm_done[qb])]
                p.vop(wv_, lambda v, qb=qb, par=par: v.scalar_tensor_tensor(
                    score[par][:, 0:512], qkp[par][:, 0:512], 0.125, maskt[qb % 2][:, 0:512],
                    ALU.mult, ALU.add))
                p.vop([], lambda v, qb=qb, par=par: v.scalar_tensor_tensor(
                    score[par][:, 512:640], vvp[par][:, 0:128], 0.125, maskt[qb % 2][:, 512:640],
                    ALU.mult, ALU.add))
                mk_used = p.vop([], lambda v, par=par: v.reduce_max(
                    negm, score[par][:, 0:640], axis=AX.X, negate=True))
                p.marks[("mask_used", qb)] = mk_used
                p.marks[("sev", qb, h)] = mk_used

                exp_d = p.sop([("v", mk_used)],
                              lambda s, par=par: s.activation(probs[par][:, 0:640],
                                                              score[par][:, 0:640], AF.Exp,
                                                              bias=negm, accum_out=ssum))
                rr_d = p.vop([("s", exp_d)],
                             lambda v, h=h: v.reciprocal(rr_all[:, h:h + 1], ssum))

                p.marks[("pexp", qb, h)] = exp_d

                def f(t, qb=qb, h=h, par=par):
                    t.wait_ge(sem["s"], p.marks[("pexp", qb, h)])
                    if h >= 2:
                        t.wait_ge(sem["v"], p.marks[("ptev", qb, h - 2)])
                    elif qb >= 1:
                        if h == 0:
                            t.wait_ge(sem["v"], p.marks[("avfm", qb - 1)])
                        else:
                            t.wait_ge(sem["v"], p.marks[("ptev", qb - 1, 15)])
                    elif h == 0:
                        t.wait_ge(sem["v"], yfm_done[NE - 1])
                    for kc in range(5):
                        ins = t.transpose(tpsx[par][:, kc * 128:(kc + 1) * 128],
                                          probs[par][:, kc * 128:(kc + 1) * 128], eyeb[:, :])
                    ins.then_inc(sem["t"], 1)
                p.emit("tensor", f)
                p.marks[("ptr", qb, h)] = p.inc("t", 1)

                p.marks[("ptev", qb, h)] = p.vop(
                    [("t", p.marks[("ptr", qb, h)])],
                    lambda v, par=par: v.tensor_copy(probsT[par][:, 0:640], tpsx[par][:, 0:640]))

                def f(t, qb=qb, h=h, par=par):
                    t.wait_ge(sem["v"], p.marks[("ptev", qb, h)])
                    if h == 0:
                        t.wait_ge(sem["s"], p.marks.get(("avev_s", qb - 1), 0))
                    for kc in range(5):
                        ins = t.matmul(oav[h // 8][:, (h % 8) * 64:(h % 8 + 1) * 64],
                                       probsT[par][:, kc * 128:(kc + 1) * 128],
                                       v_sb[:, (qb + kc) * 1024 + h * 64: (qb + kc) * 1024 + (h + 1) * 64],
                                       start=(h % 8 == 0 and kc == 0),
                                       stop=(h % 8 == 7 and kc == 4))
                    if h % 8 == 7:
                        ins.then_inc(sem["t"], 1)
                    else:
                        ins.then_inc(sem["t"], 1)
                p.emit("tensor", f)
                p.marks[("avmm", qb, h)] = p.inc("t", 1)

                bank_final = (qb, 7 if h < 8 else 15)
                p.marks[("avev_pending", qb, h)] = (bank_final, rr_d)
            for h in range(16):
                bank_final, rr_d = p.marks[("avev_pending", qb, h)]
                p.marks[("avev", qb, h)] = p.sop(
                    [("t", p.marks[("avmm", bank_final[0], bank_final[1])]), ("v", rr_d)],
                    lambda s, h=h: s.activation(avt[:, h * 64:(h + 1) * 64],
                                                oav[h // 8][:, (h % 8) * 64:(h % 8 + 1) * 64],
                                                AF.Identity, scale=rr_all[:, h:h + 1]))
            p.marks[("avev_s", qb)] = p.marks[("avev", qb, 15)]

            def f(t, qb=qb):
                t.wait_ge(sem["s"], p.marks[("avev", qb, 15)])
                t.wait_ge(sem["v"], p.marks[("ptev", qb, 14)])
                for k in range(8):
                    ins = t.transpose(tps[:, k * 128:(k + 1) * 128],
                                      avt[:, k * 128:(k + 1) * 128], eyeb[:, :])
                ins.then_inc(sem["t"], 1)
            p.emit("tensor", f)
            avtr = p.inc("t", 1)

            avfm_d = p.vop([("t", avtr)], lambda v: v.tensor_copy(avfm, tps[:, :]))
            p.marks[("avfm", qb)] = avfm_d

            def f(t, qb=qb, avfm_d=avfm_d):
                t.wait_ge(sem["v"], avfm_d)
                t.wait_ge(sem["ldwo"], ldwo_all)
                t.wait_ge(sem["v"], p.marks[("sev", qb, 15)])
                for half in range(2):
                    for c in range(8):
                        ins = t.matmul(qkp[half][:, 0:512],
                                       avfm[:, c * 128:(c + 1) * 128],
                                       wo_sb[:, c * 1024 + half * 512: c * 1024 + (half + 1) * 512],
                                       start=(c == 0), stop=(c == 7))
                ins.then_inc(sem["t"], 1)
            p.emit("tensor", f)
            opj_d = p.inc("t", 1)

            wv_ = [("t", opj_d), ("ldo", ldo_done[qb])]
            if qb >= 2:
                wv_.append(("stx", 16 * (qb - 1)))
            p.vop(wv_, lambda v, qb=qb: v.tensor_add(outs[qb % 2][:, 0:512],
                                                     qkp[0][:, 0:512], xo[qb % 2][:, 0:512]))
            p.marks[("res", qb)] = p.vop(
                [], lambda v, qb=qb: v.tensor_add(outs[qb % 2][:, 512:1024],
                                                  qkp[1][:, 0:512], xo[qb % 2][:, 512:1024]))

            def f(sy, qb=qb):
                sy.wait_ge(sem["v"], p.marks[("res", qb)])
                if qb > 0:
                    sy.wait_ge(sem["stx"], 16 * qb)
                sy.dma_start(out=x1[qb * 128:(qb + 1) * 128, :],
                             in_=outs[qb % 2]).then_inc(sem["stx"], 16)
            p.emit("sync", f)
            stx_done[qb] = p.inc("stx", 16)

        def g_final(g):
            g.wait_ge(sem["stx"], 16 * nq)
        p.emit("gpsimd", g_final)



def a_masks(S=4096, nq=NT, s0=0):
    pos = np.arange(S)
    out = np.full((nq, 128, 640), -1e30, np.float32)
    for qb in range(nq):
        qs = s0 + qb * 128
        for i in range(128):
            a = qs + i
            ks = qs - 256
            j = np.arange(ks, ks + 640)
            valid = (j >= 0) & (j < S)
            jj = np.clip(j, 0, S - 1)
            seg = (a // 64) == (jj // 64)
            diff = np.abs(a - jj)
            dil = (diff > 0) & (diff % 4 == 0) & (diff <= 256)
            allowed = valid & (seg | dil)
            out[qb, i, allowed] = 0.0
    return out


def a_in_maps(x_full, inp):
    maps = []
    S = 4096
    for c in range(NCORES):
        b, h = c // 2, c % 2
        s0 = h * TOK
        lo, hi = s0 - 256, s0 + TOK + 256
        xe = np.zeros((TOK + 512, D), np.float32)
        cl, ch = max(lo, 0), min(hi, S)
        xe[cl - lo:ch - lo, :] = x_full[b, cl:ch, :]
        maps.append({
            "xe": xe,
            "wq": inp["L0_attn_w_q"].astype(np.float32),
            "wk": inp["L0_attn_w_k"].astype(np.float32),
            "wv": inp["L0_attn_w_v"].astype(np.float32),
            "wo": inp["L0_attn_w_o"].astype(np.float32),
            "mask": a_masks(S=S, nq=NT, s0=s0),
            "eye": _eye128(),
        })
    return maps



def emit_c1(nc, p, sem, dram, ps, nq=NT):
    xhh = dram["xhh"]
    flhal = dram["flhal"]
    TOKQ = nq * 128
    NTI = nq + 1
    TKC = (nq * 128) // 512
    NG = nq // 4
    DH = 2048
    CC = DH // 128

    xh = dram["xh"]
    w1T = dram["w1T"]
    dlT = dram["dlT"]
    awT = dram["awT"]
    bwT = dram["bwT"]
    cwpl = dram["cwpl"]
    cbpl = dram["cbpl"]
    abpl = dram["abpl"]
    bbpl = dram["bbpl"]
    dbpl = dram["dbpl"]
    eye = dram["eye"]
    hd = dram["hd"]
    x3d = dram["x3d"]
    totd = dram["totd"]

    es = ExitStack()
    with es:
        w1_sb = es.enter_context(nc.sbuf_tensor("w1sbc", [128, 8 * 1024], BF16))
        wst = [es.enter_context(nc.sbuf_tensor(f"wst{i}", [128, 2 * 3 * 1024], BF16))
               for i in range(1)]
        yroll = es.enter_context(nc.sbuf_tensor("yrollc", [128, 4096], BF16))
        ytile16 = es.enter_context(nc.sbuf_tensor("yt16", [128, 1024], BF16))
        x1f = es.enter_context(nc.sbuf_tensor("x1f", [128, 8 * (TOKQ + 2)], F32))
        x3f = es.enter_context(nc.sbuf_tensor("x3f", [128, 8 * TOKQ], BF16))
        cb0 = es.enter_context(nc.sbuf_tensor("cb0", [128, TOKQ], F32))
        cb1 = es.enter_context(nc.sbuf_tensor("cb1", [128, TOKQ], F32))
        cb2 = es.enter_context(nc.sbuf_tensor("cb2", [128, TOKQ], F32))
        cb3 = es.enter_context(nc.sbuf_tensor("cb3", [128, TOKQ], F32))
        cb4 = es.enter_context(nc.sbuf_tensor("cb4", [128, TOKQ], F32))
        zz = es.enter_context(nc.sbuf_tensor("zzc", [128, TOKQ], F32))
        tots = es.enter_context(nc.sbuf_tensor("tots", [128, 2 * CC], F32))
        ybf = es.enter_context(nc.sbuf_tensor("ybfc", [128, 1024], BF16))
        cwsb = es.enter_context(nc.sbuf_tensor("cwsb", [128, 24], F32))
        cbsb = es.enter_context(nc.sbuf_tensor("cbsb", [128, 8], F32))
        absb = es.enter_context(nc.sbuf_tensor("absb", [128, CC], F32))
        bbsb = es.enter_context(nc.sbuf_tensor("bbsb", [128, CC], F32))
        dbsb = es.enter_context(nc.sbuf_tensor("dbsb", [128, CC], F32))
        eyeb = es.enter_context(nc.sbuf_tensor("eyebc", [128, 128], BF16))
        fl2c = es.enter_context(nc.sbuf_tensor("fl2c", [2, 1], F32))
        aF = es.enter_context(nc.sbuf_tensor("aFc", [128, 2048], F32))
        xet = aF[:, 0:1024]
        yt = aF[:, 1024:2048]
        small = es.enter_context(nc.sbuf_tensor("smallc", [128, 16], F32))
        sum1 = small[:, 0:1]; mean = small[:, 1:2]; ex2 = small[:, 2:3]
        m2 = small[:, 3:4]; var = small[:, 4:5]; lnv = small[:, 5:6]
        rstd = small[:, 6:7]; sumsq = small[:, 7:8]
        epsv = small[:, 8:9]; nhalf = small[:, 9:10]

        tps = ps["tb0"]
        pp = [ps["pf0"], ps["pf1"], ps["pf2"], ps["pf3"]]


        def g_init(g):
            g.dma_start(out=eyeb[:, :], in_=eye[:, :]).then_inc(sem["ldw"], 16)
            for c in range(8):
                g.dma_start(out=w1_sb[:, c * 1024:(c + 1) * 1024],
                            in_=w1T[c * 128:(c + 1) * 128, :]).then_inc(sem["ldw"], 16)
            g.dma_start(out=cwsb[:, :], in_=cwpl[:, :]).then_inc(sem["ldw"], 16)
            g.dma_start(out=cbsb[:, :], in_=cbpl[:, :]).then_inc(sem["ldw"], 16)
            g.dma_start(out=absb[:, :], in_=abpl[:, :]).then_inc(sem["ldw"], 16)
            g.dma_start(out=bbsb[:, :], in_=bbpl[:, :]).then_inc(sem["ldw"], 16)
            g.dma_start(out=dbsb[:, :], in_=dbpl[:, :]).then_inc(sem["ldw"], 16)
            g.memset(zz[:, :], 0.0)
            g.memset(epsv, 1e-5)
            g.dma_start(out=fl2c[:, :], in_=flhal[:, :]).then_inc(sem["ldw"], 16)
            g.memset(nhalf, -0.5).then_inc(sem["g"], 1)
        p.emit("gpsimd", g_init)
        ldw_all = p.inc("ldw", 16 * 15)
        g_done = p.inc("g", 1)

        ln_done = [0] * NTI
        sq_done = [0] * NTI
        ytr_done = [0] * NTI
        yfm_done = [0] * NTI
        x1ev = {}

        for i in range(NTI):
            if i < nq:
                def f(sy, i=i):
                    if i > 0:
                        sy.wait_ge(sem["v"], ln_done[i - 1])
                        sy.wait_ge(sem["s"], sq_done[i - 1])
                    sy.dma_start(out=xet, in_=xh[i * 128:(i + 1) * 128, :]
                                 ).then_inc(sem["ldx"], 16)
                p.emit("sync", f)
                ldx_i = p.inc("ldx", 16)
            else:
                zero_d = p.vop([("v", ln_done[i - 1]), ("s", sq_done[i - 1]),
                                ("g", g_done)],
                               lambda v: v.tensor_scalar_mul(xet, xet, 0.0))

                def f(sy, i=i, zero_d=zero_d):
                    sy.wait_ge(sem["v"], zero_d)
                    sy.dma_start(out=xet[0:2, :], in_=xhh[0:2, :]).then_inc(sem["ldx"], 16)
                p.emit("sync", f)
                ldx_i = p.inc("ldx", 16)
                halm = p.vop([("ldx", ldx_i), ("ldw", ldw_all)],
                             lambda v: v.tensor_scalar(xet[0:2, :], xet[0:2, :],
                                                       fl2c[:, 0:1], None, ALU.mult))

            w0 = [("ldx", ldx_i)] + ([("g", g_done)] if i == 0 else [])
            if i == NTI - 1:
                w0.append(("v", halm))
            p.vop(w0, lambda v: v.reduce_sum(sum1, xet, axis=AX.X))
            p.vop([], lambda v: v.tensor_scalar_mul(mean, sum1, 1.0 / D))
            sq_done[i] = p.sop(w0, lambda s: s.activation(yt, xet, AF.Square, accum_out=sumsq))
            p.vop([("s", sq_done[i])], lambda v: v.tensor_scalar_mul(ex2, sumsq, 1.0 / D))
            p.vop([], lambda v: v.tensor_mul(m2, mean, mean))
            var_d = p.vop([], lambda v: v.tensor_sub(var, ex2, m2))
            p.sop([("v", var_d)], lambda s: s.activation(lnv, var, AF.Ln, bias=epsv))
            rstd_d = p.sop([], lambda s: s.activation(rstd, lnv, AF.Exp, scale=nhalf))
            p.vop([("s", rstd_d)],
                  lambda v: v.tensor_scalar(yt, xet, mean, rstd, ALU.subtract, ALU.mult))
            ln_done[i] = p.vop([], lambda v: v.tensor_copy(ybf[:, :], yt))

            def f(t, i=i):
                t.wait_ge(sem["v"], ln_done[i])
                if i == 0:
                    t.wait_ge(sem["ldw"], ldw_all)
                if i > 0:
                    t.wait_ge(sem["v"], yfm_done[i - 1])
                for k in range(8):
                    ins = t.transpose(tps[:, k * 128:(k + 1) * 128],
                                      ybf[:, k * 128:(k + 1) * 128], eyeb[:, :])
                ins.then_inc(sem["t"], 1)
            p.emit("tensor", f)
            ytr_done[i] = p.inc("t", 1)

            if i < nq:
                def fv(v, i=i):
                    return v.tensor_copy(
                        yroll.rearrange("p (c n) -> p c n", c=8)[:, :, (i % 4) * 128:(i % 4 + 1) * 128],
                        tps[:, :].rearrange("p (c n) -> p c n", c=8))
                wv_ = [("t", ytr_done[i])]
                if i >= 4:
                    wv_.append(("t", p.marks[("w1mm", i // 4 - 1, 7)]))
                yfm_done[i] = p.vop(wv_, fv)
            else:
                yfm_done[i] = p.vop([("t", ytr_done[i])],
                                    lambda v: v.tensor_copy(ytile16[:, :], tps[:, :]))

            if i % 4 == 3 and i < nq:
                gi = i // 4
                for dc in range(8):
                    def f(t, gi=gi, dc=dc, i=i):
                        if dc == 0:
                            t.wait_ge(sem["v"], yfm_done[i])
                        if dc >= 4:
                            t.wait_ge(sem["v"], x1ev[(gi, dc - 4)])
                        elif gi >= 1:
                            t.wait_ge(sem["v"], x1ev[(gi - 1, dc + 4)])
                        bank = pp[dc % 4]
                        for c in range(8):
                            ins = t.matmul(bank[:, 0:512],
                                           w1_sb[:, c * 1024 + dc * 128: c * 1024 + (dc + 1) * 128],
                                           yroll[:, c * 512:(c + 1) * 512],
                                           start=(c == 0), stop=(c == 7))
                        ins.then_inc(sem["t"], 1)
                    p.emit("tensor", f)
                    p.marks[("w1mm", gi, dc)] = p.inc("t", 1)

                    def fv(v, gi=gi, dc=dc):
                        return v.tensor_copy(
                            x1f[:, dc * (TOKQ + 2) + 1 + gi * 512: dc * (TOKQ + 2) + 1 + (gi + 1) * 512],
                            pp[dc % 4][:, 0:512])
                    x1ev[(gi, dc)] = p.vop([("t", p.marks[("w1mm", gi, dc)])], fv)

        halo_ev = {}
        for dc in range(8):
            def f(t, dc=dc):
                if dc == 0:
                    t.wait_ge(sem["v"], yfm_done[NTI - 1])
                    t.wait_ge(sem["v"], x1ev[(NG - 1, 7)])
                if dc >= 4:
                    t.wait_ge(sem["v"], halo_ev[dc - 4])
                bank = pp[dc % 4]
                for c in range(8):
                    ins = t.matmul(bank[:, 0:2],
                                   w1_sb[:, c * 1024 + dc * 128: c * 1024 + (dc + 1) * 128],
                                   ytile16[:, c * 128: c * 128 + 2],
                                   start=(c == 0), stop=(c == 7))
                ins.then_inc(sem["t"], 1)
            p.emit("tensor", f)
            hmm = p.inc("t", 1)
            p.vop([("t", hmm)],
                  lambda v, dc=dc: v.tensor_copy(x1f[:, dc * (TOKQ + 2): dc * (TOKQ + 2) + 1],
                                                 pp[dc % 4][:, 0:1]))
            halo_ev[dc] = p.vop([],
                                lambda v, dc=dc: v.tensor_copy(
                                    x1f[:, dc * (TOKQ + 2) + TOKQ + 1: dc * (TOKQ + 2) + TOKQ + 2],
                                    pp[dc % 4][:, 1:2]))
        x1_done = p.cur("v")

        x3dc_done = [0] * 8
        for dc in range(8):
            b0 = dc * (TOKQ + 2)
            wv_c = [("v", x1_done)] if dc == 0 else [("s", x3dc_done[dc - 1])]
            p.vop(wv_c,
                  lambda v, b0=b0, dc=dc: v.tensor_scalar(cb0[:, 0:TOKQ], x1f[:, b0:b0 + TOKQ],
                                                          cwsb[:, dc * 3: dc * 3 + 1], None, ALU.mult))
            p.vop([], lambda v, b0=b0, dc=dc: v.scalar_tensor_tensor(
                cb0[:, 0:TOKQ], x1f[:, b0 + 1:b0 + 1 + TOKQ], cwsb[:, dc * 3 + 1: dc * 3 + 2],
                cb0[:, 0:TOKQ], ALU.mult, ALU.add))
            p.vop([], lambda v, b0=b0, dc=dc: v.scalar_tensor_tensor(
                cb0[:, 0:TOKQ], x1f[:, b0 + 2:b0 + 2 + TOKQ], cwsb[:, dc * 3 + 2: dc * 3 + 3],
                cb0[:, 0:TOKQ], ALU.mult, ALU.add))
            cadd = p.vop([], lambda v, dc=dc: v.tensor_scalar(
                cb0[:, 0:TOKQ], cb0[:, 0:TOKQ], cbsb[:, dc:dc + 1], None, ALU.add))
            x3dc_done[dc] = p.sop([("v", cadd)],
                                  lambda s, dc=dc: s.activation(x3f[:, dc * TOKQ:(dc + 1) * TOKQ],
                                                                cb0[:, 0:TOKQ], AF.Silu))
            def f(sy, dc=dc):
                sy.wait_ge(sem["s"], x3dc_done[dc])
                if dc > 0:
                    sy.wait_ge(sem["stx3"], 16 * dc)
                sy.dma_start(out=x3d[dc * 128:(dc + 1) * 128, :],
                             in_=x3f[:, dc * TOKQ:(dc + 1) * TOKQ]).then_inc(sem["stx3"], 16)
            p.emit("sync", f)
            p.inc("stx3", 16)

        for cc in range(CC):
            buf = (cc % 2) * 3072

            def f(g, cc=cc, buf=buf):
                if cc >= 1:
                    g.wait_ge(sem["ldwc"], 48 * cc)
                if cc >= 2:
                    g.wait_ge(sem["t"], p.marks[("ccmm", cc - 2)])
                for ti, wt in enumerate((dlT, awT, bwT)):
                    g.dma_start(
                        out=wst[0][:, buf + ti * 1024: buf + (ti + 1) * 1024]
                        .rearrange("p (c n) -> p c n", c=8),
                        in_=wt.rearrange("(c p) h -> p c h", p=128)[:, :, cc * 128:(cc + 1) * 128],
                    ).then_inc(sem["ldwc"], 16)
            p.emit("gpsimd", f)
            ldwc_cc = p.inc("ldwc", 16 * 3)

            for ti in range(3):
                for tk in range(TKC):
                    def f(t, cc=cc, ti=ti, tk=tk, buf=buf, ldwc_cc=ldwc_cc):
                        if ti == 0 and tk == 0:
                            t.wait_ge(sem["ldwc"], ldwc_cc)
                            t.wait_ge(sem["s"], x3dc_done[7])
                            if cc >= 1:
                                t.wait_ge(sem["v"], p.marks[("ccev", cc - 1, 3 * TKC - 1)])
                        bank = pp[(ti * TKC + tk) % 4]
                        if ti * TKC + tk >= 4:
                            t.wait_ge(sem["v"], p.marks[("ccev", cc, ti * TKC + tk - 4)])
                        for c in range(8):
                            ins = t.matmul(bank[:, 0:512],
                                           wst[0][:, buf + ti * 1024 + c * 128: buf + ti * 1024 + (c + 1) * 128],
                                           x3f[:, c * TOKQ + tk * 512: c * TOKQ + (tk + 1) * 512],
                                           start=(c == 0), stop=(c == 7))
                        ins.then_inc(sem["t"], 1)
                    p.emit("tensor", f)
                    p.marks[("ccpm", cc, ti * TKC + tk)] = p.inc("t", 1)
                    if ti == 2 and tk == TKC - 1:
                        p.marks[("ccmm", cc)] = p.marks[("ccpm", cc, 3 * TKC - 1)]

                    tgt = (cb2, cb0, cb1)[ti]
                    bias = (dbsb, absb, bbsb)[ti]
                    p.marks[("ccev", cc, ti * TKC + tk)] = p.vop(
                        [("t", p.marks[("ccpm", cc, ti * TKC + tk)])],
                        lambda v, tgt=tgt, bias=bias, cc=cc, tk=tk, ti=ti: v.tensor_scalar(
                            tgt[:, tk * 512:(tk + 1) * 512], pp[(ti * TKC + tk) % 4][:, 0:512],
                            bias[:, cc:cc + 1], None, ALU.add))

            last_ev = p.marks[("ccev", cc, 3 * TKC - 1)]
            wv_sp = [("v", last_ev)] + ([("sth", 16 * cc)] if cc > 0 else [])
            p.sop(wv_sp, lambda s: s.activation(cb3[:, :], cb2[:, :], AF.Abs))
            sp1 = p.sop([],
                        lambda s: s.activation(cb4[:, :], cb3[:, :], AF.Exp, scale=-1.0))
            sp2 = p.sop([], lambda s: s.activation(cb3[:, :], cb4[:, :], AF.Ln, bias=1.0))
            p.vop([("s", sp2)],
                  lambda v: v.scalar_tensor_tensor(cb2[:, :], cb2[:, :], 0.0, cb3[:, :],
                                                   ALU.max, ALU.add))
            p.vop([], lambda v: v.tensor_mul(cb4[:, :], cb2[:, :], cb0[:, :]))
            ab_d = p.sop([("v", p.cur("v"))],
                         lambda s: s.activation(cb3[:, :], cb4[:, :], AF.Exp, scale=-1.0))
            ac_d = p.vop([("s", ab_d)],
                         lambda v: v.tensor_tensor_scan(cb4[:, :], cb3[:, :], zz[:, :],
                                                        1.0, ALU.mult, ALU.add))
            p.vop([], lambda v: v.tensor_mul(cb1[:, :], cb1[:, :], cb4[:, :]))
            h_d = p.vop([], lambda v: v.tensor_tensor_scan(cb3[:, :], cb1[:, :], zz[:, :],
                                                           0.0, ALU.add, ALU.add))
            p.vop([], lambda v, cc=cc: v.tensor_copy(tots[:, cc:cc + 1],
                                                     cb4[:, TOKQ - 1:TOKQ]))
            tot_d = p.vop([], lambda v, cc=cc: v.tensor_copy(tots[:, CC + cc: CC + cc + 1],
                                                             cb3[:, TOKQ - 1:TOKQ]))

            def f(sy, cc=cc, h_d=h_d):
                sy.wait_ge(sem["v"], h_d)
                if cc > 0:
                    sy.wait_ge(sem["sth"], 16 * cc)
                sy.dma_start(out=hd[cc * 128:(cc + 1) * 128, :],
                             in_=cb3[:, :]).then_inc(sem["sth"], 16)
            p.emit("sync", f)
            p.inc("sth", 16)
            p.marks[("hstored", cc)] = 16 * (cc + 1)
            p.marks[("cb3st", cc)] = 16 * (cc + 1)

        def f(sy):
            pass
        tot_final = p.cur("v")

        def f2(sy, tot_final=tot_final):
            sy.wait_ge(sem["v"], tot_final)
            sy.dma_start(out=totd[:, :], in_=tots[:, :]).then_inc(sem["sttot"], 16)
        p.emit("sync", f2)
        p.inc("sttot", 16)

        def g_final(g):
            g.wait_ge(sem["sth"], 16 * CC)
            g.wait_ge(sem["stx3"], 16 * 8)
            g.wait_ge(sem["sttot"], 16)
        p.emit("gpsimd", g_final)



def emit_c2(nc, p, sem, dram, ps, nq=NT):
    TOKQ = nq * 128
    DH = 2048
    CC = DH // 128

    hd = dram["hd"]
    x3d = dram["x3d"]
    carA = dram["carA"]
    carH = dram["carH"]
    x2o = dram["x2o"]
    cwT = dram["cwT"]
    dwT = dram["dwT"]
    w2T = dram["w2T"]
    w3T = dram["w3T"]
    cbr = dram["cbr"]
    dbr = dram["dbr"]
    eye = dram["eye"]
    x3s = dram["x3s"]

    es = ExitStack()
    with es:
        cwsb = es.enter_context(nc.sbuf_tensor("cwsb2", [128, CC * 1024], BF16))
        dwsb = es.enter_context(nc.sbuf_tensor("dwsb2", [128, 8 * 1024], BF16))
        w2sb = es.enter_context(nc.sbuf_tensor("w2sb2", [128, 8 * 1024], BF16))
        w3sb = es.enter_context(nc.sbuf_tensor("w3sb2", [128, 8 * 1024], BF16))
        htf = [es.enter_context(nc.sbuf_tensor(f"htf{i}", [128, CC * 128], F32)) for i in range(2)]
        hbt = [es.enter_context(nc.sbuf_tensor(f"hbt{i}", [128, CC * 128], BF16)) for i in range(2)]
        x3t = [es.enter_context(nc.sbuf_tensor(f"x3t{i}", [128, 8 * 128], BF16)) for i in range(2)]
        carAs = es.enter_context(nc.sbuf_tensor("carAs", [128, CC], F32))
        carHs = es.enter_context(nc.sbuf_tensor("carHs", [128, CC], F32))
        ssm_sb = [es.enter_context(nc.sbuf_tensor(f"ssm{i}", [128, 1024], F32)) for i in range(2)]
        ssmbf = es.enter_context(nc.sbuf_tensor("ssmbf", [128, 1024], BF16))
        ssmfm = es.enter_context(nc.sbuf_tensor("ssmfm", [128, 1024], BF16))
        r_sb = es.enter_context(nc.sbuf_tensor("r_sb", [128, 1024], F32))
        zbf = es.enter_context(nc.sbuf_tensor("zbf", [128, 1024], BF16))
        zfm = es.enter_context(nc.sbuf_tensor("zfm", [128, 1024], BF16))
        x2t = [es.enter_context(nc.sbuf_tensor(f"x2t{i}", [128, 1024], F32)) for i in range(2)]
        outs = [es.enter_context(nc.sbuf_tensor(f"outs2{i}", [128, 1024], F32)) for i in range(2)]
        cbrow = es.enter_context(nc.sbuf_tensor("cbrow", [1, 1024], BF16))
        dbrow = es.enter_context(nc.sbuf_tensor("dbrow", [1, 1024], BF16))
        onesb = es.enter_context(nc.sbuf_tensor("onesb2", [1, 128], BF16))
        eyeb = es.enter_context(nc.sbuf_tensor("eyeb2", [128, 128], BF16))

        tps = ps["tb0"]
        pp = [ps["pf0"], ps["pf1"], ps["pf2"], ps["pf3"]]


        def g_init(g):
            g.dma_start(out=eyeb[:, :], in_=eye[:, :]).then_inc(sem["ldw"], 16)
            for cc in range(CC):
                g.dma_start(out=cwsb[:, cc * 1024:(cc + 1) * 1024],
                            in_=cwT[cc * 128:(cc + 1) * 128, :]).then_inc(sem["ldw"], 16)
            for c in range(8):
                g.dma_start(out=dwsb[:, c * 1024:(c + 1) * 1024],
                            in_=dwT[c * 128:(c + 1) * 128, :]).then_inc(sem["ldw"], 16)
            for c in range(8):
                g.dma_start(out=w2sb[:, c * 1024:(c + 1) * 1024],
                            in_=w2T[c * 128:(c + 1) * 128, :]).then_inc(sem["ldw"], 16)
            for c in range(8):
                g.dma_start(out=w3sb[:, c * 1024:(c + 1) * 1024],
                            in_=w3T[c * 128:(c + 1) * 128, :]).then_inc(sem["ldw"], 16)
            g.dma_start(out=carAs[:, :], in_=carA[:, :]).then_inc(sem["ldw"], 16)
            g.dma_start(out=carHs[:, :], in_=carH[:, :]).then_inc(sem["ldw"], 16)
            g.dma_start(out=cbrow[:, :], in_=cbr[:, :]).then_inc(sem["ldw"], 16)
            g.dma_start(out=dbrow[:, :], in_=dbr[:, :]).then_inc(sem["ldw"], 16)
            g.memset(onesb[:, :], 1.0).then_inc(sem["g"], 1)
        p.emit("gpsimd", g_init)
        ldw_all = p.inc("ldw", 16 * (1 + CC + 24 + 4))
        g_done = p.inc("g", 1)

        fin_done = [0] * nq
        x3ld = [0] * nq
        ssm_done = [0] * nq
        res_done = [0] * nq

        for i in range(nq):
            def f(sy, i=i):
                if i > 0:
                    sy.wait_ge(sem["ldh"], 16 * i)
                if i >= 2:
                    sy.wait_ge(sem["v"], fin_done[i - 2])
                sy.dma_start(out=htf[i % 2][:, :].rearrange("p (cc n) -> p cc n", cc=CC),
                             in_=hd.rearrange("(cc p) n -> p cc n", p=128)[:, :, i * 128:(i + 1) * 128],
                             ).then_inc(sem["ldh"], 16)
            p.emit("sync", f)
            ldh_i = p.inc("ldh", 16)

            for cc in range(CC):
                wv_ = [("ldh", ldh_i)] if cc == 0 else []
                if cc == 0 and i == 0:
                    wv_.append(("ldw", ldw_all))
                if cc == 0 and i >= 2:
                    wv_.append(("t", p.marks[("ssmm", i - 2)]))
                fin_done[i] = p.vop(
                    wv_, lambda v, i=i, cc=cc: v.tensor_scalar(
                        hbt[i % 2][:, cc * 128:(cc + 1) * 128],
                        htf[i % 2][:, cc * 128:(cc + 1) * 128],
                        carAs[:, cc:cc + 1], carHs[:, cc:cc + 1], ALU.mult, ALU.add))

            def f(sy, i=i):
                if i > 0:
                    sy.wait_ge(sem["ldx3"], 16 * i)
                if i >= 2:
                    sy.wait_ge(sem["t"], p.marks[("ssmm", i - 2)])
                sy.dma_start(out=x3t[i % 2][:, :].rearrange("p (c n) -> p c n", c=8),
                             in_=x3d.rearrange("(c p) n -> p c n", p=128)[:, :, i * 128:(i + 1) * 128],
                             ).then_inc(sem["ldx3"], 16)
            p.emit("sync", f)
            x3ld[i] = p.inc("ldx3", 16)

            def f(sy, i=i):
                if i > 0:
                    sy.wait_ge(sem["ldx2"], 16 * i)
                if i >= 2:
                    sy.wait_ge(sem["v"], res_done[i - 2])
                sy.dma_start(out=x2t[i % 2][:, :],
                             in_=x2o[i * 128:(i + 1) * 128, :]).then_inc(sem["ldx2"], 16)
            p.emit("sync", f)
            ldx2_i = p.inc("ldx2", 16)

            def f(t, i=i):
                t.wait_ge(sem["v"], fin_done[i])
                t.wait_ge(sem["ldx3"], x3ld[i])
                if i == 0:
                    t.wait_ge(sem["g"], g_done)
                if i >= 1:
                    t.wait_ge(sem["v"], res_done[i - 1])
                for half in range(2):
                    bank = pp[(i % 2) * 2 + half]
                    t.matmul(bank[:, 0:512], onesb[:, :], cbrow[:, half * 512:(half + 1) * 512],
                             start=True, stop=False)
                    t.matmul(bank[:, 0:512], onesb[:, :], dbrow[:, half * 512:(half + 1) * 512],
                             start=False, stop=False)
                    for cc in range(CC):
                        t.matmul(bank[:, 0:512],
                                 hbt[i % 2][:, cc * 128:(cc + 1) * 128],
                                 cwsb[:, cc * 1024 + half * 512: cc * 1024 + (half + 1) * 512],
                                 start=False, stop=False)
                    for c in range(8):
                        ins = t.matmul(bank[:, 0:512],
                                       x3t[i % 2][:, c * 128:(c + 1) * 128],
                                       dwsb[:, c * 1024 + half * 512: c * 1024 + (half + 1) * 512],
                                       start=False, stop=(c == 7))
                ins.then_inc(sem["t"], 1)
            p.emit("tensor", f)
            p.marks[("ssmm", i)] = p.inc("t", 1)

            for half in range(2):
                p.marks[("ssmev", i, half)] = p.vop(
                    [("t", p.marks[("ssmm", i)])] if half == 0 else [],
                    lambda v, i=i, half=half: v.tensor_copy(
                        ssm_sb[i % 2][:, half * 512:(half + 1) * 512],
                        pp[(i % 2) * 2 + half][:, 0:512]))
            ssm_done[i] = p.marks[("ssmev", i, 1)]

            sbf_d = p.vop([], lambda v, i=i: v.tensor_copy(ssmbf[:, :], ssm_sb[i % 2][:, :]))

            def f(t, i=i, sbf_d=sbf_d):
                t.wait_ge(sem["v"], sbf_d)
                if i > 0:
                    t.wait_ge(sem["v"], p.marks[("zfm_ev", i - 1)])
                for k in range(8):
                    ins = t.transpose(tps[:, k * 128:(k + 1) * 128],
                                      ssmbf[:, k * 128:(k + 1) * 128], eyeb[:, :])
                ins.then_inc(sem["t"], 1)
            p.emit("tensor", f)
            str_d = p.inc("t", 1)

            sfm_d = p.vop([("t", str_d)], lambda v: v.tensor_copy(ssmfm[:, :], tps[:, :]))

            def f(t, i=i, sfm_d=sfm_d):
                t.wait_ge(sem["v"], sfm_d)
                if i >= 1:
                    t.wait_ge(sem["v"], ssm_done[i - 1])
                for half in range(2):
                    bank = pp[((i + 1) % 2) * 2 + half]
                    for c in range(8):
                        ins = t.matmul(bank[:, 0:512],
                                       ssmfm[:, c * 128:(c + 1) * 128],
                                       w2sb[:, c * 1024 + half * 512: c * 1024 + (half + 1) * 512],
                                       start=(c == 0), stop=(c == 7))
                ins.then_inc(sem["t"], 1)
            p.emit("tensor", f)
            w2mm = p.inc("t", 1)

            for half in range(2):
                p.marks[("rsilu", i, half)] = p.sop(
                    [("t", w2mm)] if half == 0 else [],
                    lambda s, i=i, half=half: s.activation(
                        r_sb[:, half * 512:(half + 1) * 512],
                        pp[((i + 1) % 2) * 2 + half][:, 0:512], AF.Silu))

            z_d = p.vop([("s", p.marks[("rsilu", i, 1)])],
                        lambda v, i=i: v.tensor_mul(zbf[:, :], ssm_sb[i % 2][:, :], r_sb[:, :]))

            def f(t, i=i, z_d=z_d, sfm_d=sfm_d):
                t.wait_ge(sem["v"], z_d)
                t.wait_ge(sem["v"], sfm_d)
                for k in range(8):
                    ins = t.transpose(tps[:, k * 128:(k + 1) * 128],
                                      zbf[:, k * 128:(k + 1) * 128], eyeb[:, :])
                ins.then_inc(sem["t"], 1)
            p.emit("tensor", f)
            p.marks[("sfm2", i)] = sfm_d
            ztr_d = p.inc("t", 1)

            p.marks[("zfm_ev", i)] = p.vop([("t", ztr_d)],
                                           lambda v: v.tensor_copy(zfm[:, :], tps[:, :]))

            def f(t, i=i):
                t.wait_ge(sem["v"], p.marks[("zfm_ev", i)])
                for half in range(2):
                    bank = pp[((i + 1) % 2) * 2 + half]
                    for c in range(8):
                        ins = t.matmul(bank[:, 0:512],
                                       zfm[:, c * 128:(c + 1) * 128],
                                       w3sb[:, c * 1024 + half * 512: c * 1024 + (half + 1) * 512],
                                       start=(c == 0), stop=(c == 7))
                ins.then_inc(sem["t"], 1)
            p.emit("tensor", f)
            w3mm = p.inc("t", 1)

            for half in range(2):
                wv_ = [("t", w3mm), ("ldx2", ldx2_i)] if half == 0 else []
                if half == 0 and i >= 2:
                    wv_.append(("stx", 16 * (i - 1)))
                p.vop(wv_, lambda v, i=i, half=half: v.tensor_add(
                    outs[i % 2][:, half * 512:(half + 1) * 512],
                    pp[((i + 1) % 2) * 2 + half][:, 0:512],
                    ssm_sb[i % 2][:, half * 512:(half + 1) * 512]))
                res_done[i] = p.vop([], lambda v, i=i, half=half: v.tensor_add(
                    outs[i % 2][:, half * 512:(half + 1) * 512],
                    outs[i % 2][:, half * 512:(half + 1) * 512],
                    x2t[i % 2][:, half * 512:(half + 1) * 512]))

            def f(sy, i=i):
                sy.wait_ge(sem["v"], res_done[i])
                if i > 0:
                    sy.wait_ge(sem["stx"], 16 * i)
                sy.dma_start(out=x3s[i * 128:(i + 1) * 128, :],
                             in_=outs[i % 2][:, :]).then_inc(sem["stx"], 16)
            p.emit("sync", f)
            p.inc("stx", 16)

        def g_final(g):
            g.wait_ge(sem["stx"], 16 * nq)
        p.emit("gpsimd", g_final)




def emit_d(nc, p, sem, dram, ps, nq=NT):
    TOKQ = nq * 128
    NB = 2 * nq
    TOKB = NB * 128
    NGB = NB // 4

    xr1 = dram["xr1"]
    xr2 = dram["xr2"]
    qkvT = dram["qkvT"]
    projT = dram["projT"]
    rwT = dram["rwT"]
    n1r = dram["n1r"]
    n2r = dram["n2r"]
    wgT = dram["wgT"]
    wuT = dram["wuT"]
    wdT = dram["wdT"]
    cosq = dram["cosq"]
    sinq = dram["sinq"]
    cosk = dram["cosk"]
    sink = dram["sink"]
    m128 = dram["m128"]
    cmask = dram["cmask"]
    flagd = dram["flagd"]
    eye = dram["eye"]
    s4 = dram["s4"]
    kd = dram["kd"]
    vd = dram["vd"]
    qd = dram["qd"]
    ud = dram["ud"]
    u2d = dram["u2d"]
    y2d = dram["y2d"]
    hmd = dram["hmd"]

    es = ExitStack()
    with es:
        NA = 32768 * 2 + 8192 + 8192
        aB = es.enter_context(nc.sbuf_tensor("aBd", [128, NA], BF16))
        RA = aB[:, 0:32768]
        RB = aB[:, 32768:65536]
        qkvw = aB[:, 0:24576]
        ckt = aB[:, 24576:24576 + TOKB]
        skt = aB[:, 24576 + TOKB:24576 + 2 * TOKB]
        cqt = aB[:, 24576 + 2 * TOKB:24576 + 2 * TOKB + TOKQ]
        sqt = aB[:, 24576 + 2 * TOKB + TOKQ:24576 + 2 * TOKB + 2 * TOKQ]
        o1 = 24576 + 2 * TOKB + 2 * TOKQ
        y1roll = aB[:, o1:o1 + 4096]
        ktmp = aB[:, o1 + 4096:o1 + 4608]
        y1bf = aB[:, o1 + 4608:o1 + 5632]
        kout = aB[:, o1 + 5632:o1 + 6144]
        khp = aB[:, 0:TOKB]
        qhp = aB[:, TOKB:TOKB + TOKQ]
        vhp = aB[:, TOKB + TOKQ:2 * TOKB + TOKQ]
        probs = aB[:, 2 * TOKB + TOKQ:3 * TOKB + TOKQ]
        probsT = aB[:, 3 * TOKB + TOKQ:4 * TOKB + TOKQ]
        avall = aB[:, 4 * TOKB + TOKQ:4 * TOKB + TOKQ + nq * 1024]
        o2 = 4 * TOKB + TOKQ + nq * 1024
        projw = aB[:, o2:o2 + 8192]
        rww = aB[:, o2 + 8192:o2 + 8256]
        avt = aB[:, o2 + 8256:o2 + 9280]
        avfm = aB[:, o2 + 9280:o2 + 10304]
        y2bf = aB[:, o2 + 10304:o2 + 11328]
        y2fm = aB[:, o2 + 11328:o2 + 12352]
        h_sb = aB[:, 65536:65536 + 8192]
        y2t = [aB[:, 65536 + 8192 + i * 4096:65536 + 8192 + (i + 1) * 4096]
               for i in range(2)]
        hsub = [aB[:, 65536:65536 + 4096], aB[:, 65536 + 4096:65536 + 8192]]

        aF = es.enter_context(nc.sbuf_tensor("aFd", [128, 5120 + max(TOKB, 3072) + 128], F32))
        xet = aF[:, 0:1024]
        ut = aF[:, 1024:2048]
        scr = aF[:, 2048:3072]
        n1b = aF[:, 3072:4096]
        n2b = aF[:, 4096:5120]
        score = aF[:, 5120:5120 + TOKB]
        u2t = [aF[:, 5120 + 1024:5120 + 2048], aF[:, 5120 + 2048:5120 + 3072]]
        otile = [aF[:, 0:1024], aF[:, 1024:2048]]
        gs = aF[:, 2048:2560]
        cmt = aF[:, 5120 + max(TOKB, 3072):5120 + max(TOKB, 3072) + 128]
        small = es.enter_context(nc.sbuf_tensor("smalld", [128, 24], F32))
        sum1 = small[:, 0:1]; mean = small[:, 1:2]; ex2 = small[:, 2:3]
        m2 = small[:, 3:4]; var = small[:, 4:5]; lnv = small[:, 5:6]
        rstd = small[:, 6:7]; sumsq = small[:, 7:8]
        negm = small[:, 8:9]; bias1 = small[:, 9:10]
        ssum1 = small[:, 10:11]; ssum2 = small[:, 11:12]; ssum = small[:, 12:13]
        rr2 = small[:, 13:15]
        epsv = small[:, 15:16]; nhalf = small[:, 16:17]; eps6 = small[:, 17:18]
        m1p = small[:, 18:19]; m2p = small[:, 19:20]; negm2 = small[:, 20:21]
        w1g = small[:, 21:22]; w2g = small[:, 22:23]
        flagt = es.enter_context(nc.sbuf_tensor("flagtd", [128, 1], F32))
        geff = es.enter_context(nc.sbuf_tensor("geffd", [128, NT * 8], F32))
        lgr = es.enter_context(nc.sbuf_tensor("lgrd", [128, 8], F32))
        lg2 = es.enter_context(nc.sbuf_tensor("lg2d", [128, 8], F32))
        lge = es.enter_context(nc.sbuf_tensor("lged", [128, 8], F32))
        oh1 = es.enter_context(nc.sbuf_tensor("oh1d", [128, 8], F32))
        oh2 = es.enter_context(nc.sbuf_tensor("oh2d", [128, 8], F32))
        eyeb = es.enter_context(nc.sbuf_tensor("eyebd", [128, 128], BF16))
        m128b = es.enter_context(nc.sbuf_tensor("m128b", [128, 128], BF16))
        onesb = es.enter_context(nc.sbuf_tensor("onesbd", [1, 128], BF16))
        onesf = es.enter_context(nc.sbuf_tensor("onesfd", [1, 128], F32))

        tpsD = ps["tb0"]
        tpsD2 = ps["tb1"]
        pb = [ps["pf0"], ps["pf1"], ps["pf2"], ps["pf3"], ps["pf4"], ps["pf5"], ps["pf5"]]


        def g_init(g):
            g.dma_start(out=eyeb[:, :], in_=eye[:, :]).then_inc(sem["ldw"], 16)
            g.dma_start(out=m128b[:, :], in_=m128[:, :]).then_inc(sem["ldw"], 16)
            for c in range(8):
                g.dma_start(out=qkvw[:, c * 3072:(c + 1) * 3072],
                            in_=qkvT[c * 128:(c + 1) * 128, :]).then_inc(sem["ldw"], 16)
            g.dma_start(out=ckt, in_=cosk[:, :]).then_inc(sem["ldw"], 16)
            g.dma_start(out=skt, in_=sink[:, :]).then_inc(sem["ldw"], 16)
            g.dma_start(out=cqt, in_=cosq[:, :]).then_inc(sem["ldw"], 16)
            g.dma_start(out=sqt, in_=sinq[:, :]).then_inc(sem["ldw"], 16)
            g.dma_start(out=n1r_row, in_=n1r[:, :]).then_inc(sem["ldw"], 16)
            g.dma_start(out=n2r_row, in_=n2r[:, :]).then_inc(sem["ldw"], 16)
            g.dma_start(out=cmt, in_=cmask[:, :]).then_inc(sem["ldw"], 16)
            g.dma_start(out=flagt[:, :], in_=flagd[:, :]).then_inc(sem["ldw"], 16)
            g.memset(onesb[:, :], 1.0)
            g.memset(onesf[:, :], 1.0)
            g.memset(epsv, 1e-5)
            g.memset(eps6, 1e-6)
            g.memset(nhalf, -0.5).then_inc(sem["g"], 1)
        n1r_row = aF[0:1, 2048:3072]
        n2r_row = aF[0:1, 4096:5120]
        p.emit("gpsimd", g_init)
        ldw_init = p.inc("ldw", 16 * 18)
        g_done = p.inc("g", 1)

        def t_bc(t):
            t.wait_ge(sem["ldw"], ldw_init)
            t.wait_ge(sem["g"], g_done)
            t.matmul(pb[0][:, 0:512], onesf[:, :], n1r_row[:, 0:512], start=True, stop=True)
            t.matmul(pb[1][:, 0:512], onesf[:, :], n1r_row[:, 512:1024], start=True, stop=True)
            t.matmul(pb[2][:, 0:512], onesf[:, :], n2r_row[:, 0:512], start=True, stop=True)
            t.matmul(pb[3][:, 0:512], onesf[:, :], n2r_row[:, 512:1024],
                     start=True, stop=True).then_inc(sem["t"], 1)
        p.emit("tensor", t_bc)
        t_bc_d = p.inc("t", 1)
        p.vop([("t", t_bc_d)], lambda v: v.tensor_copy(n1b[:, 0:512], pb[0][:, 0:512]))
        p.vop([], lambda v: v.tensor_copy(n1b[:, 512:1024], pb[1][:, 0:512]))
        p.vop([], lambda v: v.tensor_copy(n2b[:, 0:512], pb[2][:, 0:512]))
        bc_done = p.vop([], lambda v: v.tensor_copy(n2b[:, 512:1024], pb[3][:, 0:512]))

        ln_done = [0] * NB
        sq_done = [0] * NB
        yfm_done = [0] * NB

        for i in range(NB):
            src_ap = (xr1[i * 128:(i + 1) * 128, :] if i < nq
                      else xr2[(i - nq) * 128:(i - nq + 1) * 128, :])

            def f(sy, i=i, src_ap=src_ap):
                if i > 0:
                    sy.wait_ge(sem["v"], ln_done[i - 1])
                    sy.wait_ge(sem["s"], sq_done[i - 1])
                sy.dma_start(out=xet, in_=src_ap).then_inc(sem["ldx"], 16)
            p.emit("sync", f)
            ldx_i = p.inc("ldx", 16)

            w0 = [("ldx", ldx_i)] + ([("v", bc_done)] if i == 0 else [])
            p.vop(w0, lambda v: v.reduce_sum(sum1, xet, axis=AX.X))
            p.vop([], lambda v: v.tensor_scalar_mul(mean, sum1, 1.0 / D))
            sq1 = p.sop(w0, lambda s: s.activation(scr, xet, AF.Square, accum_out=sumsq))
            p.vop([("s", sq1)], lambda v: v.tensor_scalar_mul(ex2, sumsq, 1.0 / D))
            p.vop([], lambda v: v.tensor_mul(m2, mean, mean))
            var_d = p.vop([], lambda v: v.tensor_sub(var, ex2, m2))
            p.sop([("v", var_d)], lambda s: s.activation(lnv, var, AF.Ln, bias=epsv))
            rstd_d = p.sop([], lambda s: s.activation(rstd, lnv, AF.Exp, scale=nhalf))
            wv_u = [("s", rstd_d)]
            if i >= nq + 1:
                wv_u.append(("stu", 16 * (i - nq)))
            u_d = p.vop(wv_u,
                        lambda v: v.tensor_scalar(ut, xet, mean, rstd, ALU.subtract, ALU.mult))

            if i >= nq:
                def f(sy, i=i, u_d=u_d):
                    sy.wait_ge(sem["v"], u_d)
                    if i > nq:
                        sy.wait_ge(sem["stu"], 16 * (i - nq))
                    sy.dma_start(out=ud[(i - nq) * 128:(i - nq + 1) * 128, :],
                                 in_=ut).then_inc(sem["stu"], 16)
                p.emit("sync", f)
                p.inc("stu", 16)

            sq2 = p.sop([("v", u_d)], lambda s: s.activation(scr, ut, AF.Square, accum_out=sumsq))
            p.vop([("s", sq2)], lambda v: v.tensor_scalar_mul(ex2, sumsq, 1.0 / D))
            p.sop([("v", p.cur("v"))], lambda s: s.activation(lnv, ex2, AF.Ln, bias=eps6))
            rms_d = p.sop([], lambda s: s.activation(rstd, lnv, AF.Exp, scale=nhalf))
            p.vop([("s", rms_d)], lambda v: v.tensor_scalar(scr, ut, rstd, None, ALU.mult))
            ln_done[i] = p.vop([], lambda v: v.tensor_mul(y1bf, scr, n1b))

            def f(t, i=i):
                t.wait_ge(sem["v"], ln_done[i])
                if i > 0:
                    t.wait_ge(sem["v"], yfm_done[i - 1])
                for k in range(8):
                    ins = t.transpose(tpsD[:, k * 128:(k + 1) * 128],
                                      y1bf[:, k * 128:(k + 1) * 128], eyeb[:, :])
                ins.then_inc(sem["t"], 1)
            p.emit("tensor", f)
            ytr = p.inc("t", 1)

            def fv(v, i=i):
                return v.tensor_copy(
                    y1roll.rearrange("p (c n) -> p c n", c=8)[:, :, (i % 4) * 128:(i % 4 + 1) * 128],
                    tpsD[:, :].rearrange("p (c n) -> p c n", c=8))
            wv_ = [("t", ytr)]
            if i >= 4:
                wv_.append(("t", p.marks[("p1mm", i // 4 - 1)]))
            yfm_done[i] = p.vop(wv_, fv)

            if i % 4 != 3:
                continue
            gi = i // 4
            g0 = gi * 512

            for dc in range(8):
                def f(t, gi=gi, dc=dc, i=i):
                    if dc == 0:
                        t.wait_ge(sem["v"], yfm_done[i])
                        if gi == 0:
                            t.wait_ge(sem["v"], bc_done)
                        else:
                            t.wait_ge(sem["v"], p.marks[("pgv", gi - 1)])
                    else:
                        t.wait_ge(sem["v"], p.marks[("kev", gi, dc - 1)])
                    for c in range(8):
                        ins = t.matmul(pb[0][:, 0:512],
                                       qkvw[:, c * 3072 + 1024 + dc * 128: c * 3072 + 1024 + (dc + 1) * 128],
                                       y1roll[:, c * 512:(c + 1) * 512],
                                       start=(c == 0), stop=(c == 7))
                    ins.then_inc(sem["t"], 1)
                p.emit("tensor", f)
                kmm = p.inc("t", 1)

                kev = p.vop([("t", kmm)], lambda v: v.tensor_copy(ktmp, pb[0][:, 0:512]))
                p.marks[("kev", gi, dc)] = kev

                def f(t, kev=kev, gi=gi, dc=dc):
                    t.wait_ge(sem["v"], kev)
                    if gi > 0 or dc > 0:
                        t.wait_ge(sem["v"], p.marks.get(("kr2", gi, dc - 1),
                                                        p.marks.get(("kr2", gi - 1, 7), 0)))
                    for n0 in range(4):
                        ins = t.matmul(pb[1][:, n0 * 128:(n0 + 1) * 128], m128b[:, :],
                                       ktmp[:, n0 * 128:(n0 + 1) * 128],
                                       start=True, stop=True)
                    ins.then_inc(sem["t"], 1)
                p.emit("tensor", f)
                rot = p.inc("t", 1)

                p.vop([("t", rot)],
                      lambda v, g0=g0: v.tensor_mul(score[:, 0:512], pb[1][:, 0:512],
                                                    skt[:, g0:g0 + 512]))
                p.vop([], lambda v, g0=g0: v.tensor_mul(score[:, 512:1024], ktmp,
                                                        ckt[:, g0:g0 + 512]))
                wv_k = [("stk", 16 * (gi * 8 + dc))]
                if dc == 0:
                    wv_k.append(("stv", 16 * gi * 8))
                    if gi - 1 >= NGB // 2:
                        wv_k.append(("stq", 16 * 8 * (gi - NGB // 2)))
                kr2 = p.vop(wv_k, lambda v: v.tensor_add(kout, score[:, 0:512],
                                                         score[:, 512:1024]))
                p.marks[("kr2", gi, dc)] = kr2

                def f(sy, gi=gi, dc=dc, kr2=kr2):
                    sy.wait_ge(sem["v"], kr2)
                    sy.wait_ge(sem["stk"], 16 * (gi * 8 + dc))
                    sy.dma_start(out=kd[dc * 128:(dc + 1) * 128, gi * 512:(gi + 1) * 512],
                                 in_=kout).then_inc(sem["stk"], 16)
                p.emit("sync", f)
                p.inc("stk", 16)

            for tt in range(4):
                for half in range(2):
                    idx = tt * 2 + half

                    def f(t, gi=gi, tt=tt, half=half, idx=idx):
                        if idx >= 2:
                            t.wait_ge(sem["v"], p.marks[("vev2", gi, idx - 2)])
                        elif gi >= 1:
                            t.wait_ge(sem["v"], p.marks[("vev2", gi - 1, 6 + idx)])
                        bank = pb[2 + idx % 2]
                        for c in range(8):
                            ins = t.matmul(bank[:, 0:512],
                                           y1roll[:, c * 512 + tt * 128: c * 512 + (tt + 1) * 128],
                                           qkvw[:, c * 3072 + 2048 + half * 512: c * 3072 + 2048 + (half + 1) * 512],
                                           start=(c == 0), stop=(c == 7))
                        ins.then_inc(sem["t"], 1)
                    p.emit("tensor", f)
                    vmm = p.inc("t", 1)

                    wv_v = [("t", vmm)]
                    if idx == 0:
                        wv_v.append(("stk", 16 * (gi + 1) * 8))
                    else:
                        wv_v.append(("stv", 16 * (gi * 8 + idx)))
                    vev = p.vop(wv_v,
                                lambda v, half=half, idx=idx: v.tensor_copy(
                                    kout, pb[2 + idx % 2][:, 0:512]))
                    p.marks[("vev2", gi, idx)] = vev
                    if idx == 7:
                        p.marks[("vev", gi, 7)] = vev

                    def f(sy, gi=gi, tt=tt, half=half, vev=vev):
                        sy.wait_ge(sem["v"], vev)
                        sy.wait_ge(sem["stv"], 16 * (gi * 8 + tt * 2 + half))
                        sy.dma_start(out=vd[(gi * 4 + tt) * 128:(gi * 4 + tt + 1) * 128,
                                            half * 512:(half + 1) * 512],
                                     in_=kout).then_inc(sem["stv"], 16)
                    p.emit("sync", f)
                    p.inc("stv", 16)

            if i >= nq + 3:
                qgi = gi - NGB // 2
                q0 = qgi * 512
                for dc in range(8):
                    def f(t, dc=dc, qgi=qgi, i=i):
                        if dc == 0:
                            t.wait_ge(sem["v"], p.marks[("vev2", i // 4, 7)])
                        else:
                            t.wait_ge(sem["v"], p.marks[("qev", qgi, dc - 1)])
                        for c in range(8):
                            ins = t.matmul(pb[0][:, 0:512],
                                           qkvw[:, c * 3072 + dc * 128: c * 3072 + (dc + 1) * 128],
                                           y1roll[:, c * 512:(c + 1) * 512],
                                           start=(c == 0), stop=(c == 7))
                        ins.then_inc(sem["t"], 1)
                    p.emit("tensor", f)
                    qmm = p.inc("t", 1)

                    qev = p.vop([("t", qmm)], lambda v: v.tensor_copy(ktmp, pb[0][:, 0:512]))
                    p.marks[("qev", qgi, dc)] = qev

                    def f(t, qev=qev, qgi=qgi, dc=dc):
                        t.wait_ge(sem["v"], qev)
                        if dc > 0:
                            t.wait_ge(sem["v"], p.marks[("qr2", qgi, dc - 1)])
                        else:
                            t.wait_ge(sem["v"], p.marks[("kr2", qgi + NGB // 2, 7)])
                        for n0 in range(4):
                            ins = t.matmul(pb[1][:, n0 * 128:(n0 + 1) * 128], m128b[:, :],
                                           ktmp[:, n0 * 128:(n0 + 1) * 128],
                                           start=True, stop=True)
                        ins.then_inc(sem["t"], 1)
                    p.emit("tensor", f)
                    rot = p.inc("t", 1)

                    p.vop([("t", rot)],
                          lambda v, q0=q0: v.tensor_mul(score[:, 0:512], pb[1][:, 0:512],
                                                        sqt[:, q0:q0 + 512]))
                    p.vop([], lambda v, q0=q0: v.tensor_mul(score[:, 512:1024], ktmp,
                                                            cqt[:, q0:q0 + 512]))
                    wv_q = [("stq", 16 * (qgi * 8 + dc))]
                    if dc == 0:
                        wv_q.append(("stv", 16 * (qgi + NGB // 2 + 1) * 8))
                    qr2 = p.vop(wv_q, lambda v: v.tensor_add(kout, score[:, 0:512],
                                                             score[:, 512:1024]))
                    p.marks[("qr2", qgi, dc)] = qr2

                    def f(sy, qgi=qgi, dc=dc, qr2=qr2):
                        sy.wait_ge(sem["v"], qr2)
                        sy.wait_ge(sem["stq"], 16 * (qgi * 8 + dc))
                        sy.dma_start(out=qd[dc * 128:(dc + 1) * 128, qgi * 512:(qgi + 1) * 512],
                                     in_=kout).then_inc(sem["stq"], 16)
                    p.emit("sync", f)
                    p.inc("stq", 16)

            p.marks[("p1mm", gi)] = p.cur("t")
            p.marks[("pgv", gi)] = p.cur("v")

        p1_v = p.cur("v")
        p1_t = p.cur("t")
        p1_s = p.cur("s")
        nstk = p.cur("stk")
        nstv = p.cur("stv")
        nstq = p.cur("stq")
        nstu = p.cur("stu")

        def f(g):
            g.wait_ge(sem["t"], p1_t)
            for c in range(8):
                g.dma_start(out=projw[:, c * 1024:(c + 1) * 1024],
                            in_=projT[c * 128:(c + 1) * 128, :]).then_inc(sem["ldw3"], 16)
            g.dma_start(out=rww[:, :].rearrange("p (c e) -> p c e", c=8),
                        in_=rwT.rearrange("(c p) e -> p c e", p=128)).then_inc(sem["ldw3"], 16)
        p.emit("gpsimd", f)
        ldproj = p.inc("ldw3", 16 * 9)

        for hp in range(8):
            def f(sy, hp=hp):
                if hp == 0:
                    sy.wait_ge(sem["stk"], nstk)
                    sy.wait_ge(sem["stq"], nstq)
                    sy.wait_ge(sem["stv"], nstv)
                else:
                    sy.wait_ge(sem["ldhp"], 48 * hp)
                    sy.wait_ge(sem["t"], p.marks[("avmm", hp - 1, nq - 1, 1)])
                sy.dma_start(out=khp, in_=kd[hp * 128:(hp + 1) * 128, :]).then_inc(sem["ldhp"], 16)
                sy.dma_start(out=qhp, in_=qd[hp * 128:(hp + 1) * 128, :]).then_inc(sem["ldhp"], 16)
                sy.dma_start(out=vhp.rearrange("p (tt n) -> p tt n", tt=NB),
                             in_=vd[:, hp * 128:(hp + 1) * 128]
                             .rearrange("(tt p) n -> p tt n", p=128)).then_inc(sem["ldhp"], 16)
            p.emit("sync", f)
            ldhp_d = p.inc("ldhp", 48)

            for qb in range(nq):
                kl = TOKQ + (qb + 1) * 128
                nkc = (kl + 511) // 512
                nkt = kl // 128
                for h in range(2):
                    base = 64 * h
                    for c5 in range(nkc):
                        n0 = c5 * 512
                        n1 = min(n0 + 512, kl)

                        def f(t, hp=hp, qb=qb, h=h, c5=c5, n0=n0, n1=n1, base=base,
                              ldhp_d=ldhp_d):
                            if c5 == 0 and h == 0 and qb == 0:
                                t.wait_ge(sem["ldhp"], ldhp_d)
                                t.wait_ge(sem["ldw3"], ldproj)
                                if hp == 0:
                                    t.wait_ge(sem["v"], p1_v)
                            if c5 >= 2:
                                mk = p.marks[("scev", hp, qb, h, c5 - 2)]
                                t.wait_ge(sem[mk[0]], mk[1])
                            elif (h > 0 or qb > 0 or hp > 0) and c5 < 2:
                                pk = p.marks.get(("sclast", hp, qb, h))
                                if pk:
                                    for mk in pk:
                                        t.wait_ge(sem[mk[0]], mk[1])
                            t.matmul(pb[c5 % 2][:, 0:n1 - n0],
                                     qhp[base:base + 64, qb * 128:(qb + 1) * 128],
                                     khp[base:base + 64, n0:n1],
                                     start=True, stop=True).then_inc(sem["t"], 1)
                        p.emit("tensor", f)
                        smm = p.inc("t", 1)

                        if c5 % 2 == 0:
                            wv_e = [("t", smm)]
                            if c5 == 0:
                                pr = p.marks.get(("rmaxprev",))
                                if pr:
                                    wv_e.append(pr)
                            mk = ("s", p.sop(
                                wv_e,
                                lambda s, c5=c5, n0=n0, n1=n1: s.activation(
                                    score[:, n0:n1], pb[c5 % 2][:, 0:n1 - n0],
                                    AF.Identity)))
                        else:
                            mk = ("v", p.vop(
                                [("t", smm)],
                                lambda v, c5=c5, n0=n0, n1=n1: v.tensor_copy(
                                    score[:, n0:n1], pb[c5 % 2][:, 0:n1 - n0])))
                        p.marks[("scev", hp, qb, h, c5)] = mk
                    p.marks[("sclast", hp, qb, 1 - h) if h == 0 else
                            ("sclast", hp, qb + 1, 0) if qb + 1 < nq else
                            ("sclast", hp + 1, 0, 0)] = (
                        p.marks[("scev", hp, qb, h, nkc - 1)],
                        p.marks[("scev", hp, qb, h, nkc - 2)])

                    evw = [p.marks[("scev", hp, qb, h, c5)] for c5 in range(nkc)]
                    evw = [m for m in evw if m[0] == "s"][-1:] + [
                        m for m in evw if m[0] == "v"][-1:]
                    dm = p.vop(evw, lambda v, kl=kl: v.tensor_add(
                        score[:, kl - 128:kl], score[:, kl - 128:kl], cmt))
                    rmax_d = p.vop([], lambda v, kl=kl: v.reduce_max(
                        negm, score[:, 0:kl], axis=AX.X, negate=True))
                    p.marks[("rmaxprev",)] = ("v", rmax_d)
                    ng_d = p.vop([], lambda v: v.tensor_add(bias1, negm, flagt[:, :]))
                    e1 = p.sop([("v", ng_d)],
                               lambda s: s.activation(probs[:, 0:TOKQ], score[:, 0:TOKQ],
                                                      AF.Exp, bias=bias1, accum_out=ssum1))
                    e2 = p.sop([], lambda s, kl=kl: s.activation(
                        probs[:, TOKQ:kl], score[:, TOKQ:kl], AF.Exp, bias=negm,
                        accum_out=ssum2))
                    sm_d = p.vop([("s", e2)], lambda v: v.tensor_add(ssum, ssum1, ssum2))
                    rr_d = p.vop([], lambda v, h=h: v.reciprocal(rr2[:, h:h + 1], ssum))

                    ptev = 0
                    for tg in range((nkt + 7) // 8):
                        lo = tg * 8
                        hi = min(lo + 8, nkt)
                        bnk = tpsD if tg % 2 == 0 else tpsD2

                        prev_ptev = p.marks.get(("ptev_par", tg % 2), 0)

                        def f(t, hp=hp, qb=qb, h=h, tg=tg, lo=lo, hi=hi, e2=e2,
                              prev_ptev=prev_ptev, bnk=bnk):
                            t.wait_ge(sem["s"], e2)
                            if prev_ptev:
                                t.wait_ge(sem["v"], prev_ptev)
                            for kc in range(lo, hi):
                                ins = t.transpose(bnk[:, (kc - lo) * 128:(kc - lo + 1) * 128],
                                                  probs[:, kc * 128:(kc + 1) * 128], eyeb[:, :])
                            ins.then_inc(sem["t"], 1)
                        p.emit("tensor", f)
                        ptr = p.inc("t", 1)

                        ptev = p.vop([("t", ptr)],
                                     lambda v, lo=lo, hi=hi, bnk=bnk: v.tensor_copy(
                                         probsT[:, lo * 128:hi * 128],
                                         bnk[:, 0:(hi - lo) * 128]))
                        p.marks[("ptev_par", tg % 2)] = ptev
                        p.marks[("ptev_prev",)] = ptev

                    def f(t, hp=hp, qb=qb, h=h, nkt=nkt, ptev=ptev):
                        t.wait_ge(sem["v"], ptev)
                        if h == 0 and (qb > 0 or hp > 0):
                            pk = p.marks.get(("avev_done", hp, qb - 1) if qb > 0
                                             else ("avev_done", hp - 1, nq - 1))
                            if pk:
                                t.wait_ge(sem["s"], pk)
                        for kc in range(nkt):
                            ins = t.matmul(pb[4][:, h * 64:(h + 1) * 64],
                                           probsT[:, kc * 128:(kc + 1) * 128],
                                           vhp[:, kc * 128 + h * 64: kc * 128 + h * 64 + 64],
                                           start=(h == 0 and kc == 0),
                                           stop=(h == 1 and kc == nkt - 1))
                        ins.then_inc(sem["t"], 1)
                    p.emit("tensor", f)
                    p.marks[("avmm", hp, qb, h)] = p.inc("t", 1)

                for h in range(2):
                    p.marks[("avev", hp, qb, h)] = p.sop(
                        [("t", p.marks[("avmm", hp, qb, 1)]),
                         ("v", p.marks.get(("rrd", hp, qb), rr_d))],
                        lambda s, hp=hp, qb=qb, h=h: s.activation(
                            avall[:, qb * 1024 + hp * 128 + h * 64:
                                  qb * 1024 + hp * 128 + (h + 1) * 64],
                            pb[4][:, h * 64:(h + 1) * 64], AF.Identity,
                            scale=rr2[:, h:h + 1]))
                p.marks[("rrd", hp, qb)] = rr_d
                p.marks[("avev_done", hp, qb)] = p.marks[("avev", hp, qb, 1)]

        for qb in range(nq):
            def f(sy, qb=qb):
                if qb == 0:
                    sy.wait_ge(sem["stu"], nstu)
                else:
                    sy.wait_ge(sem["ldu"], 16 * qb)
                if qb >= 2:
                    sy.wait_ge(sem["v"], p.marks[("u2_d", qb - 2)])
                sy.dma_start(out=ut, in_=ud[qb * 128:(qb + 1) * 128, :]).then_inc(sem["ldu"], 16)
            p.emit("sync", f)
            ldu_d = p.inc("ldu", 16)

            def f(sy, qb=qb):
                if qb == 0:
                    sy.wait_ge(sem["v"], p1_v)
                    sy.wait_ge(sem["s"], p1_s)
                else:
                    sy.wait_ge(sem["ldxo"], 16 * qb)
                if qb >= 2:
                    sy.wait_ge(sem["v"], p.marks[("s4i_d", qb - 2)])
                sy.dma_start(out=xet, in_=xr2[qb * 128:(qb + 1) * 128, :]
                             ).then_inc(sem["ldxo"], 16)
            p.emit("sync", f)
            ldxo_d = p.inc("ldxo", 16)

            ptev_final = max(p.marks.get(("ptev_par", 0), 0),
                             p.marks.get(("ptev_par", 1), 0))

            def f(t, qb=qb, ptev_final=ptev_final):
                t.wait_ge(sem["s"], p.marks[("avev_done", 7, qb)])
                if qb > 0:
                    t.wait_ge(sem["v"], p.marks[("y2fm_d", qb - 1)])
                else:
                    t.wait_ge(sem["v"], ptev_final)
                for k in range(8):
                    ins = t.transpose(tpsD[:, k * 128:(k + 1) * 128],
                                      avall[:, qb * 1024 + k * 128: qb * 1024 + (k + 1) * 128],
                                      eyeb[:, :])
                ins.then_inc(sem["t"], 1)
            p.emit("tensor", f)
            avtr = p.inc("t", 1)
            p.marks[("avfm_d", qb)] = p.vop([("t", avtr)],
                                            lambda v: v.tensor_copy(avfm, tpsD[:, :]))

            def f(t, qb=qb):
                t.wait_ge(sem["v"], p.marks[("avfm_d", qb)])
                if qb > 0:
                    t.wait_ge(sem["v"], p.marks[("u2h_d", qb - 1)])
                for half in range(2):
                    bank = pb[2 + half]
                    for c in range(8):
                        ins = t.matmul(bank[:, 0:512],
                                       avfm[:, c * 128:(c + 1) * 128],
                                       projw[:, c * 1024 + half * 512: c * 1024 + (half + 1) * 512],
                                       start=(c == 0), stop=(c == 7))
                ins.then_inc(sem["t"], 1)
            p.emit("tensor", f)
            pjmm = p.inc("t", 1)

            for half in range(2):
                wv_ = [("t", pjmm), ("ldu", ldu_d)] if half == 0 else []
                if half == 0 and qb >= 2:
                    wv_.append(("s", p.marks[("sq2u", qb - 2)]))
                    wv_.append(("stu2", 16 * (qb - 1)))
                p.marks[("u2h_d", qb)] = p.vop(
                    wv_, lambda v, qb=qb, half=half: v.tensor_add(
                        u2t[qb % 2][:, half * 512:(half + 1) * 512],
                        pb[2 + half][:, 0:512],
                        ut[:, half * 512:(half + 1) * 512]))
            p.marks[("u2_d", qb)] = p.marks[("u2h_d", qb)]
            wv_ = [("ldxo", ldxo_d)]
            if qb > 0:
                wv_.append(("sts4i", 16 * qb))
            s4i = p.vop(wv_, lambda v, qb=qb: v.tensor_add(scr, u2t[qb % 2], xet))
            p.marks[("s4i_d", qb)] = s4i

            def f(sy, qb=qb, s4i=s4i):
                sy.wait_ge(sem["v"], s4i)
                if qb > 0:
                    sy.wait_ge(sem["sts4i"], 16 * qb)
                sy.dma_start(out=s4[qb * 128:(qb + 1) * 128, :], in_=scr).then_inc(sem["sts4i"], 16)
            p.emit("sync", f)
            p.inc("sts4i", 16)

            def f(sy, qb=qb):
                sy.wait_ge(sem["v"], p.marks[("u2_d", qb)])
                if qb > 0:
                    sy.wait_ge(sem["stu2"], 16 * qb)
                sy.dma_start(out=u2d[qb * 128:(qb + 1) * 128, :],
                             in_=u2t[qb % 2][:, :]).then_inc(sem["stu2"], 16)
            p.emit("sync", f)
            p.inc("stu2", 16)

            sq2 = p.sop([("v", p.marks[("u2_d", qb)])],
                        lambda s, qb=qb: s.activation(score[:, 0:1024], u2t[qb % 2],
                                                      AF.Square, accum_out=sumsq))
            p.marks[("sq2u", qb)] = sq2
            p.vop([("s", sq2)], lambda v: v.tensor_scalar_mul(ex2, sumsq, 1.0 / D))
            p.sop([("v", p.cur("v"))], lambda s: s.activation(lnv, ex2, AF.Ln, bias=eps6))
            rms_d = p.sop([], lambda s: s.activation(rstd, lnv, AF.Exp, scale=nhalf))
            p.vop([("s", rms_d)],
                  lambda v, qb=qb: v.tensor_scalar(score[:, 0:1024], u2t[qb % 2], rstd,
                                                   None, ALU.mult))
            y2_d = p.vop([], lambda v: v.tensor_mul(y2bf, score[:, 0:1024], n2b))

            def f(t, qb=qb, y2_d=y2_d):
                t.wait_ge(sem["v"], y2_d)
                t.wait_ge(sem["v"], p.marks[("avfm_d", qb)])
                for k in range(8):
                    ins = t.transpose(tpsD[:, k * 128:(k + 1) * 128],
                                      y2bf[:, k * 128:(k + 1) * 128], eyeb[:, :])
                ins.then_inc(sem["t"], 1)
            p.emit("tensor", f)
            y2tr = p.inc("t", 1)

            y2fm_d = p.vop([("t", y2tr)] + ([("sty2", 16 * qb)] if qb > 0 else []),
                           lambda v: v.tensor_copy(y2fm, tpsD[:, :]))
            p.marks[("y2fm_d", qb)] = y2fm_d

            def f(sy, qb=qb, y2fm_d=y2fm_d):
                sy.wait_ge(sem["v"], y2fm_d)
                sy.dma_start(out=y2d.rearrange("(c p) n -> p c n", p=128)[:, :, qb * 128:(qb + 1) * 128],
                             in_=y2fm.rearrange("p (c n) -> p c n", c=8)).then_inc(sem["sty2"], 16)
            p.emit("sync", f)
            p.inc("sty2", 16)

            def f(t, qb=qb, y2fm_d=y2fm_d):
                t.wait_ge(sem["v"], y2fm_d)
                if qb > 0:
                    t.wait_ge(sem["v"], p.marks[("geff_d", qb - 1)])
                for c in range(8):
                    ins = t.matmul(pb[6][:, 0:8], y2fm[:, c * 128:(c + 1) * 128],
                                   rww[:, c * 8:(c + 1) * 8],
                                   start=(c == 0), stop=(c == 7))
                ins.then_inc(sem["t"], 1)
            p.emit("tensor", f)
            rmm = p.inc("t", 1)

            p.vop([("t", rmm)], lambda v: v.tensor_copy(lgr[:, :], pb[6][:, 0:8]))
            p.vop([], lambda v: v.reduce_max(m1p, lgr[:, :], axis=AX.X))
            p.vop([], lambda v: v.tensor_scalar(oh1[:, :], lgr[:, :], m1p, None, ALU.is_equal))
            p.vop([], lambda v: v.scalar_tensor_tensor(lg2[:, :], oh1[:, :], -1e30,
                                                       lgr[:, :], ALU.mult, ALU.add))
            p.vop([], lambda v: v.reduce_max(m2p, lg2[:, :], axis=AX.X))
            oh2_d = p.vop([], lambda v: v.tensor_scalar(oh2[:, :], lg2[:, :], m2p, None,
                                                        ALU.is_equal))
            ng2 = p.vop([], lambda v: v.tensor_scalar_mul(negm2, m2p, -1.0))
            w1_d = p.sop([("v", ng2)],
                         lambda s: s.activation(w1g, m1p, AF.Sigmoid, bias=negm2))
            w2_d = p.vop([("s", w1_d)], lambda v: v.tensor_scalar(w2g, w1g, -1.0, 1.0,
                                                                  ALU.mult, ALU.add))
            p.vop([], lambda v: v.tensor_scalar(oh1[:, :], oh1[:, :], w1g, None, ALU.mult))
            p.vop([], lambda v: v.tensor_scalar(oh2[:, :], oh2[:, :], w2g, None, ALU.mult))
            p.marks[("geff_d", qb)] = p.vop(
                [], lambda v, qb=qb: v.tensor_add(geff[:, qb * 8:(qb + 1) * 8],
                                                  oh1[:, :], oh2[:, :]))

        p2_v = p.cur("v")
        p2_t = p.cur("t")
        p2_s = p.cur("s")
        nsty2 = p.cur("sty2")
        nstu2 = p.cur("stu2")
        nsts4i = p.cur("sts4i")

        NT3 = TOKQ // 512
        def wgreg(e):
            return RA if e % 2 == 0 else RB

        def wureg(e):
            return RB if e % 2 == 0 else RA

        def emit_wg_load(e):
            def f(g, e=e):
                if e > 0:
                    g.wait_ge(sem["t"], p.marks[("amm", e - 1)])
                else:
                    g.wait_ge(sem["t"], p2_t)
                    g.wait_ge(sem["v"], p2_v)
                    g.wait_ge(sem["s"], p2_s)
                    g.wait_ge(sem["sty2"], nsty2)
                for c in range(8):
                    g.dma_start(out=wgreg(e)[:, c * 4096:(c + 1) * 4096],
                                in_=wgT[e, c * 128:(c + 1) * 128, :]).then_inc(sem["ldwg"], 16)
            p.emit("gpsimd", f)
            return p.inc("ldwg", 16 * 8)

        def emit_wu_load(e):
            def f(g, e=e):
                if e > 0:
                    g.wait_ge(sem["t"], p.marks[("bmm", e - 1)])
                for c in range(8):
                    g.dma_start(out=wureg(e)[:, c * 4096:(c + 1) * 4096],
                                in_=wuT[e, c * 128:(c + 1) * 128, :]).then_inc(sem["ldwu"], 16)
            p.emit("gpsimd", f)
            return p.inc("ldwu", 16 * 8)

        def emit_wd_load(e):
            def f(g, e=e):
                g.wait_ge(sem["t"], p.marks[("amm", e)])
                for j4 in range(8):
                    g.dma_start(
                        out=wgreg(e)[:, j4 * 4096:(j4 + 1) * 4096].rearrange("p (gg d) -> p gg d", gg=4),
                        in_=wdT[e, j4 * 512:(j4 + 1) * 512, :].rearrange("(gg p) d -> p gg d", p=128),
                    ).then_inc(sem["ldwd"], 16)
            p.emit("gpsimd", f)
            return p.inc("ldwd", 16 * 8)

        wg_rdy = [0] * 8
        wu_rdy = [0] * 8
        wd_rdy = [0] * 8
        wg_rdy[0] = emit_wg_load(0)
        wu_rdy[0] = emit_wu_load(0)

        for e in range(8):
            for t_ in range(NT3):
                def f(sy, e=e, t_=t_):
                    j = e * NT3 + t_
                    if j == 0:
                        sy.wait_ge(sem["sty2"], nsty2)
                    else:
                        sy.wait_ge(sem["ldy2"], 16 * j)
                        sy.wait_ge(sem["t"], p.marks[("hmm_t", e, t_ - 1) if t_ > 0 else
                                                     ("hmm_t", e - 1, NT3 - 1)])
                    sy.dma_start(out=y2t[t_ % 2][:, :].rearrange("p (c n) -> p c n", c=8),
                                 in_=y2d.rearrange("(c p) n -> p c n", p=128)
                                 [:, :, t_ * 512:(t_ + 1) * 512]).then_inc(sem["ldy2"], 16)
                p.emit("sync", f)
                ldy2_d = p.inc("ldy2", 16)

                for fc in range(32):
                    gidx = (e * NT3 + t_) * 32 + fc

                    def f(t, e=e, t_=t_, fc=fc, gidx=gidx, ldy2_d=ldy2_d):
                        if fc == 0:
                            t.wait_ge(sem["ldy2"], ldy2_d)
                            t.wait_ge(sem["ldwg"], wg_rdy[e])
                            t.wait_ge(sem["ldwu"], wu_rdy[e])
                            if e == 0 and t_ == 0:
                                t.wait_ge(sem["v"], p2_v)
                                t.wait_ge(sem["s"], p2_s)
                        if gidx >= 2:
                            t.wait_ge(sem["s"], p.marks[("gsil", gidx - 2)])
                            t.wait_ge(sem["v"], p.marks[("hev", gidx - 2)])
                        for c in range(8):
                            ins = t.matmul(pb[gidx % 2][:, 0:512],
                                           wgreg(e)[:, c * 4096 + fc * 128: c * 4096 + (fc + 1) * 128],
                                           y2t[t_ % 2][:, c * 512:(c + 1) * 512],
                                           start=(c == 0), stop=(c == 7))
                        ins.then_inc(sem["t"], 1)
                        for c in range(8):
                            ins = t.matmul(pb[2 + gidx % 2][:, 0:512],
                                           wureg(e)[:, c * 4096 + fc * 128: c * 4096 + (fc + 1) * 128],
                                           y2t[t_ % 2][:, c * 512:(c + 1) * 512],
                                           start=(c == 0), stop=(c == 7))
                        ins.then_inc(sem["t"], 1)
                    p.emit("tensor", f)
                    p.inc("t", 1)
                    gumm = p.inc("t", 1)
                    if fc == 31:
                        p.marks[("hmm_t", e, t_)] = gumm

                    wv_g = [("t", gumm)]
                    if gidx == 0:
                        wv_g.append(("sts4i", nsts4i))
                    if gidx >= 1:
                        wv_g.append(("v", p.marks[("hev", gidx - 1)]))
                    p.marks[("gsil", gidx)] = p.sop(
                        wv_g,
                        lambda s, gidx=gidx: s.activation(
                            gs, pb[gidx % 2][:, 0:512], AF.Silu))
                    wv_h = [("s", p.marks[("gsil", gidx)])]
                    if fc == 0:
                        wv_h.append(("sth", 16 * 2 * (e * NT3 + t_)))
                    elif fc == 16:
                        wv_h.append(("sth", 16 * (2 * (e * NT3 + t_) + 1)))
                    p.marks[("hev", gidx)] = p.vop(
                        wv_h, lambda v, fc=fc, gidx=gidx: v.tensor_mul(
                            h_sb[:, (fc % 16) * 512:((fc % 16) + 1) * 512],
                            gs, pb[2 + gidx % 2][:, 0:512]))
                    if fc == 15 or fc == 31:
                        hf = fc // 16

                        def f(sy, e=e, t_=t_, hf=hf, gidx=gidx):
                            sy.wait_ge(sem["v"], p.marks[("hev", gidx)])
                            sy.wait_ge(sem["sth"], 16 * (2 * (e * NT3 + t_) + hf))
                            sy.dma_start(
                                out=hmd.rearrange("(fc p) n -> p fc n", p=128)
                                [:, hf * 16:(hf + 1) * 16, t_ * 512:(t_ + 1) * 512],
                                in_=h_sb[:, :].rearrange("p (fc n) -> p fc n", fc=16)
                            ).then_inc(sem["sth"], 16)
                        p.emit("sync", f)
                        p.inc("sth", 16)
            p.marks[("amm", e)] = p.marks[("hmm_t", e, NT3 - 1)]

            wd_rdy[e] = emit_wd_load(e)
            if e + 1 < 8:
                wg_rdy[e + 1] = emit_wg_load(e + 1)

            NSUB = NT3 * 4
            for t_ in range(NT3):
                for s4i_ in range(4):
                    sub = t_ * 4 + s4i_

                    def f(sy, e=e, t_=t_, s4i_=s4i_, sub=sub, NSUB=NSUB):
                        j = e * NSUB + sub
                        if j > 0:
                            sy.wait_ge(sem["ldh"], 16 * j)
                        if j >= 2:
                            sy.wait_ge(sem["t"], p.marks[("bmm_s", e, sub - 2) if sub >= 2
                                                         else ("bmm_s", e - 1, NSUB - 2 + sub)])
                        sy.wait_ge(sem["sth"], 16 * 2 * NT3 * (e + 1))
                        sy.dma_start(out=hsub[sub % 2][:, :].rearrange("p (fc n) -> p fc n", fc=32),
                                     in_=hmd.rearrange("(fc p) n -> p fc n", p=128)
                                     [:, :, sub * 128:(sub + 1) * 128]).then_inc(sem["ldh"], 16)
                    p.emit("sync", f)
                    ldh_d = p.inc("ldh", 16)

                    def f(t, e=e, sub=sub, ldh_d=ldh_d):
                        t.wait_ge(sem["ldh"], ldh_d)
                        t.wait_ge(sem["ldwd"], wd_rdy[e])
                        if e > 0 or sub >= 1:
                            t.wait_ge(sem["v"], p.marks.get(("oev", e, sub - 1),
                                                            p.marks.get(("oev", e - 1, NSUB - 1), 0)))
                        for half in range(2):
                            bank = pb[4 + half]
                            for fc in range(32):
                                ins = t.matmul(
                                    bank[:, 0:512],
                                    hsub[sub % 2][:, fc * 128:(fc + 1) * 128],
                                    wgreg(e)[:, fc * 1024 + half * 512: fc * 1024 + (half + 1) * 512],
                                    start=(fc == 0), stop=(fc == 31))
                        ins.then_inc(sem["t"], 1)
                    p.emit("tensor", f)
                    p.marks[("bmm_s", e, sub)] = p.inc("t", 1)

                    for half in range(2):
                        wv_ = [("t", p.marks[("bmm_s", e, sub)])] if half == 0 else []
                        if half == 0:
                            ai = e * NSUB + sub
                            if ai >= 2:
                                wv_.append(("acc", 16 * (ai - 1)))
                            if ai == 0:
                                wv_.append(("sts4i", nsts4i))
                                wv_.append(("ldxo", 16 * nq))
                            if ai == 1:
                                wv_.append(("ldu", 16 * nq))
                        p.marks[("oev", e, sub)] = p.vop(
                            wv_, lambda v, e=e, sub=sub, half=half: v.tensor_scalar(
                                otile[sub % 2][:, half * 512:(half + 1) * 512],
                                pb[4 + half][:, 0:512],
                                geff[:, sub * 8 + e: sub * 8 + e + 1], None, ALU.mult))

                    def f(g, e=e, sub=sub, NSUB=NSUB):
                        ai = e * NSUB + sub
                        g.wait_ge(sem["v"], p.marks[("oev", e, sub)])
                        if ai == 0:
                            g.wait_ge(sem["sts4i"], nsts4i)
                        else:
                            g.wait_ge(sem["acc"], 16 * ai)
                        g.dma_start(out=s4[sub * 128:(sub + 1) * 128, :],
                                    in_=otile[sub % 2][:, :],
                                    accum_op=ALU.add).then_inc(sem["acc"], 16)
                    p.emit("gpsimd", f)
                    p.inc("acc", 16)
            p.marks[("bmm", e)] = p.marks[("bmm_s", e, NSUB - 1)]
            if e + 1 < 8:
                wu_rdy[e + 1] = emit_wu_load(e + 1)

        total_acc = p.cur("acc")

        def g_final(g):
            g.wait_ge(sem["acc"], total_acc)
            g.wait_ge(sem["sts4i"], nsts4i)
        p.emit("gpsimd", g_final)



def build_b(nt=NT, e_used=8):
    nc = bass.Bass("TRN2", target_bir_lowering=False)
    dram = {}
    TOKb = nt * 128
    x2 = nc.dram_tensor("x2", [TOKb, D], F32, kind="ExternalInput").ap()
    dram["x2"] = x2
    wr = nc.dram_tensor("wr", [D, 8], F32, kind="ExternalInput").ap()
    dram["wr"] = wr
    br = nc.dram_tensor("br", [1, 8], F32, kind="ExternalInput").ap()
    dram["br"] = br
    lng = nc.dram_tensor("lng", [1, D], F32, kind="ExternalInput").ap()
    dram["lng"] = lng
    lnb = nc.dram_tensor("lnb", [1, D], F32, kind="ExternalInput").ap()
    dram["lnb"] = lnb
    w1 = nc.dram_tensor("w1", [8, D, 4096], F32, kind="ExternalInput").ap()
    dram["w1"] = w1
    b1 = nc.dram_tensor("b1pl", [128, 256], F32, kind="ExternalInput").ap()
    dram["b1pl"] = b1
    w2 = nc.dram_tensor("w2", [8, 4096, D], F32, kind="ExternalInput").ap()
    dram["w2"] = w2
    b2 = nc.dram_tensor("b2", [8, D], F32, kind="ExternalInput").ap()
    dram["b2"] = b2
    eye = nc.dram_tensor("eye", [128, 128], BF16, kind="ExternalInput").ap()
    dram["eye"] = eye
    x3 = nc.dram_tensor("x3", [TOKb, D], F32, kind="ExternalOutput").ap()
    dram["x3"] = x3
    xnd = nc.dram_tensor("xnd", [D, TOKb], BF16).ap()
    dram["xnd"] = xnd
    es = ExitStack()
    with es:
        ps = _mk_ps(nc, es)
        sem = {n: es.enter_context(nc.semaphore(n)) for n in
               ["v", "s", "t", "g"] + ['ldx', 'sti', 'stxn', 'ldi', 'ldw1', 'ldw2', 'ldxn', 'ldxn1', 'acc', 'th', 'to']}
        p = Plan(sem)
        emit_b(nc, p, sem, dram, ps, nt=nt, e_used=e_used)
        _replay(nc, p)
    return nc

def build_a(nq=NT):
    nc = bass.Bass("TRN2", target_bir_lowering=False)
    dram = {}
    NE = nq + 4
    TOKQ = nq * 128
    TOKE = NE * 128
    W = 640
    xe_d = nc.dram_tensor("xe", [TOKE, D], F32, kind="ExternalInput").ap()
    dram["xe"] = xe_d
    wq_d = nc.dram_tensor("wq", [D, D], F32, kind="ExternalInput").ap()
    dram["wq"] = wq_d
    wk_d = nc.dram_tensor("wk", [D, D], F32, kind="ExternalInput").ap()
    dram["wk"] = wk_d
    wv_d = nc.dram_tensor("wv", [D, D], F32, kind="ExternalInput").ap()
    dram["wv"] = wv_d
    wo_d = nc.dram_tensor("wo", [D, D], F32, kind="ExternalInput").ap()
    dram["wo"] = wo_d
    mask_d = nc.dram_tensor("mask", [nq, 128, W], F32, kind="ExternalInput").ap()
    dram["mask"] = mask_d
    eye = nc.dram_tensor("eye", [128, 128], BF16, kind="ExternalInput").ap()
    dram["eye"] = eye
    x1 = nc.dram_tensor("x1", [TOKQ, D], F32, kind="ExternalOutput").ap()
    dram["x1"] = x1
    es = ExitStack()
    with es:
        ps = _mk_ps(nc, es)
        sem = {n: es.enter_context(nc.semaphore(n)) for n in
               ["v", "s", "t", "g"] + ['ldx', 'ldm', 'ldo', 'ldwa', 'ldwo', 'stx']}
        p = Plan(sem)
        emit_a(nc, p, sem, dram, ps, nq=nq)
        _replay(nc, p)
    return nc

def build_c1(nq=NT):
    nc = bass.Bass("TRN2", target_bir_lowering=False)
    dram = {}
    TOKQ = nq * 128
    NTI = nq + 1
    DH = 2048
    CC = DH // 128
    xh = nc.dram_tensor("xh", [NTI * 128, D], F32, kind="ExternalInput").ap()
    dram["xh"] = xh
    w1T = nc.dram_tensor("w1T", [D, D], F32, kind="ExternalInput").ap()
    dram["w1T"] = w1T
    dlT = nc.dram_tensor("dlT", [D, DH], F32, kind="ExternalInput").ap()
    dram["dlT"] = dlT
    awT = nc.dram_tensor("awT", [D, DH], F32, kind="ExternalInput").ap()
    dram["awT"] = awT
    bwT = nc.dram_tensor("bwT", [D, DH], F32, kind="ExternalInput").ap()
    dram["bwT"] = bwT
    cwpl = nc.dram_tensor("cwpl", [128, 24], F32, kind="ExternalInput").ap()
    dram["cwpl"] = cwpl
    cbpl = nc.dram_tensor("cbpl", [128, 8], F32, kind="ExternalInput").ap()
    dram["cbpl"] = cbpl
    abpl = nc.dram_tensor("abpl", [128, CC], F32, kind="ExternalInput").ap()
    dram["abpl"] = abpl
    bbpl = nc.dram_tensor("bbpl", [128, CC], F32, kind="ExternalInput").ap()
    dram["bbpl"] = bbpl
    dbpl = nc.dram_tensor("dbpl", [128, CC], F32, kind="ExternalInput").ap()
    dram["dbpl"] = dbpl
    eye = nc.dram_tensor("eye", [128, 128], BF16, kind="ExternalInput").ap()
    dram["eye"] = eye
    hd = nc.dram_tensor("hd", [DH, TOKQ], F32, kind="ExternalOutput").ap()
    dram["hd"] = hd
    x3d = nc.dram_tensor("x3d", [D, TOKQ], BF16, kind="ExternalOutput").ap()
    dram["x3d"] = x3d
    totd = nc.dram_tensor("totd", [128, 2 * CC], F32, kind="ExternalOutput").ap()
    dram["totd"] = totd
    dram["xhh"] = dram["xh"][nq * 128:nq * 128 + 2, :]
    flh = nc.dram_tensor("flhal", [2, 1], F32, kind="ExternalInput").ap()
    dram["flhal"] = flh
    es = ExitStack()
    with es:
        ps = _mk_ps(nc, es)
        sem = {n: es.enter_context(nc.semaphore(n)) for n in
               ["v", "s", "t", "g"] + ['ldx', 'ldw', 'ldwc', 'sth', 'stx3', 'sttot']}
        p = Plan(sem)
        emit_c1(nc, p, sem, dram, ps, nq=nq)
        _replay(nc, p)
    return nc

def build_c2(nq=NT):
    nc = bass.Bass("TRN2", target_bir_lowering=False)
    dram = {}
    TOKQ = nq * 128
    DH = 2048
    CC = DH // 128
    hd = nc.dram_tensor("hd", [DH, TOKQ], F32, kind="ExternalInput").ap()
    dram["hd"] = hd
    x3d = nc.dram_tensor("x3d", [D, TOKQ], BF16, kind="ExternalInput").ap()
    dram["x3d"] = x3d
    carA = nc.dram_tensor("carA", [128, CC], F32, kind="ExternalInput").ap()
    dram["carA"] = carA
    carH = nc.dram_tensor("carH", [128, CC], F32, kind="ExternalInput").ap()
    dram["carH"] = carH
    x2o = nc.dram_tensor("x2o", [TOKQ, D], F32, kind="ExternalInput").ap()
    dram["x2o"] = x2o
    cwT = nc.dram_tensor("cwT", [DH, D], F32, kind="ExternalInput").ap()
    dram["cwT"] = cwT
    dwT = nc.dram_tensor("dwT", [D, D], F32, kind="ExternalInput").ap()
    dram["dwT"] = dwT
    w2T = nc.dram_tensor("w2T", [D, D], F32, kind="ExternalInput").ap()
    dram["w2T"] = w2T
    w3T = nc.dram_tensor("w3T", [D, D], F32, kind="ExternalInput").ap()
    dram["w3T"] = w3T
    cbr = nc.dram_tensor("cbr", [1, D], F32, kind="ExternalInput").ap()
    dram["cbr"] = cbr
    dbr = nc.dram_tensor("dbr", [1, D], F32, kind="ExternalInput").ap()
    dram["dbr"] = dbr
    eye = nc.dram_tensor("eye", [128, 128], BF16, kind="ExternalInput").ap()
    dram["eye"] = eye
    x3s = nc.dram_tensor("x3s", [TOKQ, D], F32, kind="ExternalOutput").ap()
    dram["x3s"] = x3s
    es = ExitStack()
    with es:
        ps = _mk_ps(nc, es)
        sem = {n: es.enter_context(nc.semaphore(n)) for n in
               ["v", "s", "t", "g"] + ['ldw', 'ldh', 'ldx3', 'ldx2', 'stx']}
        p = Plan(sem)
        emit_c2(nc, p, sem, dram, ps, nq=nq)
        _replay(nc, p)
    return nc

def build_d(nq=NT):
    nc = bass.Bass("TRN2", target_bir_lowering=False)
    dram = {}
    TOKQ = nq * 128
    NB = 2 * nq
    TOKB = NB * 128
    xr = nc.dram_tensor("xr", [TOKB, D], F32, kind="ExternalInput").ap()
    dram["xr"] = xr
    qkvT = nc.dram_tensor("qkvT", [D, 3072], F32, kind="ExternalInput").ap()
    dram["qkvT"] = qkvT
    projT = nc.dram_tensor("projT", [D, D], F32, kind="ExternalInput").ap()
    dram["projT"] = projT
    rwT = nc.dram_tensor("rwT", [D, 8], F32, kind="ExternalInput").ap()
    dram["rwT"] = rwT
    n1r = nc.dram_tensor("n1r", [1, D], F32, kind="ExternalInput").ap()
    dram["n1r"] = n1r
    n2r = nc.dram_tensor("n2r", [1, D], F32, kind="ExternalInput").ap()
    dram["n2r"] = n2r
    wgT = nc.dram_tensor("wgT", [8, D, 4096], F32, kind="ExternalInput").ap()
    dram["wgT"] = wgT
    wuT = nc.dram_tensor("wuT", [8, D, 4096], F32, kind="ExternalInput").ap()
    dram["wuT"] = wuT
    wdT = nc.dram_tensor("wdT", [8, 4096, D], F32, kind="ExternalInput").ap()
    dram["wdT"] = wdT
    cosq = nc.dram_tensor("cosq", [128, TOKQ], F32, kind="ExternalInput").ap()
    dram["cosq"] = cosq
    sinq = nc.dram_tensor("sinq", [128, TOKQ], F32, kind="ExternalInput").ap()
    dram["sinq"] = sinq
    cosk = nc.dram_tensor("cosk", [128, TOKB], F32, kind="ExternalInput").ap()
    dram["cosk"] = cosk
    sink = nc.dram_tensor("sink", [128, TOKB], F32, kind="ExternalInput").ap()
    dram["sink"] = sink
    m128 = nc.dram_tensor("m128", [128, 128], F32, kind="ExternalInput").ap()
    dram["m128"] = m128
    cmask = nc.dram_tensor("cmask", [128, 128], F32, kind="ExternalInput").ap()
    dram["cmask"] = cmask
    flagd = nc.dram_tensor("flagd", [128, 1], F32, kind="ExternalInput").ap()
    dram["flagd"] = flagd
    eye = nc.dram_tensor("eye", [128, 128], BF16, kind="ExternalInput").ap()
    dram["eye"] = eye
    s4 = nc.dram_tensor("s4", [TOKQ, D], F32, kind="ExternalOutput").ap()
    dram["s4"] = s4
    kd = nc.dram_tensor("kd", [D, TOKB], BF16).ap()
    dram["kd"] = kd
    vd = nc.dram_tensor("vd", [TOKB, D], BF16).ap()
    dram["vd"] = vd
    qd = nc.dram_tensor("qd", [D, TOKQ], BF16).ap()
    dram["qd"] = qd
    ud = nc.dram_tensor("ud", [TOKQ, D], F32).ap()
    dram["ud"] = ud
    u2d = nc.dram_tensor("u2d", [TOKQ, D], F32).ap()
    dram["u2d"] = u2d
    y2d = nc.dram_tensor("y2d", [D, TOKQ], BF16).ap()
    dram["y2d"] = y2d
    hmd = nc.dram_tensor("hmd", [4096, TOKQ], BF16).ap()
    dram["hmd"] = hmd
    dram["xr1"] = dram["xr"]
    dram["xr2"] = dram["xr"][nq * 128:2 * nq * 128, :]
    es = ExitStack()
    with es:
        ps = _mk_ps(nc, es)
        sem = {n: es.enter_context(nc.semaphore(n)) for n in
               ["v", "s", "t", "g"] + ['ldx', 'ldw', 'ldwg', 'ldwu', 'ldwd', 'ldt', 'stk', 'stv', 'stq', 'stu', 'ldhp', 'ldq2', 'stu2', 'sty2', 'sts4i', 'ldu', 'ldxo', 'ldy2', 'sth', 'ldh', 'ldw3', 'acc']}
        p = Plan(sem)
        emit_d(nc, p, sem, dram, ps, nq=nq)
        _replay(nc, p)
    return nc

def build_fused(nq=NT):
    nc = bass.Bass("TRN2", target_bir_lowering=False)
    TOKQ = nq * 128
    NE = nq + 4
    DH = 2048
    CC = DH // 128
    dram = {}

    def ext(name, shape):
        dram[name] = nc.dram_tensor(name, shape, F32, kind="ExternalInput").ap()

    ext("xe", [(NE) * 128, D])
    for n in ("wq", "wk", "wv", "wo"):
        ext(n, [D, D])
    dram["mask"] = nc.dram_tensor("mask", [nq, 128, 640], F32, kind="ExternalInput").ap()
    dram["eye"] = nc.dram_tensor("eye", [128, 128], BF16, kind="ExternalInput").ap()
    ext("wr", [D, 8]); ext("br", [1, 8]); ext("lng", [1, D]); ext("lnb", [1, D])
    ext("w1", [8, D, 4096]); ext("b1pl", [128, 256]); ext("w2", [8, 4096, D]); ext("b2", [8, D])
    ext("w1T", [D, D]); ext("dlT", [D, DH]); ext("awT", [D, DH]); ext("bwT", [D, DH])
    ext("cwpl", [128, 24]); ext("cbpl", [128, 8]); ext("abpl", [128, CC])
    ext("bbpl", [128, CC]); ext("dbpl", [128, CC])
    ext("cwT", [DH, D]); ext("dwT", [D, D]); ext("w2T", [D, D]); ext("w3T", [D, D])
    ext("cbr", [1, D]); ext("dbr", [1, D])
    ext("qkvT", [D, 3072]); ext("projT", [D, D]); ext("rwT", [D, 8])
    ext("n1r", [1, D]); ext("n2r", [1, D])
    ext("wgT", [8, D, 4096]); ext("wuT", [8, D, 4096]); ext("wdT", [8, 4096, D])
    ext("cosq", [128, TOKQ]); ext("sinq", [128, TOKQ])
    ext("cosk", [128, 2 * TOKQ]); ext("sink", [128, 2 * TOKQ])
    ext("m128", [128, 128]); ext("cmask", [128, 128]); ext("flagd", [128, 1])
    ext("flhal", [2, 1]); ext("flcar", [128, 1]); ext("flcar2", [128, 1])

    x1d = nc.dram_tensor("x1d", [TOKQ, D], F32).ap()
    x2d = nc.dram_tensor("x2d", [TOKQ, D], F32).ap()
    x3sd = nc.dram_tensor("x3sd", [TOKQ, D], F32).ap()
    dram["xnd"] = nc.dram_tensor("xnd", [D, TOKQ], BF16).ap()
    dram["hd"] = nc.dram_tensor("hd", [DH, TOKQ], F32).ap()
    dram["x3d"] = nc.dram_tensor("x3d", [D, TOKQ], BF16).ap()
    dram["totd"] = nc.dram_tensor("totd", [128, 2 * CC], F32).ap()
    dram["carA"] = nc.dram_tensor("carAd", [128, CC], F32).ap()
    dram["carH"] = nc.dram_tensor("carHd", [128, CC], F32).ap()
    for n, shape, dt in (("kd", [D, 2 * TOKQ], BF16), ("vd", [2 * TOKQ, D], BF16),
                         ("qd", [D, TOKQ], BF16), ("ud", [TOKQ, D], F32),
                         ("u2d", [TOKQ, D], F32), ("y2d", [D, TOKQ], BF16),
                         ("hmd", [4096, TOKQ], BF16)):
        dram[n] = nc.dram_tensor(n, shape, dt).ap()
    cc1i = nc.dram_tensor("cc1i", [2, D], F32).ap()
    cc1o = nc.dram_tensor("cc1o", [4, D], F32).ap()
    cc2o = nc.dram_tensor("cc2o", [256, 2 * CC], F32).ap()
    cc3o = nc.dram_tensor("cc3o", [2 * TOKQ, D], F32).ap()


    dram["x1"] = x1d
    dram["x2"] = x1d
    dram["x3"] = x2d
    dram["xh"] = x2d
    dram["xhh"] = cc1o[1:3, :]
    dram["x2o"] = x2d
    dram["x3s"] = x3sd
    dram["xr1"] = cc3o
    dram["xr2"] = x3sd
    dram["s4"] = nc.dram_tensor("s4", [TOKQ, D], F32, kind="ExternalOutput").ap()

    RG = [[0, 1], [2, 3], [4, 5], [6, 7]]
    es = ExitStack()
    with es:
        ps = _mk_ps(nc, es)
        shared = {n: es.enter_context(nc.semaphore(n)) for n in
                  ["v", "s", "t", "g", "gl1", "gl2", "gl3", "gl4",
                   "cc1", "cc2", "cc3"]}
        allsem = dict(shared)
        P = Plan(shared)

        carbuf = es.enter_context(nc.sbuf_tensor("carbuf", [128, 2 * CC], F32))
        carout = es.enter_context(nc.sbuf_tensor("carout", [128, 2 * CC], F32))
        flA = es.enter_context(nc.sbuf_tensor("flAf", [128, 1], F32))
        flB = es.enter_context(nc.sbuf_tensor("flBf", [128, 1], F32))

        def g_init(g):
            g.dma_start(out=flA[:, :], in_=dram["flcar"][:, :]).then_inc(shared["gl1"], 16)
            g.dma_start(out=flB[:, :], in_=dram["flcar2"][:, :]).then_inc(shared["gl1"], 16)
        P.emit("gpsimd", g_init)
        gl_init = P.inc("gl1", 32)

        def join():
            waits = [(k, v) for k, v in P.counts.items() if v > 0 and k in allsem]
            for eng in Plan.ENGINES:
                def f(h, waits=tuple(waits)):
                    for n, val in waits:
                        h.wait_ge(allsem[n], val)
                P.emit(eng, f)

        def mksub(prefix, names):
            local = {n: shared[n] for n in ("v", "s", "t", "g")}
            for n in names:
                h = es.enter_context(nc.semaphore(prefix + n))
                local[n] = h
                allsem[prefix + n] = h
            return SubPlan(P, prefix, local)

        pa = mksub("a_", ["ldx", "ldm", "ldo", "ldwa", "ldwo", "stx"])
        emit_a(nc, pa, pa.sem, dram, ps, nq=nq)
        join()

        pb_ = mksub("b_", ["ldx", "sti", "stxn", "ldi", "ldw1", "ldw2", "ldxn",
                           "ldxn1", "acc", "th", "to"])
        emit_b(nc, pb_, pb_.sem, dram, ps, nt=nq, e_used=8)
        join()

        def f(sy):
            sy.dma_start(out=cc1i[0:1, :], in_=x2d[0:1, :]).then_inc(shared["gl2"], 16)
            sy.dma_start(out=cc1i[1:2, :], in_=x2d[TOKQ - 1:TOKQ, :]).then_inc(shared["gl2"], 16)
        P.emit("sync", f)
        gl2a = P.inc("gl2", 32)

        def f(g, gl2a=gl2a):
            g.wait_ge(shared["gl2"], gl2a)
            g.collective_compute("AllGather", ALU.bypass, replica_groups=RG,
                                 ins=[cc1i.opt()], outs=[cc1o.opt()]).then_inc(shared["cc1"])
            g.wait_ge(shared["cc1"], 1)
        P.emit("gpsimd", f)
        P.inc("cc1", 1)
        join()

        pc = mksub("c_", ["ldx", "ldw", "ldwc", "sth", "stx3", "sttot"])
        emit_c1(nc, pc, pc.sem, dram, ps, nq=nq)
        join()

        def f(g):
            g.collective_compute("AllGather", ALU.bypass, replica_groups=RG,
                                 ins=[dram["totd"].opt()], outs=[cc2o.opt()]).then_inc(shared["cc2"])
            g.wait_ge(shared["cc2"], 1)
        P.emit("gpsimd", f)
        P.inc("cc2", 1)

        def f(sy):
            sy.wait_ge(shared["cc2"], 1)
            sy.dma_start(out=carbuf[:, :], in_=cc2o[0:128, :]).then_inc(shared["gl3"], 16)
        P.emit("sync", f)
        gl3a = P.inc("gl3", 16)

        P.vop([("gl3", gl3a)],
              lambda v: v.tensor_scalar(carout[:, 0:CC], carbuf[:, 0:CC], flA[:, 0:1],
                                        flB[:, 0:1], ALU.mult, ALU.add))
        car_d = P.vop([], lambda v: v.tensor_scalar(carout[:, CC:2 * CC],
                                                    carbuf[:, CC:2 * CC], flA[:, 0:1],
                                                    None, ALU.mult))

        def f(sy, car_d=car_d):
            sy.wait_ge(shared["v"], car_d)
            sy.dma_start(out=dram["carA"][:, :], in_=carout[:, 0:CC]).then_inc(shared["gl4"], 16)
            sy.dma_start(out=dram["carH"][:, :], in_=carout[:, CC:2 * CC]).then_inc(shared["gl4"], 16)
        P.emit("sync", f)
        P.inc("gl4", 32)
        join()

        pe = mksub("e_", ["ldw", "ldh", "ldx3", "ldx2", "stx"])
        emit_c2(nc, pe, pe.sem, dram, ps, nq=nq)
        join()

        def f(g):
            g.collective_compute("AllGather", ALU.bypass, replica_groups=RG,
                                 ins=[x3sd.opt()], outs=[cc3o.opt()]).then_inc(shared["cc3"])
            g.wait_ge(shared["cc3"], 1)
        P.emit("gpsimd", f)
        P.inc("cc3", 1)
        join()

        pd = mksub("d_", ["ldx", "ldw", "ldwg", "ldwu", "ldwd", "ldt", "stk",
                          "stv", "stq", "stu", "ldhp", "ldq2", "stu2", "sty2",
                          "sts4i", "ldu", "ldxo", "ldy2", "sth", "ldh", "ldw3",
                          "acc"])
        emit_d(nc, pd, pd.sem, dram, ps, nq=nq)

        _replay(nc, P)
    return nc


def fused_in_maps(inputs, nq=NT):
    inp = {k: np.asarray(v) for k, v in inputs.items()}
    x0 = inp["L0_attn_hidden_states"].astype(np.float32)
    TOKQ = nq * 128
    S = 2 * TOKQ
    eye = _eye128()
    cm = np.where(np.arange(128)[None, :] <= np.arange(128)[:, None], 0.0,
                  -1e30).astype(np.float32)
    m128 = rot_matrix()
    maps = []
    for c in range(NCORES):
        b, h = c // 2, c % 2
        s0 = h * TOKQ
        lo, hi = s0 - 256, s0 + TOKQ + 256
        xe = np.zeros((TOKQ + 512, D), np.float32)
        cl, ch = max(lo, 0), min(hi, S)
        xe[cl - lo:ch - lo, :] = x0[b, cl:ch, :]
        qpos = np.arange(s0, s0 + TOKQ)
        kpos = np.concatenate([np.arange(0, TOKQ), np.arange(s0, s0 + TOKQ)])
        cq, sq = rope_tables(qpos, scale=0.125)
        ck, sk = rope_tables(kpos, scale=1.0)
        flag = (np.full((128, 1), -1e30, np.float32) if h == 0
                else np.zeros((128, 1), np.float32))
        flhal = np.array([[1.0 if h == 1 else 0.0],
                          [1.0 if h == 0 else 0.0]], np.float32)
        flcar = np.full((128, 1), 1.0 if h == 1 else 0.0, np.float32)
        maps.append({
            "xe": xe,
            "wq": inp["L0_attn_w_q"].astype(np.float32),
            "wk": inp["L0_attn_w_k"].astype(np.float32),
            "wv": inp["L0_attn_w_v"].astype(np.float32),
            "wo": inp["L0_attn_w_o"].astype(np.float32),
            "mask": a_masks(S=S, nq=nq, s0=s0),
            "eye": eye,
            "wr": inp["L0_ffn_Wr"].astype(np.float32),
            "br": inp["L0_ffn_br"].reshape(1, 8).astype(np.float32),
            "lng": inp["L0_ffn_ln_g"].reshape(1, D).astype(np.float32),
            "lnb": inp["L0_ffn_ln_b"].reshape(1, D).astype(np.float32),
            "w1": inp["L0_ffn_W1"].astype(np.float32),
            "b1pl": _b1pl(inp["L0_ffn_b1"]),
            "w2": inp["L0_ffn_W2"].astype(np.float32),
            "b2": inp["L0_ffn_b2"].astype(np.float32),
            "w1T": np.ascontiguousarray(inp["L1_attn_w1"].T).astype(np.float32),
            "dlT": np.ascontiguousarray(inp["L1_attn_dl_w"].T).astype(np.float32),
            "awT": np.ascontiguousarray(inp["L1_attn_A_w"].T).astype(np.float32),
            "bwT": np.ascontiguousarray(inp["L1_attn_B_w"].T).astype(np.float32),
            "cwpl": inp["L1_attn_conv_w"].reshape(1024, 3).reshape(8, 128, 3)
                    .transpose(1, 0, 2).reshape(128, 24).copy(),
            "cbpl": inp["L1_attn_conv_b"].reshape(8, 128).T.copy(),
            "abpl": inp["L1_attn_A_b"].reshape(16, 128).T.copy(),
            "bbpl": inp["L1_attn_B_b"].reshape(16, 128).T.copy(),
            "dbpl": inp["L1_attn_dl_b"].reshape(16, 128).T.copy(),
            "cwT": np.ascontiguousarray(inp["L1_attn_C_w"].T).astype(np.float32),
            "dwT": np.ascontiguousarray(inp["L1_attn_D_w"].T).astype(np.float32),
            "w2T": np.ascontiguousarray(inp["L1_attn_w2"].T).astype(np.float32),
            "w3T": np.ascontiguousarray(inp["L1_attn_w3"].T).astype(np.float32),
            "cbr": inp["L1_attn_C_b"].reshape(1, D).astype(np.float32),
            "dbr": inp["L1_attn_D_b"].reshape(1, D).astype(np.float32),
            "qkvT": np.ascontiguousarray(inp["L1_ffn_qkv_w"].T).astype(np.float32),
            "projT": np.ascontiguousarray(inp["L1_ffn_proj_w"].T).astype(np.float32),
            "rwT": np.ascontiguousarray(inp["L1_ffn_router_w"].T).astype(np.float32),
            "n1r": inp["L1_ffn_norm1_w"].reshape(1, D).astype(np.float32),
            "n2r": inp["L1_ffn_norm2_w"].reshape(1, D).astype(np.float32),
            "wgT": np.ascontiguousarray(inp["L1_ffn_wg"].transpose(0, 2, 1)).astype(np.float32),
            "wuT": np.ascontiguousarray(inp["L1_ffn_wu"].transpose(0, 2, 1)).astype(np.float32),
            "wdT": np.ascontiguousarray(inp["L1_ffn_wd"].transpose(0, 2, 1)).astype(np.float32),
            "cosq": cq, "sinq": sq, "cosk": ck, "sink": sk,
            "m128": m128, "cmask": cm, "flagd": flag,
            "flhal": flhal, "flcar": flcar, "flcar2": 1.0 - flcar,
        })
    return maps


def kernel_fused(**inputs):
    nc = _get_nc("fused", build_fused)
    maps = fused_in_maps(inputs)
    res = run_bass_kernel_spmd(nc, maps, core_ids=list(range(NCORES)))
    return _gather(res.results, "s4")



_EYE = None


def _eye128():
    global _EYE
    if _EYE is None:
        import ml_dtypes
        _EYE = np.eye(128, dtype=ml_dtypes.bfloat16)
    return _EYE


def _b1pl(b1):
    return np.ascontiguousarray(b1.reshape(8, 32, 128).transpose(2, 0, 1)
                                .reshape(128, 256)).astype(np.float32)


def b_in_maps(x2_full, inp):
    maps = []
    for c in range(NCORES):
        b, h = c // 2, c % 2
        rows = np.ascontiguousarray(x2_full[b, h * TOK:(h + 1) * TOK, :]).astype(np.float32)
        maps.append({
            "x2": rows,
            "wr": inp["L0_ffn_Wr"].astype(np.float32),
            "br": inp["L0_ffn_br"].reshape(1, 8).astype(np.float32),
            "lng": inp["L0_ffn_ln_g"].reshape(1, D).astype(np.float32),
            "lnb": inp["L0_ffn_ln_b"].reshape(1, D).astype(np.float32),
            "w1": inp["L0_ffn_W1"].astype(np.float32),
            "b1pl": _b1pl(inp["L0_ffn_b1"]),
            "w2": inp["L0_ffn_W2"].astype(np.float32),
            "b2": inp["L0_ffn_b2"].astype(np.float32),
            "eye": _eye128(),
        })
    return maps


def rope_tables(pos, scale=1.0):
    inv_freq = 1.0 / (10000.0 ** (np.arange(0, 64, 2, dtype=np.float64) / 64.0))
    freqs = np.outer(pos.astype(np.float64), inv_freq)
    emb = np.concatenate([freqs, freqs], axis=-1)
    cos = (np.cos(emb).T * scale).astype(np.float32)
    sin = (np.sin(emb).T * scale).astype(np.float32)
    return np.vstack([cos, cos]).copy(), np.vstack([sin, sin]).copy()


def rot_matrix():
    M = np.zeros((128, 128), np.float32)
    for b in (0, 64):
        for r in range(32):
            M[b + r, b + 32 + r] = -1.0
            M[b + 32 + r, b + r] = 1.0
    return np.ascontiguousarray(M.T)


def d_in_maps(x3s_full, inp, nq=NT):
    TOKQ = nq * 128
    eye = _eye128()
    cm = np.where(np.arange(128)[None, :] <= np.arange(128)[:, None], 0.0,
                  -1e30).astype(np.float32)
    m128 = rot_matrix()
    maps = []
    for c in range(NCORES):
        b, h = c // 2, c % 2
        if h == 0:
            xr = np.concatenate([x3s_full[b, TOKQ:2 * TOKQ], x3s_full[b, 0:TOKQ]], 0)
            flag = np.full((128, 1), -1e30, np.float32)
            kpos = np.concatenate([np.arange(TOKQ, 2 * TOKQ), np.arange(0, TOKQ)])
            qpos = np.arange(0, TOKQ)
        else:
            xr = x3s_full[b, 0:2 * TOKQ]
            flag = np.zeros((128, 1), np.float32)
            kpos = np.arange(0, 2 * TOKQ)
            qpos = np.arange(TOKQ, 2 * TOKQ)
        cq, sq = rope_tables(qpos, scale=0.125)
        ck, sk = rope_tables(kpos, scale=1.0)
        maps.append({
            "xr": np.ascontiguousarray(xr).astype(np.float32),
            "qkvT": np.ascontiguousarray(inp["L1_ffn_qkv_w"].T).astype(np.float32),
            "projT": np.ascontiguousarray(inp["L1_ffn_proj_w"].T).astype(np.float32),
            "rwT": np.ascontiguousarray(inp["L1_ffn_router_w"].T).astype(np.float32),
            "n1r": inp["L1_ffn_norm1_w"].reshape(1, D).astype(np.float32),
            "n2r": inp["L1_ffn_norm2_w"].reshape(1, D).astype(np.float32),
            "wgT": np.ascontiguousarray(inp["L1_ffn_wg"].transpose(0, 2, 1)).astype(np.float32),
            "wuT": np.ascontiguousarray(inp["L1_ffn_wu"].transpose(0, 2, 1)).astype(np.float32),
            "wdT": np.ascontiguousarray(inp["L1_ffn_wd"].transpose(0, 2, 1)).astype(np.float32),
            "cosq": cq, "sinq": sq, "cosk": ck, "sink": sk,
            "m128": m128, "cmask": cm, "flagd": flag, "eye": eye,
        })
    return maps


def c1_in_maps(x2s_full, inp):
    eye = _eye128()
    maps = []
    S = 4096
    for c in range(NCORES):
        b, h = c // 2, c % 2
        s0 = h * TOK
        xh = np.zeros((TOK + 128, D), np.float32)
        xh[0:TOK] = x2s_full[b, s0:s0 + TOK]
        if s0 - 1 >= 0:
            xh[TOK] = x2s_full[b, s0 - 1]
        if s0 + TOK < S:
            xh[TOK + 1] = x2s_full[b, s0 + TOK]
        maps.append({
            "xh": xh,
            "w1T": np.ascontiguousarray(inp["L1_attn_w1"].T).astype(np.float32),
            "dlT": np.ascontiguousarray(inp["L1_attn_dl_w"].T).astype(np.float32),
            "awT": np.ascontiguousarray(inp["L1_attn_A_w"].T).astype(np.float32),
            "bwT": np.ascontiguousarray(inp["L1_attn_B_w"].T).astype(np.float32),
            "cwpl": inp["L1_attn_conv_w"].reshape(1024, 3).reshape(8, 128, 3)
                    .transpose(1, 0, 2).reshape(128, 24).copy(),
            "cbpl": inp["L1_attn_conv_b"].reshape(8, 128).T.copy(),
            "abpl": inp["L1_attn_A_b"].reshape(16, 128).T.copy(),
            "bbpl": inp["L1_attn_B_b"].reshape(16, 128).T.copy(),
            "dbpl": inp["L1_attn_dl_b"].reshape(16, 128).T.copy(),
            "eye": eye,
            "flhal": np.ones((2, 1), np.float32),
        })
    return maps


def c2_in_maps(x2s_full, inp, c1_results):
    eye = _eye128()
    maps = []
    for c in range(NCORES):
        b, h = c // 2, c % 2
        s0 = h * TOK
        if h == 0:
            carA = np.ones((128, 16), np.float32)
            carH = np.zeros((128, 16), np.float32)
        else:
            tot = c1_results[c - 1]["totd"]
            carA = np.ascontiguousarray(tot[:, 0:16])
            carH = np.ascontiguousarray(tot[:, 16:32])
        maps.append({
            "hd": c1_results[c]["hd"], "x3d": c1_results[c]["x3d"],
            "carA": carA, "carH": carH,
            "x2o": np.ascontiguousarray(x2s_full[b, s0:s0 + TOK]).astype(np.float32),
            "cwT": np.ascontiguousarray(inp["L1_attn_C_w"].T).astype(np.float32),
            "dwT": np.ascontiguousarray(inp["L1_attn_D_w"].T).astype(np.float32),
            "w2T": np.ascontiguousarray(inp["L1_attn_w2"].T).astype(np.float32),
            "w3T": np.ascontiguousarray(inp["L1_attn_w3"].T).astype(np.float32),
            "cbr": inp["L1_attn_C_b"].reshape(1, D).astype(np.float32),
            "dbr": inp["L1_attn_D_b"].reshape(1, D).astype(np.float32),
            "eye": eye,
        })
    return maps


def _gather(results, key):
    out = np.empty((4, 4096, D), np.float32)
    for c in range(NCORES):
        b, h = c // 2, c % 2
        out[b, h * TOK:(h + 1) * TOK, :] = results[c][key]
    return out


_NC_CACHE = {}


def _get_nc(name, builder):
    if name not in _NC_CACHE:
        _NC_CACHE[name] = builder()
    return _NC_CACHE[name]


def kernel(**inputs):
    return kernel_fused(**inputs)


def kernel_unfused(**inputs):
    inp = {k: np.asarray(v) for k, v in inputs.items()}
    x0 = inp["L0_attn_hidden_states"].astype(np.float32)
    cores = list(range(NCORES))

    nc_a = _get_nc("a", build_a)
    res = run_bass_kernel_spmd(nc_a, a_in_maps(x0, inp), core_ids=cores)
    x1 = _gather(res.results, "x1")

    nc_b = _get_nc("b", build_b)
    res = run_bass_kernel_spmd(nc_b, b_in_maps(x1, inp), core_ids=cores)
    x2 = _gather(res.results, "x3")

    nc_c1 = _get_nc("c1", build_c1)
    res1 = run_bass_kernel_spmd(nc_c1, c1_in_maps(x2, inp), core_ids=cores)

    nc_c2 = _get_nc("c2", build_c2)
    res = run_bass_kernel_spmd(nc_c2, c2_in_maps(x2, inp, res1.results), core_ids=cores)
    x3 = _gather(res.results, "x3s")

    nc_d = _get_nc("d", build_d)
    res = run_bass_kernel_spmd(nc_d, d_in_maps(x3, inp), core_ids=cores)
    return _gather(res.results, "s4")


# version 06 (final)
# speedup vs baseline: 1.2010x; 1.0529x over previous
import numpy as np
from contextlib import ExitStack

import concourse.bass as bass
import concourse.mybir as mybir
from concourse.bass_utils import run_bass_kernel_spmd

F32 = mybir.dt.float32
BF16 = mybir.dt.bfloat16
AF = mybir.ActivationFunctionType
ALU = mybir.AluOpType
AX = mybir.AxisListType

NCORES = 8
D = 1024
TOK = 2048
NT = TOK // 128


class Plan:

    ENGINES = ("sync", "gpsimd", "vector", "scalar", "tensor")

    def __init__(self, sem):
        self.ops = {k: [] for k in self.ENGINES}
        self.counts = {}
        self.marks = {}
        self.sem = sem

    def emit(self, eng, fn):
        self.ops[eng].append(fn)

    def inc(self, s, n=1):
        c = self.counts.get(s, 0) + n
        self.counts[s] = c
        return c

    def cur(self, s):
        return self.counts.get(s, 0)

    def vop(self, waits, fn):
        n_prev = self.cur("v")
        sem = self.sem

        def closure(v, waits=tuple(waits), fn=fn, n_prev=n_prev):
            if n_prev:
                v.wait_ge(sem["v"], n_prev)
            for s_, val in waits:
                v.wait_ge(sem[s_], val)
            fn(v).then_inc(sem["v"], 1)
        self.emit("vector", closure)
        return self.inc("v", 1)

    def sop(self, waits, fn):
        n_prev = self.cur("s")
        sem = self.sem

        def closure(s, waits=tuple(waits), fn=fn, n_prev=n_prev):
            if n_prev:
                s.wait_ge(sem["s"], n_prev)
            for s_, val in waits:
                s.wait_ge(sem[s_], val)
            fn(s).then_inc(sem["s"], 1)
        self.emit("scalar", closure)
        return self.inc("s", 1)

    def replay(self, eng, handle):
        for fn in self.ops[eng]:
            fn(handle)



class SubPlan:

    def __init__(self, parent, prefix, localsem):
        self.parent = parent
        self.prefix = prefix
        self.sem = localsem
        self.marks = {}

    def _t(self, n):
        return n if n in ("v", "s", "t", "g") else self.prefix + n

    def emit(self, eng, fn):
        self.parent.emit(eng, fn)

    def inc(self, n, k=1):
        return self.parent.inc(self._t(n), k)

    def cur(self, n):
        return self.parent.cur(self._t(n))

    def vop(self, waits, fn):
        n_prev = self.cur("v")
        sem = self.sem

        def closure(v, waits=tuple(waits), fn=fn, n_prev=n_prev):
            if n_prev:
                v.wait_ge(sem["v"], n_prev)
            for s_, val in waits:
                v.wait_ge(sem[s_], val)
            fn(v).then_inc(sem["v"], 1)
        self.parent.emit("vector", closure)
        return self.inc("v", 1)

    def sop(self, waits, fn):
        n_prev = self.cur("s")
        sem = self.sem

        def closure(s, waits=tuple(waits), fn=fn, n_prev=n_prev):
            if n_prev:
                s.wait_ge(sem["s"], n_prev)
            for s_, val in waits:
                s.wait_ge(sem[s_], val)
            fn(s).then_inc(sem["s"], 1)
        self.parent.emit("scalar", closure)
        return self.inc("s", 1)


def _mk_ps(nc, es):
    return {"tb0": es.enter_context(nc.psum_tensor("tb0", [128, 1024], BF16)),
            "tb1": es.enter_context(nc.psum_tensor("tb1", [128, 1024], BF16)),
            **{f"pf{i}": es.enter_context(nc.psum_tensor(f"pf{i}", [128, 512], F32))
               for i in range(6)}}


def _replay(nc, p):
    with nc.Block() as block:
        @block.sync
        def _(sy):
            p.replay("sync", sy)

        @block.gpsimd
        def _(g):
            p.replay("gpsimd", g)

        @block.vector
        def _(v):
            p.replay("vector", v)

        @block.scalar
        def _(s):
            p.replay("scalar", s)

        @block.tensor
        def _(t):
            p.replay("tensor", t)


def _mk_subplan(parent, prefix, names, es, nc, shared):
    localsem = dict(shared)
    for n in names:
        localsem[n] = es.enter_context(nc.semaphore(prefix + n))
    return SubPlan(parent, prefix, localsem), localsem



def emit_b(nc, p, sem, dram, ps, nt=NT, e_used=8):

    TOKb = nt * 128
    x2 = dram["x2"]
    wr = dram["wr"]
    br = dram["br"]
    lng = dram["lng"]
    lnb = dram["lnb"]
    w1 = dram["w1"]
    b1pl = dram["b1pl"]
    w2 = dram["w2"]
    b2 = dram["b2"]
    eye = dram["eye"]
    x3 = dram["x3"]
    xnd = dram["xnd"]

    E, FC, ET = e_used, 32, TOKb // 512

    es = ExitStack()
    with es:
        w1_sb = es.enter_context(nc.sbuf_tensor("w1_sb", [128, 8 * 4096], BF16))
        w2_sb = es.enter_context(nc.sbuf_tensor("w2_sb", [128, 32 * 1024], BF16))
        aB = es.enter_context(nc.sbuf_tensor("aB", [128, 25600], BF16))
        h_sb = aB[:, 0:16384]
        ybf = aB[:, 0:1024]
        xnbf = aB[:, 1024:2048]
        yfm = aB[:, 2048:3072]
        xnfm = aB[:, 3072:4096]
        xn_sb = [aB[:, 16384:20480], aB[:, 20480:24576]]
        b2row = aB[0:1, 24576:25600]
        aF = es.enter_context(nc.sbuf_tensor("aF", [128, 5120], F32))
        xt = aF[:, 0:1024]
        yt = aF[:, 1024:2048]
        t2 = aF[:, 2048:3072]
        gb = aF[:, 3072:4096]
        bb = aF[:, 4096:5120]
        lngrow = aF[0:1, 2048:3072]
        lnbrow = aF[0:1, 3072:4096]
        otile = [aF[:, 0:1024], aF[:, 1024:2048]]

        small = es.enter_context(nc.sbuf_tensor("small", [128, 16], F32))
        sum1 = small[:, 0:1]
        mean = small[:, 1:2]
        ex2 = small[:, 2:3]
        m2 = small[:, 3:4]
        var = small[:, 4:5]
        lnv = small[:, 5:6]
        rstd = small[:, 6:7]
        sumsq = small[:, 7:8]
        negm = small[:, 8:9]
        m1p = small[:, 9:10]
        ssum = small[:, 10:11]
        gts = small[:, 11:12]
        epsv = small[:, 12:13]
        nhalf = small[:, 13:14]
        geff = es.enter_context(nc.sbuf_tensor("geff", [128, nt * 8], F32))
        lgr = es.enter_context(nc.sbuf_tensor("lgr", [128, 8], F32))
        oh = es.enter_context(nc.sbuf_tensor("oh", [128, 8], F32))
        lge = oh
        eyeb = es.enter_context(nc.sbuf_tensor("eyeb", [128, 128], BF16))
        b1pl_sb = es.enter_context(nc.sbuf_tensor("b1pl_sb", [128, 256], F32))
        wr_sb = es.enter_context(nc.sbuf_tensor("wr_sb", [128, 64], BF16))
        brrow = es.enter_context(nc.sbuf_tensor("brrow", [1, 8], BF16))
        onesb = es.enter_context(nc.sbuf_tensor("onesb", [1, 512], BF16))
        onesf = es.enter_context(nc.sbuf_tensor("onesf", [1, 128], F32))

        tps = ps["tb0"]
        rps = ps["pf5"]
        hp = [ps["pf0"], ps["pf1"]]
        op = [ps["pf2"], ps["pf3"], ps["pf4"], ps["pf5"]]


        def g_init(g):
            g.dma_start(out=eyeb[:, :], in_=eye[:, :]).then_inc(sem["ldi"], 16)
            g.dma_start(out=wr_sb[:, :].rearrange("p (c e) -> p c e", c=8),
                        in_=wr.rearrange("(c p) e -> p c e", p=128)).then_inc(sem["ldi"], 16)
            g.dma_start(out=brrow[:, :], in_=br[:, :]).then_inc(sem["ldi"], 16)
            g.memset(onesb[:, :], 1.0)
            g.memset(onesf[:, :], 1.0)
            g.memset(epsv, 1e-5)
            g.memset(nhalf, -0.5).then_inc(sem["g"], 1)
            g.dma_start(out=lngrow, in_=lng[:, :]).then_inc(sem["ldi"], 16)
            g.dma_start(out=lnbrow, in_=lnb[:, :]).then_inc(sem["ldi"], 16)
            g.dma_start(out=b1pl_sb[:, :], in_=b1pl[:, :]).then_inc(sem["ldi"], 16)
        p.emit("gpsimd", g_init)
        ldi_all = p.inc("ldi", 16 * 6)
        g_init_done = p.inc("g", 1)

        def t_bcast_fn(t):
            t.wait_ge(sem["ldi"], ldi_all)
            t.wait_ge(sem["g"], g_init_done)
            t.matmul(rps[:, 0:512], onesf[:, :], lngrow[:, 0:512], start=True, stop=True)
            t.matmul(hp[0][:, 0:512], onesf[:, :], lngrow[:, 512:1024], start=True, stop=True)
            t.matmul(hp[1][:, 0:512], onesf[:, :], lnbrow[:, 0:512], start=True, stop=True)
            t.matmul(op[0][:, 0:512], onesf[:, :], lnbrow[:, 512:1024],
                     start=True, stop=True).then_inc(sem["t"], 1)
        p.emit("tensor", t_bcast_fn)
        t_bcast = p.inc("t", 1)

        p.vop([("t", t_bcast)], lambda v: v.tensor_copy(gb[:, 0:512], rps[:, 0:512]))
        p.vop([], lambda v: v.tensor_copy(gb[:, 512:1024], hp[0][:, 0:512]))
        p.vop([], lambda v: v.tensor_copy(bb[:, 0:512], hp[1][:, 0:512]))
        v_bcast = p.vop([], lambda v: v.tensor_copy(bb[:, 512:1024], op[0][:, 0:512]))

        y_done = [0] * nt
        sq2_done = [0] * nt
        var2_done = [0] * nt
        ybf_done = [0] * nt
        xnbf_done = [0] * nt
        ytr_done = [0] * nt
        yfm_done = [0] * nt
        router_mm_done = [0] * nt
        router_done = [0] * nt
        xntr_done = [0] * nt
        xnfm_done = [0] * nt
        sti_done = [0] * nt
        stxn_done = [0] * nt
        ldx_done = [0] * nt

        for i in range(nt):
            r0 = i * 128

            def f(sy, i=i, r0=r0):
                if i > 0:
                    sy.wait_ge(sem["s"], sq2_done[i - 1])
                    sy.wait_ge(sem["v"], y_done[i - 1])
                    sy.wait_ge(sem["sti"], sti_done[i - 1])
                sy.dma_start(out=xt, in_=x2[r0:r0 + 128, :]).then_inc(sem["ldx"], 16)
            p.emit("sync", f)
            ldx_done[i] = p.inc("ldx", 16)

            def f(sy, r0=r0, i=i):
                sy.wait_ge(sem["ldx"], ldx_done[i])
                sy.dma_start(out=x3[r0:r0 + 128, :], in_=xt).then_inc(sem["sti"], 16)
            p.emit("sync", f)
            sti_done[i] = p.inc("sti", 16)

            w0 = [("ldx", ldx_done[i])] + ([("v", v_bcast)] if i == 0 else [])
            p.vop(w0, lambda v: v.reduce_sum(sum1, xt, axis=AX.X))
            p.vop([], lambda v: v.tensor_scalar_mul(mean, sum1, 1.0 / D))

            sw = [("ldx", ldx_done[i])] + ([("ldi", ldi_all), ("g", g_init_done)] if i == 0 else
                                           [("v", var2_done[i - 1])])
            sq1 = p.sop(sw, lambda s: s.activation(yt, xt, AF.Square, accum_out=sumsq))
            p.vop([("s", sq1)], lambda v: v.tensor_scalar_mul(ex2, sumsq, 1.0 / D))
            p.vop([], lambda v: v.tensor_mul(m2, mean, mean))
            var_d = p.vop([], lambda v: v.tensor_sub(var, ex2, m2))
            p.sop([("v", var_d)], lambda s: s.activation(lnv, var, AF.Ln, bias=epsv))
            rstd1 = p.sop([], lambda s: s.activation(rstd, lnv, AF.Exp, scale=nhalf))
            y_done[i] = p.vop([("s", rstd1)],
                              lambda v: v.tensor_scalar(yt, xt, mean, rstd,
                                                        ALU.subtract, ALU.mult))
            ybf_done[i] = p.vop([], lambda v: v.tensor_copy(ybf, yt))

            p.vop([], lambda v: v.reduce_sum(sum1, yt, axis=AX.X))
            p.vop([], lambda v: v.tensor_scalar_mul(mean, sum1, 1.0 / D))
            sq2_done[i] = p.sop([("v", y_done[i])],
                                lambda s: s.activation(t2, yt, AF.Square, accum_out=sumsq))
            p.vop([("s", sq2_done[i])], lambda v: v.tensor_scalar_mul(ex2, sumsq, 1.0 / D))
            p.vop([], lambda v: v.tensor_mul(m2, mean, mean))
            var2_done[i] = p.vop([], lambda v: v.tensor_sub(var, ex2, m2))
            p.sop([("v", var2_done[i])], lambda s: s.activation(lnv, var, AF.Ln, bias=epsv))
            rstd2 = p.sop([], lambda s: s.activation(rstd, lnv, AF.Exp, scale=nhalf))
            p.vop([("s", rstd2)],
                  lambda v: v.tensor_scalar(t2, yt, mean, rstd, ALU.subtract, ALU.mult))
            p.vop([], lambda v: v.tensor_mul(t2, t2, gb))
            xnbf_done[i] = p.vop([], lambda v: v.tensor_add(xnbf, t2, bb))

            def f(t, i=i):
                t.wait_ge(sem["v"], ybf_done[i])
                if i > 0:
                    t.wait_ge(sem["v"], xnfm_done[i - 1])
                for k in range(8):
                    ins = t.transpose(tps[:, k * 128:(k + 1) * 128],
                                      ybf[:, k * 128:(k + 1) * 128], eyeb[:, :])
                ins.then_inc(sem["t"], 1)
            p.emit("tensor", f)
            ytr_done[i] = p.inc("t", 1)

            yfm_done[i] = p.vop([("t", ytr_done[i])],
                                lambda v: v.tensor_copy(yfm, tps[:, :]))

            def f(t, i=i):
                t.wait_ge(sem["v"], yfm_done[i])
                if i > 0:
                    t.wait_ge(sem["v"], router_done[i - 1])
                t.matmul(rps[:, 0:8], onesb[:, 0:128], brrow[:, :], start=True, stop=False)
                for c in range(8):
                    ins = t.matmul(rps[:, 0:8], yfm[:, c * 128:(c + 1) * 128],
                                   wr_sb[:, c * 8:(c + 1) * 8],
                                   start=False, stop=(c == 7))
                ins.then_inc(sem["t"], 1)
            p.emit("tensor", f)
            router_mm_done[i] = p.inc("t", 1)

            p.vop([("t", router_mm_done[i])], lambda v: v.tensor_copy(lgr[:, :], rps[:, 0:8]))
            p.vop([], lambda v: v.reduce_max(negm, lgr[:, :], axis=AX.X, negate=True))
            negm_d = p.vop([], lambda v: v.reduce_max(m1p, lgr[:, :], axis=AX.X))
            exp_d = p.sop([("v", negm_d)],
                          lambda s: s.activation(lge[:, :], lgr[:, :], AF.Exp, bias=negm,
                                                 accum_out=ssum))
            p.vop([("s", exp_d)], lambda v: v.reciprocal(gts, ssum))
            p.vop([], lambda v: v.tensor_scalar(oh[:, :], lgr[:, :], m1p, None, ALU.is_equal))
            router_done[i] = p.vop(
                [], lambda v, i=i: v.tensor_scalar(geff[:, i * 8:(i + 1) * 8], oh[:, :],
                                                   gts, None, ALU.mult))

            def f(t, i=i):
                t.wait_ge(sem["v"], xnbf_done[i])
                t.wait_ge(sem["v"], yfm_done[i])
                for k in range(8):
                    ins = t.transpose(tps[:, k * 128:(k + 1) * 128],
                                      xnbf[:, k * 128:(k + 1) * 128], eyeb[:, :])
                ins.then_inc(sem["t"], 1)
            p.emit("tensor", f)
            xntr_done[i] = p.inc("t", 1)

            wv = [("t", xntr_done[i])] + ([("stxn", stxn_done[i - 1])] if i > 0 else [])
            xnfm_done[i] = p.vop(wv, lambda v: v.tensor_copy(xnfm, tps[:, :]))

            def f(sy, i=i):
                sy.wait_ge(sem["v"], xnfm_done[i])
                sy.dma_start(
                    out=xnd.rearrange("(c p) n -> p c n", p=128)[:, :, i * 128:(i + 1) * 128],
                    in_=xnfm.rearrange("p (c n) -> p c n", c=8)).then_inc(sem["stxn"], 16)
            p.emit("sync", f)
            stxn_done[i] = p.inc("stxn", 16)

        def emit_w1_load(e):
            def f(g, e=e):
                for c in range(8):
                    g.dma_start(out=w1_sb[:, c * 4096:(c + 1) * 4096],
                                in_=w1[e, c * 128:(c + 1) * 128, :]).then_inc(sem["ldw1"], 16)
            p.emit("gpsimd", f)
            return p.inc("ldw1", 16 * 8)

        def emit_w2_load(e):
            def f(g, e=e):
                for j4 in range(8):
                    g.dma_start(
                        out=w2_sb[:, j4 * 4096:(j4 + 1) * 4096].rearrange("p (gg d) -> p gg d", gg=4),
                        in_=w2[e, j4 * 512:(j4 + 1) * 512, :].rearrange("(gg p) d -> p gg d", p=128),
                    ).then_inc(sem["ldw2"], 16)
                g.dma_start(out=b2row, in_=b2[e:e + 1, :]).then_inc(sem["ldw2"], 16)
            p.emit("gpsimd", f)
            return p.inc("ldw2", 16 * 9)

        w1rdy = [0] * E
        w2rdy = [0] * E
        w1rdy[0] = emit_w1_load(0)
        w2rdy[0] = emit_w2_load(0)

        th_after = {}
        sh_after = {}
        to_after = {}
        vo_after = {}
        acc_after = {}

        for e in range(E):
            for t_ in range(ET):
                j = e * ET + t_

                lsem = "ldxn" if j % 2 == 0 else "ldxn1"

                def f(sy, e=e, t_=t_, j=j, lsem=lsem):
                    if j == 0:
                        sy.wait_ge(sem["stxn"], stxn_done[nt - 1])
                    elif j >= 2:
                        sy.wait_ge(sem["th"], th_after[(j - 2) * FC + FC - 1])
                    sy.dma_start(
                        out=xn_sb[j % 2].rearrange("p (c n) -> p c n", c=8),
                        in_=xnd.rearrange("(c p) n -> p c n", p=128)[:, :, t_ * 512:(t_ + 1) * 512],
                    ).then_inc(sem[lsem], 16)
                p.emit("sync", f)
                ldxn_j = (lsem, p.inc(lsem, 16))

                for fc in range(FC):
                    g_ = j * FC + fc

                    def f(t, e=e, fc=fc, g_=g_, ldxn_j=ldxn_j):
                        if fc == 0:
                            t.wait_ge(sem[ldxn_j[0]], ldxn_j[1])
                            t.wait_ge(sem["ldw1"], w1rdy[e])
                            if e == 0 and t_ == 0:
                                t.wait_ge(sem["v"], v_bcast)
                        if g_ >= 2:
                            t.wait_ge(sem["s"], sh_after[g_ - 2])
                        bank = hp[g_ % 2]
                        for c in range(8):
                            ins = t.matmul(bank[:, 0:512],
                                           w1_sb[:, c * 4096 + fc * 128: c * 4096 + (fc + 1) * 128],
                                           xn_sb[(g_ // FC) % 2][:, c * 512:(c + 1) * 512],
                                           start=(c == 0), stop=(c == 7))
                        ins.then_inc(sem["th"], 1)
                    p.emit("tensor", f)
                    th_after[g_] = p.inc("th", 1)

                    sh_after[g_] = p.sop(
                        [("th", th_after[g_])],
                        lambda s, e=e, fc=fc, g_=g_: s.activation(
                            h_sb[:, fc * 512:(fc + 1) * 512], hp[g_ % 2][:, 0:512],
                            AF.Relu, bias=b1pl_sb[:, e * 32 + fc:e * 32 + fc + 1]))

                for s4 in range(4):
                    for half in range(2):
                        og = (j * 4 + s4) * 2 + half

                        def f(t, e=e, s4=s4, half=half, og=og, j=j):
                            if s4 == 0 and half == 0:
                                t.wait_ge(sem["s"], sh_after[j * FC + FC - 1])
                                t.wait_ge(sem["ldw2"], w2rdy[e])
                            if og >= 4:
                                t.wait_ge(sem["v"], vo_after[og - 4])
                            bank = op[og % 4]
                            t.matmul(bank[:, 0:512], onesb[:, 0:128],
                                     b2row[:, half * 512:(half + 1) * 512],
                                     start=True, stop=False)
                            for fc in range(FC):
                                ins = t.matmul(
                                    bank[:, 0:512],
                                    h_sb[:, fc * 512 + s4 * 128: fc * 512 + (s4 + 1) * 128],
                                    w2_sb[:, fc * 1024 + half * 512: fc * 1024 + (half + 1) * 512],
                                    start=False, stop=(fc == FC - 1))
                            ins.then_inc(sem["to"], 1)
                        p.emit("tensor", f)
                        to_after[og] = p.inc("to", 1)

                        def mkev(e, t_, s4, half, og):
                            def fv(v):
                                jj = t_ * 4 + s4
                                return v.tensor_scalar(
                                    otile[s4 % 2][:, half * 512:(half + 1) * 512],
                                    op[og % 4][:, 0:512],
                                    geff[:, jj * 8 + e: jj * 8 + e + 1], None, ALU.mult)
                            return fv
                        ai = og // 2
                        wv = [("to", to_after[og])]
                        if og == 0:
                            wv.append(("sti", sti_done[nt - 1]))
                        if ai >= 2 and half == 0:
                            wv.append(("acc", 16 * (ai - 2 + 1)))
                        vo_v = p.vop(wv, mkev(e, t_, s4, half, og))
                        vo_after[og] = vo_v

                if t_ == ET - 1 and e + 1 < E:
                    def f(g, j=j):
                        g.wait_ge(sem["th"], th_after[j * FC + FC - 1])
                    p.emit("gpsimd", f)
                    w1rdy[e + 1] = emit_w1_load(e + 1)

                for s4 in range(4):
                    ai = j * 4 + s4

                    def f(g, t_=t_, s4=s4, ai=ai):
                        g.wait_ge(sem["v"], vo_after[ai * 2 + 1])
                        if ai == 0:
                            g.wait_ge(sem["sti"], sti_done[nt - 1])
                        else:
                            g.wait_ge(sem["acc"], 16 * ai)
                        r = t_ * 512 + s4 * 128
                        g.dma_start(out=x3[r:r + 128, :], in_=otile[s4 % 2][:, :],
                                    accum_op=ALU.add).then_inc(sem["acc"], 16)
                    p.emit("gpsimd", f)
                    acc_after[ai] = p.inc("acc", 16)

            if e + 1 < E:
                def f(g, e=e):
                    g.wait_ge(sem["to"], to_after[(e * ET + ET - 1) * 8 + 7])
                p.emit("gpsimd", f)
                w2rdy[e + 1] = emit_w2_load(e + 1)

        total_acc = p.cur("acc")

        def g_final(g):
            g.wait_ge(sem["acc"], total_acc)
            g.wait_ge(sem["sti"], sti_done[nt - 1])
        p.emit("gpsimd", g_final)




def emit_a(nc, p, sem, dram, ps, nq=NT):

    NE = nq + 4
    TOKQ = nq * 128
    TOKE = NE * 128
    NG = NE // 4
    W = 640

    xe_d = dram["xe"]
    wq_d = dram["wq"]
    wk_d = dram["wk"]
    wv_d = dram["wv"]
    wo_d = dram["wo"]
    mask_d = dram["mask"]
    eye = dram["eye"]
    x1 = dram["x1"]

    es = ExitStack()
    with es:
        nqc = nq * 128
        nec = NE * 128
        aB = es.enter_context(nc.sbuf_tensor("aBa", [128, 8 * nqc + 8 * nec + NE * 1024 +
                                                     24576 + 1024 + 4096], BF16))
        o = 0
        q_sb = aB[:, o:o + 8 * nqc]; o += 8 * nqc
        k_sb = aB[:, o:o + 8 * nec]; o += 8 * nec
        v_sb = aB[:, o:o + NE * 1024]; o += NE * 1024
        wreg = o
        wq_sb = aB[:, o:o + 8192]
        wk_sb = aB[:, o + 8192:o + 16384]
        wv_sb = aB[:, o + 16384:o + 24576]
        wo_sb = aB[:, o:o + 8192]
        probs = [aB[:, o + 8192:o + 8832], aB[:, o + 8832:o + 9472]]
        probsT = [aB[:, o + 9472:o + 10112], aB[:, o + 10112:o + 10752]]
        avt = aB[:, o + 10752:o + 11776]
        avfm = aB[:, o + 11776:o + 12800]
        o += 24576
        ybf = aB[:, o:o + 1024]; o += 1024
        yfmroll = aB[:, o:o + 4096]; o += 4096

        aF = es.enter_context(nc.sbuf_tensor("aFa", [128, 7424], F32))
        xet = aF[:, 0:1024]
        yt = aF[:, 1024:2048]
        score = [aF[:, 0:640], aF[:, 640:1280]]
        xo = [aF[:, 2048:3072], aF[:, 3072:4096]]
        outs = [aF[:, 4096:5120], aF[:, 5120:6144]]
        maskt = [aF[:, 6144:6784], aF[:, 6784:7424]]

        small = es.enter_context(nc.sbuf_tensor("smalla", [128, 32], F32))
        rr_all = small[:, 16:32]
        sum1 = small[:, 0:1]
        mean = small[:, 1:2]
        ex2 = small[:, 2:3]
        m2 = small[:, 3:4]
        var = small[:, 4:5]
        lnv = small[:, 5:6]
        rstd = small[:, 6:7]
        sumsq = small[:, 7:8]
        negm = small[:, 8:9]
        ssum = small[:, 9:10]
        rr = small[:, 10:11]
        epsv = small[:, 11:12]
        nhalf = small[:, 12:13]
        eyeb = es.enter_context(nc.sbuf_tensor("eyeba", [128, 128], BF16))

        qkp = [ps["pf0"], ps["pf1"]]
        vvp = [ps["pf2"], ps["pf3"]]
        tps = ps["tb0"]
        tps2 = ps["tb1"]
        tpsx = [tps, tps2]
        oav = [ps["pf4"], ps["pf5"]]


        def g_init(g):
            g.dma_start(out=eyeb[:, :], in_=eye[:, :]).then_inc(sem["ldwa"], 16)
            g.memset(epsv, 1e-5)
            g.memset(nhalf, -0.5).then_inc(sem["g"], 1)
            for c in range(8):
                g.dma_start(out=wq_sb[:, c * 1024:(c + 1) * 1024],
                            in_=wq_d[c * 128:(c + 1) * 128, :]).then_inc(sem["ldwa"], 16)
            for c in range(8):
                g.dma_start(out=wk_sb[:, c * 1024:(c + 1) * 1024],
                            in_=wk_d[c * 128:(c + 1) * 128, :]).then_inc(sem["ldwa"], 16)
            for c in range(8):
                g.dma_start(out=wv_sb[:, c * 1024:(c + 1) * 1024],
                            in_=wv_d[c * 128:(c + 1) * 128, :]).then_inc(sem["ldwa"], 16)
        p.emit("gpsimd", g_init)
        g_done = p.inc("g", 1)
        ldwa_all = p.inc("ldwa", 16 * 25)

        ldx_done = [0] * NE
        ytr_done = [0] * NE
        yfm_done = [0] * NE
        ln_done = [0] * NE
        sq_done = [0] * NE

        qproj_done = [0] * NG
        kproj_done = [0] * NG
        vproj_done = [0] * NG
        qev_done = [0] * NG
        kev_done = [0] * NG
        vev_done = [0] * NG

        for i in range(NE):
            gidx = i // 4

            def f(sy, i=i):
                if i > 0:
                    sy.wait_ge(sem["v"], ln_done[i - 1])
                    sy.wait_ge(sem["s"], sq_done[i - 1])
                sy.dma_start(out=xet, in_=xe_d[i * 128:(i + 1) * 128, :]).then_inc(sem["ldx"], 16)
            p.emit("sync", f)
            ldx_done[i] = p.inc("ldx", 16)

            w0 = [("ldx", ldx_done[i])] + ([("g", g_done)] if i == 0 else [])
            p.vop(w0, lambda v: v.reduce_sum(sum1, xet, axis=AX.X))
            p.vop([], lambda v: v.tensor_scalar_mul(mean, sum1, 1.0 / D))
            sq_done[i] = p.sop(w0, lambda s: s.activation(yt, xet, AF.Square, accum_out=sumsq))
            p.vop([("s", sq_done[i])], lambda v: v.tensor_scalar_mul(ex2, sumsq, 1.0 / D))
            p.vop([], lambda v: v.tensor_mul(m2, mean, mean))
            var_d = p.vop([], lambda v: v.tensor_sub(var, ex2, m2))
            p.sop([("v", var_d)], lambda s: s.activation(lnv, var, AF.Ln, bias=epsv))
            rstd_d = p.sop([], lambda s: s.activation(rstd, lnv, AF.Exp, scale=nhalf))
            p.vop([("s", rstd_d)],
                  lambda v: v.tensor_scalar(yt, xet, mean, rstd, ALU.subtract, ALU.mult))
            ln_done[i] = p.vop([], lambda v: v.tensor_copy(ybf, yt))

            def f(t, i=i):
                t.wait_ge(sem["v"], ln_done[i])
                if i == 0:
                    t.wait_ge(sem["ldwa"], ldwa_all)
                if i > 0:
                    t.wait_ge(sem["v"], yfm_done[i - 1])
                for k in range(8):
                    ins = t.transpose(tps[:, k * 128:(k + 1) * 128],
                                      ybf[:, k * 128:(k + 1) * 128], eyeb[:, :])
                ins.then_inc(sem["t"], 1)
            p.emit("tensor", f)
            ytr_done[i] = p.inc("t", 1)

            def fv(v, i=i):
                return v.tensor_copy(
                    yfmroll.rearrange("p (c n) -> p c n", c=8)[:, :, (i % 4) * 128:(i % 4 + 1) * 128],
                    tps[:, :].rearrange("p (c n) -> p c n", c=8))
            wv_ = [("t", ytr_done[i])]
            if i >= 4:
                wv_ += [("t", vproj_done[gidx - 1])]
            yfm_done[i] = p.vop(wv_, fv)

            if i % 4 != 3:
                continue

            g0 = gidx * 512
            qlo = max(g0, 256)
            qhi = min(g0 + 512, 256 + TOKQ)
            if qhi > qlo:
                for dc in range(8):
                    def f(t, gidx=gidx, dc=dc, qlo=qlo, qhi=qhi, g0=g0, i=i):
                        if dc == 0:
                            t.wait_ge(sem["v"], yfm_done[i])
                            t.wait_ge(sem["ldwa"], ldwa_all)
                            if gidx >= 1:
                                t.wait_ge(sem["v"], qev_done[gidx - 1])
                        bank = qkp[dc % 2]
                        if dc >= 2:
                            t.wait_ge(sem["v"], p.marks[("qev", gidx, dc - 2)])
                        for c in range(8):
                            ins = t.matmul(bank[:, 0:qhi - qlo],
                                           wq_sb[:, c * 1024 + dc * 128: c * 1024 + (dc + 1) * 128],
                                           yfmroll[:, c * 512 + qlo - g0: c * 512 + qhi - g0],
                                           start=(c == 0), stop=(c == 7))
                        ins.then_inc(sem["t"], 1)
                    p.emit("tensor", f)
                    p.marks[("qproj", gidx, dc)] = p.inc("t", 1)

                    def fv(v, gidx=gidx, dc=dc, qlo=qlo, qhi=qhi):
                        return v.tensor_copy(
                            q_sb[:, dc * nqc + qlo - 256: dc * nqc + qhi - 256],
                            qkp[dc % 2][:, 0:qhi - qlo])
                    p.marks[("qev", gidx, dc)] = p.vop([("t", p.marks[("qproj", gidx, dc)])], fv)
                qproj_done[gidx] = p.marks[("qproj", gidx, 7)]
                qev_done[gidx] = p.marks[("qev", gidx, 7)]
            else:
                qproj_done[gidx] = p.cur("t")
                qev_done[gidx] = p.cur("v")

            for dc in range(8):
                def f(t, gidx=gidx, dc=dc, g0=g0, i=i):
                    if dc == 0:
                        t.wait_ge(sem["v"], qev_done[gidx])
                    if dc >= 2:
                        t.wait_ge(sem["v"], p.marks[("kev", gidx, dc - 2)])
                    bank = qkp[dc % 2]
                    for c in range(8):
                        ins = t.matmul(bank[:, 0:512],
                                       wk_sb[:, c * 1024 + dc * 128: c * 1024 + (dc + 1) * 128],
                                       yfmroll[:, c * 512:(c + 1) * 512],
                                       start=(c == 0), stop=(c == 7))
                    ins.then_inc(sem["t"], 1)
                p.emit("tensor", f)
                p.marks[("kproj", gidx, dc)] = p.inc("t", 1)

                def fv(v, gidx=gidx, dc=dc, g0=g0):
                    return v.tensor_copy(k_sb[:, dc * nec + g0: dc * nec + g0 + 512],
                                         qkp[dc % 2][:, 0:512])
                p.marks[("kev", gidx, dc)] = p.vop([("t", p.marks[("kproj", gidx, dc)])], fv)
            kproj_done[gidx] = p.marks[("kproj", gidx, 7)]
            kev_done[gidx] = p.marks[("kev", gidx, 7)]

            for tt in range(4):
                ti = gidx * 4 + tt
                for half in range(2):
                    def f(t, gidx=gidx, tt=tt, half=half):
                        idx = tt * 2 + half
                        if idx >= 2:
                            t.wait_ge(sem["v"], p.marks[("vev", gidx, idx - 2)])
                        elif gidx >= 1:
                            t.wait_ge(sem["v"], p.marks[("vev", gidx - 1, 6 + idx)])
                        bank = vvp[idx % 2]
                        for c in range(8):
                            ins = t.matmul(bank[:, 0:512],
                                           yfmroll[:, c * 512 + tt * 128: c * 512 + (tt + 1) * 128],
                                           wv_sb[:, c * 1024 + half * 512: c * 1024 + (half + 1) * 512],
                                           start=(c == 0), stop=(c == 7))
                        ins.then_inc(sem["t"], 1)
                    p.emit("tensor", f)
                    p.marks[("vproj", gidx, tt * 2 + half)] = p.inc("t", 1)

                    def fv(v, ti=ti, half=half, gidx=gidx, tt=tt):
                        return v.tensor_copy(
                            v_sb[:, ti * 1024 + half * 512: ti * 1024 + (half + 1) * 512],
                            vvp[(tt * 2 + half) % 2][:, 0:512])
                    p.marks[("vev", gidx, tt * 2 + half)] = p.vop(
                        [("t", p.marks[("vproj", gidx, tt * 2 + half)])], fv)
            vproj_done[gidx] = p.marks[("vproj", gidx, 7)]
            vev_done[gidx] = p.marks[("vev", gidx, 7)]

        phase1_v = p.cur("v")
        phase1_t = p.cur("t")

        def f(g):
            g.wait_ge(sem["t"], phase1_t)
            for c in range(8):
                g.dma_start(out=wo_sb[:, c * 1024:(c + 1) * 1024],
                            in_=wo_d[c * 128:(c + 1) * 128, :]).then_inc(sem["ldwo"], 16)
        p.emit("gpsimd", f)
        ldwo_all = p.inc("ldwo", 16 * 8)

        ldm_done = [0] * nq
        ldo_done = [0] * nq
        stx_done = [0] * nq

        pend_back = [None]

        def flush_pend():
            if pend_back[0] is not None:
                pend_back[0]()
                pend_back[0] = None

        def make_back_a(qb, h, par):
            def emit_back():
                def f(t, qb=qb, h=h, par=par):
                    t.wait_ge(sem["s"], p.marks[("pexp", qb, h)])
                    if h >= 2:
                        t.wait_ge(sem["v"], p.marks[("ptev", qb, h - 2)])
                    elif qb >= 1:
                        if h == 0:
                            t.wait_ge(sem["v"], p.marks[("avfm", qb - 1)])
                        else:
                            t.wait_ge(sem["v"], p.marks[("ptev", qb - 1, 15)])
                    elif h == 0:
                        t.wait_ge(sem["v"], yfm_done[NE - 1])
                    for kc in range(5):
                        ins = t.transpose(tpsx[par][:, kc * 128:(kc + 1) * 128],
                                          probs[par][:, kc * 128:(kc + 1) * 128], eyeb[:, :])
                    ins.then_inc(sem["t"], 1)
                p.emit("tensor", f)
                p.marks[("ptr", qb, h)] = p.inc("t", 1)

                p.marks[("ptev", qb, h)] = p.vop(
                    [("t", p.marks[("ptr", qb, h)])],
                    lambda v, par=par: v.tensor_copy(probsT[par][:, 0:640],
                                                     tpsx[par][:, 0:640]))

                def f(t, qb=qb, h=h, par=par):
                    t.wait_ge(sem["v"], p.marks[("ptev", qb, h)])
                    if h == 0:
                        t.wait_ge(sem["s"], p.marks.get(("avev_s", qb - 1), 0))
                    for kc in range(5):
                        ins = t.matmul(oav[h // 8][:, (h % 8) * 64:(h % 8 + 1) * 64],
                                       probsT[par][:, kc * 128:(kc + 1) * 128],
                                       v_sb[:, (qb + kc) * 1024 + h * 64: (qb + kc) * 1024 + (h + 1) * 64],
                                       start=(h % 8 == 0 and kc == 0),
                                       stop=(h % 8 == 7 and kc == 4))
                    ins.then_inc(sem["t"], 1)
                p.emit("tensor", f)
                p.marks[("avmm", qb, h)] = p.inc("t", 1)
            return emit_back

        for qb in range(nq):
            def f(sy, qb=qb):
                if qb > 0:
                    sy.wait_ge(sem["ldm"], 16 * qb)
                if qb >= 2:
                    sy.wait_ge(sem["v"], p.marks[("mask_used", qb - 2)])
                sy.dma_start(out=maskt[qb % 2], in_=mask_d[qb, :, :]).then_inc(sem["ldm"], 16)
            p.emit("sync", f)
            ldm_done[qb] = p.inc("ldm", 16)

            def f(sy, qb=qb):
                if qb > 0:
                    sy.wait_ge(sem["ldo"], 16 * qb)
                if qb >= 2:
                    sy.wait_ge(sem["v"], p.marks[("res", qb - 2)])
                sy.dma_start(out=xo[qb % 2],
                             in_=xe_d[256 + qb * 128: 256 + (qb + 1) * 128, :]).then_inc(sem["ldo"], 16)
            p.emit("sync", f)
            ldo_done[qb] = p.inc("ldo", 16)

            for h in range(16):
                c, base = h // 2, 64 * (h % 2)
                par = h % 2
                def f(t, qb=qb, h=h, c=c, base=base, par=par):
                    if h == 0:
                        t.wait_ge(sem["v"], vev_done[NG - 1])
                        t.wait_ge(sem["v"], qev_done[NG - 1])
                        t.wait_ge(sem["v"], kev_done[NG - 1])
                    if h >= 2:
                        t.wait_ge(sem["v"], p.marks[("sev", qb, h - 2)])
                    elif qb >= 1:
                        t.wait_ge(sem["v"], p.marks[("res", qb - 1)])
                    lq = q_sb[base:base + 64, c * nqc + qb * 128: c * nqc + (qb + 1) * 128]
                    t.matmul(qkp[par][:, 0:512],
                             lq, k_sb[base:base + 64, c * nec + qb * 128: c * nec + qb * 128 + 512],
                             start=True, stop=True)
                    t.matmul(vvp[par][:, 0:128],
                             lq, k_sb[base:base + 64, c * nec + qb * 128 + 512: c * nec + qb * 128 + 640],
                             start=True, stop=True).then_inc(sem["t"], 1)
                p.emit("tensor", f)
                p.marks[("smm", qb, h)] = p.inc("t", 1)

                wv_ = [("t", p.marks[("smm", qb, h)]), ("ldm", ldm_done[qb])]
                p.vop(wv_, lambda v, qb=qb, par=par: v.scalar_tensor_tensor(
                    score[par][:, 0:512], qkp[par][:, 0:512], 0.125, maskt[qb % 2][:, 0:512],
                    ALU.mult, ALU.add))
                p.vop([], lambda v, qb=qb, par=par: v.scalar_tensor_tensor(
                    score[par][:, 512:640], vvp[par][:, 0:128], 0.125, maskt[qb % 2][:, 512:640],
                    ALU.mult, ALU.add))
                mk_used = p.vop([], lambda v, par=par: v.reduce_max(
                    negm, score[par][:, 0:640], axis=AX.X, negate=True))
                p.marks[("mask_used", qb)] = mk_used
                p.marks[("sev", qb, h)] = mk_used

                exp_d = p.sop([("v", mk_used)],
                              lambda s, par=par: s.activation(probs[par][:, 0:640],
                                                              score[par][:, 0:640], AF.Exp,
                                                              bias=negm, accum_out=ssum))
                p.marks[("pexp", qb, h)] = exp_d

                flush_pend()

                rr_d = p.vop([("s", exp_d)],
                             lambda v, h=h: v.reciprocal(rr_all[:, h:h + 1], ssum))

                pend_back[0] = make_back_a(qb, h, par)

                bank_final = (qb, 7 if h < 8 else 15)
                p.marks[("avev_pending", qb, h)] = (bank_final, rr_d)
            flush_pend()
            for h in range(16):
                bank_final, rr_d = p.marks[("avev_pending", qb, h)]
                p.marks[("avev", qb, h)] = p.sop(
                    [("t", p.marks[("avmm", bank_final[0], bank_final[1])]), ("v", rr_d)],
                    lambda s, h=h: s.activation(avt[:, h * 64:(h + 1) * 64],
                                                oav[h // 8][:, (h % 8) * 64:(h % 8 + 1) * 64],
                                                AF.Identity, scale=rr_all[:, h:h + 1]))
            p.marks[("avev_s", qb)] = p.marks[("avev", qb, 15)]

            def f(t, qb=qb):
                t.wait_ge(sem["s"], p.marks[("avev", qb, 15)])
                t.wait_ge(sem["v"], p.marks[("ptev", qb, 14)])
                for k in range(8):
                    ins = t.transpose(tps[:, k * 128:(k + 1) * 128],
                                      avt[:, k * 128:(k + 1) * 128], eyeb[:, :])
                ins.then_inc(sem["t"], 1)
            p.emit("tensor", f)
            avtr = p.inc("t", 1)

            avfm_d = p.vop([("t", avtr)], lambda v: v.tensor_copy(avfm, tps[:, :]))
            p.marks[("avfm", qb)] = avfm_d

            def f(t, qb=qb, avfm_d=avfm_d):
                t.wait_ge(sem["v"], avfm_d)
                t.wait_ge(sem["ldwo"], ldwo_all)
                t.wait_ge(sem["v"], p.marks[("sev", qb, 15)])
                for half in range(2):
                    for c in range(8):
                        ins = t.matmul(qkp[half][:, 0:512],
                                       avfm[:, c * 128:(c + 1) * 128],
                                       wo_sb[:, c * 1024 + half * 512: c * 1024 + (half + 1) * 512],
                                       start=(c == 0), stop=(c == 7))
                ins.then_inc(sem["t"], 1)
            p.emit("tensor", f)
            opj_d = p.inc("t", 1)

            wv_ = [("t", opj_d), ("ldo", ldo_done[qb])]
            if qb >= 2:
                wv_.append(("stx", 16 * (qb - 1)))
            p.vop(wv_, lambda v, qb=qb: v.tensor_add(outs[qb % 2][:, 0:512],
                                                     qkp[0][:, 0:512], xo[qb % 2][:, 0:512]))
            p.marks[("res", qb)] = p.vop(
                [], lambda v, qb=qb: v.tensor_add(outs[qb % 2][:, 512:1024],
                                                  qkp[1][:, 0:512], xo[qb % 2][:, 512:1024]))

            def f(sy, qb=qb):
                sy.wait_ge(sem["v"], p.marks[("res", qb)])
                if qb > 0:
                    sy.wait_ge(sem["stx"], 16 * qb)
                sy.dma_start(out=x1[qb * 128:(qb + 1) * 128, :],
                             in_=outs[qb % 2]).then_inc(sem["stx"], 16)
            p.emit("sync", f)
            stx_done[qb] = p.inc("stx", 16)

        def g_final(g):
            g.wait_ge(sem["stx"], 16 * nq)
        p.emit("gpsimd", g_final)



def a_masks(S=4096, nq=NT, s0=0):
    pos = np.arange(S)
    out = np.full((nq, 128, 640), -1e30, np.float32)
    for qb in range(nq):
        qs = s0 + qb * 128
        for i in range(128):
            a = qs + i
            ks = qs - 256
            j = np.arange(ks, ks + 640)
            valid = (j >= 0) & (j < S)
            jj = np.clip(j, 0, S - 1)
            seg = (a // 64) == (jj // 64)
            diff = np.abs(a - jj)
            dil = (diff > 0) & (diff % 4 == 0) & (diff <= 256)
            allowed = valid & (seg | dil)
            out[qb, i, allowed] = 0.0
    return out


def a_in_maps(x_full, inp):
    maps = []
    S = 4096
    for c in range(NCORES):
        b, h = c // 2, c % 2
        s0 = h * TOK
        lo, hi = s0 - 256, s0 + TOK + 256
        xe = np.zeros((TOK + 512, D), np.float32)
        cl, ch = max(lo, 0), min(hi, S)
        xe[cl - lo:ch - lo, :] = x_full[b, cl:ch, :]
        maps.append({
            "xe": xe,
            "wq": inp["L0_attn_w_q"].astype(np.float32),
            "wk": inp["L0_attn_w_k"].astype(np.float32),
            "wv": inp["L0_attn_w_v"].astype(np.float32),
            "wo": inp["L0_attn_w_o"].astype(np.float32),
            "mask": a_masks(S=S, nq=NT, s0=s0),
            "eye": _eye128(),
        })
    return maps



def emit_c1(nc, p, sem, dram, ps, nq=NT):
    xhh = dram["xhh"]
    flhal = dram["flhal"]
    TOKQ = nq * 128
    NTI = nq + 1
    TKC = (nq * 128) // 512
    NG = nq // 4
    DH = 2048
    CC = DH // 128

    xh = dram["xh"]
    w1T = dram["w1T"]
    dlT = dram["dlT"]
    awT = dram["awT"]
    bwT = dram["bwT"]
    cwpl = dram["cwpl"]
    cbpl = dram["cbpl"]
    abpl = dram["abpl"]
    bbpl = dram["bbpl"]
    dbpl = dram["dbpl"]
    eye = dram["eye"]
    hd = dram["hd"]
    x3d = dram["x3d"]
    totd = dram["totd"]

    es = ExitStack()
    with es:
        w1_sb = es.enter_context(nc.sbuf_tensor("w1sbc", [128, 8 * 1024], BF16))
        wst = [es.enter_context(nc.sbuf_tensor(f"wst{i}", [128, 2 * 3 * 1024], BF16))
               for i in range(1)]
        yroll = es.enter_context(nc.sbuf_tensor("yrollc", [128, 4096], BF16))
        ytile16 = es.enter_context(nc.sbuf_tensor("yt16", [128, 1024], BF16))
        x1f = es.enter_context(nc.sbuf_tensor("x1f", [128, 8 * (TOKQ + 2)], F32))
        x3f = es.enter_context(nc.sbuf_tensor("x3f", [128, 8 * TOKQ], BF16))
        cb0 = es.enter_context(nc.sbuf_tensor("cb0", [128, TOKQ], F32))
        cb1 = es.enter_context(nc.sbuf_tensor("cb1", [128, TOKQ], F32))
        cb2 = es.enter_context(nc.sbuf_tensor("cb2", [128, TOKQ], F32))
        cb3 = es.enter_context(nc.sbuf_tensor("cb3", [128, TOKQ], F32))
        cb4 = es.enter_context(nc.sbuf_tensor("cb4", [128, TOKQ], F32))
        zz = es.enter_context(nc.sbuf_tensor("zzc", [128, TOKQ], F32))
        tots = es.enter_context(nc.sbuf_tensor("tots", [128, 2 * CC], F32))
        ybf = es.enter_context(nc.sbuf_tensor("ybfc", [128, 1024], BF16))
        cwsb = es.enter_context(nc.sbuf_tensor("cwsb", [128, 24], F32))
        cbsb = es.enter_context(nc.sbuf_tensor("cbsb", [128, 8], F32))
        absb = es.enter_context(nc.sbuf_tensor("absb", [128, CC], F32))
        bbsb = es.enter_context(nc.sbuf_tensor("bbsb", [128, CC], F32))
        dbsb = es.enter_context(nc.sbuf_tensor("dbsb", [128, CC], F32))
        eyeb = es.enter_context(nc.sbuf_tensor("eyebc", [128, 128], BF16))
        fl2c = es.enter_context(nc.sbuf_tensor("fl2c", [2, 1], F32))
        aF = es.enter_context(nc.sbuf_tensor("aFc", [128, 2048], F32))
        xet = aF[:, 0:1024]
        yt = aF[:, 1024:2048]
        small = es.enter_context(nc.sbuf_tensor("smallc", [128, 16], F32))
        sum1 = small[:, 0:1]; mean = small[:, 1:2]; ex2 = small[:, 2:3]
        m2 = small[:, 3:4]; var = small[:, 4:5]; lnv = small[:, 5:6]
        rstd = small[:, 6:7]; sumsq = small[:, 7:8]
        epsv = small[:, 8:9]; nhalf = small[:, 9:10]

        tps = ps["tb0"]
        pp = [ps["pf0"], ps["pf1"], ps["pf2"], ps["pf3"]]


        def g_init(g):
            g.dma_start(out=eyeb[:, :], in_=eye[:, :]).then_inc(sem["ldw"], 16)
            for c in range(8):
                g.dma_start(out=w1_sb[:, c * 1024:(c + 1) * 1024],
                            in_=w1T[c * 128:(c + 1) * 128, :]).then_inc(sem["ldw"], 16)
            g.dma_start(out=cwsb[:, :], in_=cwpl[:, :]).then_inc(sem["ldw"], 16)
            g.dma_start(out=cbsb[:, :], in_=cbpl[:, :]).then_inc(sem["ldw"], 16)
            g.dma_start(out=absb[:, :], in_=abpl[:, :]).then_inc(sem["ldw"], 16)
            g.dma_start(out=bbsb[:, :], in_=bbpl[:, :]).then_inc(sem["ldw"], 16)
            g.dma_start(out=dbsb[:, :], in_=dbpl[:, :]).then_inc(sem["ldw"], 16)
            g.memset(zz[:, :], 0.0)
            g.memset(epsv, 1e-5)
            g.dma_start(out=fl2c[:, :], in_=flhal[:, :]).then_inc(sem["ldw"], 16)
            g.memset(nhalf, -0.5).then_inc(sem["g"], 1)
        p.emit("gpsimd", g_init)
        ldw_all = p.inc("ldw", 16 * 15)
        g_done = p.inc("g", 1)

        ln_done = [0] * NTI
        sq_done = [0] * NTI
        ytr_done = [0] * NTI
        yfm_done = [0] * NTI
        x1ev = {}

        for i in range(NTI):
            if i < nq:
                def f(sy, i=i):
                    if i > 0:
                        sy.wait_ge(sem["v"], ln_done[i - 1])
                        sy.wait_ge(sem["s"], sq_done[i - 1])
                    sy.dma_start(out=xet, in_=xh[i * 128:(i + 1) * 128, :]
                                 ).then_inc(sem["ldx"], 16)
                p.emit("sync", f)
                ldx_i = p.inc("ldx", 16)
            else:
                zero_d = p.vop([("v", ln_done[i - 1]), ("s", sq_done[i - 1]),
                                ("g", g_done)],
                               lambda v: v.tensor_scalar_mul(xet, xet, 0.0))

                def f(sy, i=i, zero_d=zero_d):
                    sy.wait_ge(sem["v"], zero_d)
                    sy.dma_start(out=xet[0:2, :], in_=xhh[0:2, :]).then_inc(sem["ldx"], 16)
                p.emit("sync", f)
                ldx_i = p.inc("ldx", 16)
                halm = p.vop([("ldx", ldx_i), ("ldw", ldw_all)],
                             lambda v: v.tensor_scalar(xet[0:2, :], xet[0:2, :],
                                                       fl2c[:, 0:1], None, ALU.mult))

            w0 = [("ldx", ldx_i)] + ([("g", g_done)] if i == 0 else [])
            if i == NTI - 1:
                w0.append(("v", halm))
            p.vop(w0, lambda v: v.reduce_sum(sum1, xet, axis=AX.X))
            p.vop([], lambda v: v.tensor_scalar_mul(mean, sum1, 1.0 / D))
            sq_done[i] = p.sop(w0, lambda s: s.activation(yt, xet, AF.Square, accum_out=sumsq))
            p.vop([("s", sq_done[i])], lambda v: v.tensor_scalar_mul(ex2, sumsq, 1.0 / D))
            p.vop([], lambda v: v.tensor_mul(m2, mean, mean))
            var_d = p.vop([], lambda v: v.tensor_sub(var, ex2, m2))
            p.sop([("v", var_d)], lambda s: s.activation(lnv, var, AF.Ln, bias=epsv))
            rstd_d = p.sop([], lambda s: s.activation(rstd, lnv, AF.Exp, scale=nhalf))
            p.vop([("s", rstd_d)],
                  lambda v: v.tensor_scalar(yt, xet, mean, rstd, ALU.subtract, ALU.mult))
            ln_done[i] = p.vop([], lambda v: v.tensor_copy(ybf[:, :], yt))

            def f(t, i=i):
                t.wait_ge(sem["v"], ln_done[i])
                if i == 0:
                    t.wait_ge(sem["ldw"], ldw_all)
                if i > 0:
                    t.wait_ge(sem["v"], yfm_done[i - 1])
                for k in range(8):
                    ins = t.transpose(tps[:, k * 128:(k + 1) * 128],
                                      ybf[:, k * 128:(k + 1) * 128], eyeb[:, :])
                ins.then_inc(sem["t"], 1)
            p.emit("tensor", f)
            ytr_done[i] = p.inc("t", 1)

            if i < nq:
                def fv(v, i=i):
                    return v.tensor_copy(
                        yroll.rearrange("p (c n) -> p c n", c=8)[:, :, (i % 4) * 128:(i % 4 + 1) * 128],
                        tps[:, :].rearrange("p (c n) -> p c n", c=8))
                wv_ = [("t", ytr_done[i])]
                if i >= 4:
                    wv_.append(("t", p.marks[("w1mm", i // 4 - 1, 7)]))
                yfm_done[i] = p.vop(wv_, fv)
            else:
                yfm_done[i] = p.vop([("t", ytr_done[i])],
                                    lambda v: v.tensor_copy(ytile16[:, :], tps[:, :]))

            if i % 4 == 3 and i < nq:
                gi = i // 4
                for dc in range(8):
                    def f(t, gi=gi, dc=dc, i=i):
                        if dc == 0:
                            t.wait_ge(sem["v"], yfm_done[i])
                        if dc >= 4:
                            t.wait_ge(sem["v"], x1ev[(gi, dc - 4)])
                        elif gi >= 1:
                            t.wait_ge(sem["v"], x1ev[(gi - 1, dc + 4)])
                        bank = pp[dc % 4]
                        for c in range(8):
                            ins = t.matmul(bank[:, 0:512],
                                           w1_sb[:, c * 1024 + dc * 128: c * 1024 + (dc + 1) * 128],
                                           yroll[:, c * 512:(c + 1) * 512],
                                           start=(c == 0), stop=(c == 7))
                        ins.then_inc(sem["t"], 1)
                    p.emit("tensor", f)
                    p.marks[("w1mm", gi, dc)] = p.inc("t", 1)

                    def fv(v, gi=gi, dc=dc):
                        return v.tensor_copy(
                            x1f[:, dc * (TOKQ + 2) + 1 + gi * 512: dc * (TOKQ + 2) + 1 + (gi + 1) * 512],
                            pp[dc % 4][:, 0:512])
                    x1ev[(gi, dc)] = p.vop([("t", p.marks[("w1mm", gi, dc)])], fv)

        halo_ev = {}
        for dc in range(8):
            def f(t, dc=dc):
                if dc == 0:
                    t.wait_ge(sem["v"], yfm_done[NTI - 1])
                    t.wait_ge(sem["v"], x1ev[(NG - 1, 7)])
                if dc >= 4:
                    t.wait_ge(sem["v"], halo_ev[dc - 4])
                bank = pp[dc % 4]
                for c in range(8):
                    ins = t.matmul(bank[:, 0:2],
                                   w1_sb[:, c * 1024 + dc * 128: c * 1024 + (dc + 1) * 128],
                                   ytile16[:, c * 128: c * 128 + 2],
                                   start=(c == 0), stop=(c == 7))
                ins.then_inc(sem["t"], 1)
            p.emit("tensor", f)
            hmm = p.inc("t", 1)
            p.vop([("t", hmm)],
                  lambda v, dc=dc: v.tensor_copy(x1f[:, dc * (TOKQ + 2): dc * (TOKQ + 2) + 1],
                                                 pp[dc % 4][:, 0:1]))
            halo_ev[dc] = p.vop([],
                                lambda v, dc=dc: v.tensor_copy(
                                    x1f[:, dc * (TOKQ + 2) + TOKQ + 1: dc * (TOKQ + 2) + TOKQ + 2],
                                    pp[dc % 4][:, 1:2]))
        x1_done = p.cur("v")

        x3dc_done = [0] * 8
        for dc in range(8):
            b0 = dc * (TOKQ + 2)
            wv_c = [("v", x1_done)] if dc == 0 else [("s", x3dc_done[dc - 1])]
            p.vop(wv_c,
                  lambda v, b0=b0, dc=dc: v.tensor_scalar(cb0[:, 0:TOKQ], x1f[:, b0:b0 + TOKQ],
                                                          cwsb[:, dc * 3: dc * 3 + 1], None, ALU.mult))
            p.vop([], lambda v, b0=b0, dc=dc: v.scalar_tensor_tensor(
                cb0[:, 0:TOKQ], x1f[:, b0 + 1:b0 + 1 + TOKQ], cwsb[:, dc * 3 + 1: dc * 3 + 2],
                cb0[:, 0:TOKQ], ALU.mult, ALU.add))
            p.vop([], lambda v, b0=b0, dc=dc: v.scalar_tensor_tensor(
                cb0[:, 0:TOKQ], x1f[:, b0 + 2:b0 + 2 + TOKQ], cwsb[:, dc * 3 + 2: dc * 3 + 3],
                cb0[:, 0:TOKQ], ALU.mult, ALU.add))
            cadd = p.vop([], lambda v, dc=dc: v.tensor_scalar(
                cb0[:, 0:TOKQ], cb0[:, 0:TOKQ], cbsb[:, dc:dc + 1], None, ALU.add))
            x3dc_done[dc] = p.sop([("v", cadd)],
                                  lambda s, dc=dc: s.activation(x3f[:, dc * TOKQ:(dc + 1) * TOKQ],
                                                                cb0[:, 0:TOKQ], AF.Silu))
            def f(sy, dc=dc):
                sy.wait_ge(sem["s"], x3dc_done[dc])
                if dc > 0:
                    sy.wait_ge(sem["stx3"], 16 * dc)
                sy.dma_start(out=x3d[dc * 128:(dc + 1) * 128, :],
                             in_=x3f[:, dc * TOKQ:(dc + 1) * TOKQ]).then_inc(sem["stx3"], 16)
            p.emit("sync", f)
            p.inc("stx3", 16)

        for cc in range(CC):
            buf = (cc % 2) * 3072

            def f(g, cc=cc, buf=buf):
                if cc >= 1:
                    g.wait_ge(sem["ldwc"], 48 * cc)
                if cc >= 2:
                    g.wait_ge(sem["t"], p.marks[("ccmm", cc - 2)])
                for ti, wt in enumerate((dlT, awT, bwT)):
                    g.dma_start(
                        out=wst[0][:, buf + ti * 1024: buf + (ti + 1) * 1024]
                        .rearrange("p (c n) -> p c n", c=8),
                        in_=wt.rearrange("(c p) h -> p c h", p=128)[:, :, cc * 128:(cc + 1) * 128],
                    ).then_inc(sem["ldwc"], 16)
            p.emit("gpsimd", f)
            ldwc_cc = p.inc("ldwc", 16 * 3)

            for ti in range(3):
                for tk in range(TKC):
                    def f(t, cc=cc, ti=ti, tk=tk, buf=buf, ldwc_cc=ldwc_cc):
                        if ti == 0 and tk == 0:
                            t.wait_ge(sem["ldwc"], ldwc_cc)
                            t.wait_ge(sem["s"], x3dc_done[7])
                            if cc >= 1:
                                t.wait_ge(sem["v"], p.marks[("ccev", cc - 1, 3 * TKC - 1)])
                        bank = pp[(ti * TKC + tk) % 4]
                        if ti * TKC + tk >= 4:
                            t.wait_ge(sem["v"], p.marks[("ccev", cc, ti * TKC + tk - 4)])
                        for c in range(8):
                            ins = t.matmul(bank[:, 0:512],
                                           wst[0][:, buf + ti * 1024 + c * 128: buf + ti * 1024 + (c + 1) * 128],
                                           x3f[:, c * TOKQ + tk * 512: c * TOKQ + (tk + 1) * 512],
                                           start=(c == 0), stop=(c == 7))
                        ins.then_inc(sem["t"], 1)
                    p.emit("tensor", f)
                    p.marks[("ccpm", cc, ti * TKC + tk)] = p.inc("t", 1)
                    if ti == 2 and tk == TKC - 1:
                        p.marks[("ccmm", cc)] = p.marks[("ccpm", cc, 3 * TKC - 1)]

                    tgt = (cb2, cb0, cb1)[ti]
                    bias = (dbsb, absb, bbsb)[ti]
                    p.marks[("ccev", cc, ti * TKC + tk)] = p.vop(
                        [("t", p.marks[("ccpm", cc, ti * TKC + tk)])],
                        lambda v, tgt=tgt, bias=bias, cc=cc, tk=tk, ti=ti: v.tensor_scalar(
                            tgt[:, tk * 512:(tk + 1) * 512], pp[(ti * TKC + tk) % 4][:, 0:512],
                            bias[:, cc:cc + 1], None, ALU.add))

            last_ev = p.marks[("ccev", cc, 3 * TKC - 1)]
            wv_sp = [("v", last_ev)] + ([("sth", 16 * cc)] if cc > 0 else [])
            p.sop(wv_sp, lambda s: s.activation(cb3[:, :], cb2[:, :], AF.Abs))
            sp1 = p.sop([],
                        lambda s: s.activation(cb4[:, :], cb3[:, :], AF.Exp, scale=-1.0))
            sp2 = p.sop([], lambda s: s.activation(cb3[:, :], cb4[:, :], AF.Ln, bias=1.0))
            p.vop([("s", sp2)],
                  lambda v: v.scalar_tensor_tensor(cb2[:, :], cb2[:, :], 0.0, cb3[:, :],
                                                   ALU.max, ALU.add))
            p.vop([], lambda v: v.tensor_mul(cb4[:, :], cb2[:, :], cb0[:, :]))
            ab_d = p.sop([("v", p.cur("v"))],
                         lambda s: s.activation(cb3[:, :], cb4[:, :], AF.Exp, scale=-1.0))
            ac_d = p.vop([("s", ab_d)],
                         lambda v: v.tensor_tensor_scan(cb4[:, :], cb3[:, :], zz[:, :],
                                                        1.0, ALU.mult, ALU.add))
            p.vop([], lambda v: v.tensor_mul(cb1[:, :], cb1[:, :], cb4[:, :]))
            h_d = p.vop([], lambda v: v.tensor_tensor_scan(cb3[:, :], cb1[:, :], zz[:, :],
                                                           0.0, ALU.add, ALU.add))
            p.vop([], lambda v, cc=cc: v.tensor_copy(tots[:, cc:cc + 1],
                                                     cb4[:, TOKQ - 1:TOKQ]))
            tot_d = p.vop([], lambda v, cc=cc: v.tensor_copy(tots[:, CC + cc: CC + cc + 1],
                                                             cb3[:, TOKQ - 1:TOKQ]))

            def f(sy, cc=cc, h_d=h_d):
                sy.wait_ge(sem["v"], h_d)
                if cc > 0:
                    sy.wait_ge(sem["sth"], 16 * cc)
                sy.dma_start(out=hd[cc * 128:(cc + 1) * 128, :],
                             in_=cb3[:, :]).then_inc(sem["sth"], 16)
            p.emit("sync", f)
            p.inc("sth", 16)
            p.marks[("hstored", cc)] = 16 * (cc + 1)
            p.marks[("cb3st", cc)] = 16 * (cc + 1)

        def f(sy):
            pass
        tot_final = p.cur("v")

        def f2(sy, tot_final=tot_final):
            sy.wait_ge(sem["v"], tot_final)
            sy.dma_start(out=totd[:, :], in_=tots[:, :]).then_inc(sem["sttot"], 16)
        p.emit("sync", f2)
        p.inc("sttot", 16)

        def g_final(g):
            g.wait_ge(sem["sth"], 16 * CC)
            g.wait_ge(sem["stx3"], 16 * 8)
            g.wait_ge(sem["sttot"], 16)
        p.emit("gpsimd", g_final)



def emit_c2(nc, p, sem, dram, ps, nq=NT):
    TOKQ = nq * 128
    DH = 2048
    CC = DH // 128

    hd = dram["hd"]
    x3d = dram["x3d"]
    carA = dram["carA"]
    carH = dram["carH"]
    x2o = dram["x2o"]
    cwT = dram["cwT"]
    dwT = dram["dwT"]
    w2T = dram["w2T"]
    w3T = dram["w3T"]
    cbr = dram["cbr"]
    dbr = dram["dbr"]
    eye = dram["eye"]
    x3s = dram["x3s"]

    es = ExitStack()
    with es:
        cwsb = es.enter_context(nc.sbuf_tensor("cwsb2", [128, CC * 1024], BF16))
        dwsb = es.enter_context(nc.sbuf_tensor("dwsb2", [128, 8 * 1024], BF16))
        w2sb = es.enter_context(nc.sbuf_tensor("w2sb2", [128, 8 * 1024], BF16))
        w3sb = es.enter_context(nc.sbuf_tensor("w3sb2", [128, 8 * 1024], BF16))
        htf = [es.enter_context(nc.sbuf_tensor(f"htf{i}", [128, CC * 128], F32)) for i in range(2)]
        hbt = [es.enter_context(nc.sbuf_tensor(f"hbt{i}", [128, CC * 128], BF16)) for i in range(2)]
        x3t = [es.enter_context(nc.sbuf_tensor(f"x3t{i}", [128, 8 * 128], BF16)) for i in range(2)]
        carAs = es.enter_context(nc.sbuf_tensor("carAs", [128, CC], F32))
        carHs = es.enter_context(nc.sbuf_tensor("carHs", [128, CC], F32))
        ssm_sb = [es.enter_context(nc.sbuf_tensor(f"ssm{i}", [128, 1024], F32)) for i in range(2)]
        ssmbf = es.enter_context(nc.sbuf_tensor("ssmbf", [128, 1024], BF16))
        ssmfm = es.enter_context(nc.sbuf_tensor("ssmfm", [128, 1024], BF16))
        r_sb = es.enter_context(nc.sbuf_tensor("r_sb", [128, 1024], F32))
        zbf = es.enter_context(nc.sbuf_tensor("zbf", [128, 1024], BF16))
        zfm = es.enter_context(nc.sbuf_tensor("zfm", [128, 1024], BF16))
        x2t = [es.enter_context(nc.sbuf_tensor(f"x2t{i}", [128, 1024], F32)) for i in range(2)]
        outs = [es.enter_context(nc.sbuf_tensor(f"outs2{i}", [128, 1024], F32)) for i in range(2)]
        cbrow = es.enter_context(nc.sbuf_tensor("cbrow", [1, 1024], BF16))
        dbrow = es.enter_context(nc.sbuf_tensor("dbrow", [1, 1024], BF16))
        onesb = es.enter_context(nc.sbuf_tensor("onesb2", [1, 128], BF16))
        eyeb = es.enter_context(nc.sbuf_tensor("eyeb2", [128, 128], BF16))

        tps = ps["tb0"]
        pp = [ps["pf0"], ps["pf1"], ps["pf2"], ps["pf3"]]


        def g_init(g):
            g.dma_start(out=eyeb[:, :], in_=eye[:, :]).then_inc(sem["ldw"], 16)
            for cc in range(CC):
                g.dma_start(out=cwsb[:, cc * 1024:(cc + 1) * 1024],
                            in_=cwT[cc * 128:(cc + 1) * 128, :]).then_inc(sem["ldw"], 16)
            for c in range(8):
                g.dma_start(out=dwsb[:, c * 1024:(c + 1) * 1024],
                            in_=dwT[c * 128:(c + 1) * 128, :]).then_inc(sem["ldw"], 16)
            for c in range(8):
                g.dma_start(out=w2sb[:, c * 1024:(c + 1) * 1024],
                            in_=w2T[c * 128:(c + 1) * 128, :]).then_inc(sem["ldw"], 16)
            for c in range(8):
                g.dma_start(out=w3sb[:, c * 1024:(c + 1) * 1024],
                            in_=w3T[c * 128:(c + 1) * 128, :]).then_inc(sem["ldw"], 16)
            g.dma_start(out=carAs[:, :], in_=carA[:, :]).then_inc(sem["ldw"], 16)
            g.dma_start(out=carHs[:, :], in_=carH[:, :]).then_inc(sem["ldw"], 16)
            g.dma_start(out=cbrow[:, :], in_=cbr[:, :]).then_inc(sem["ldw"], 16)
            g.dma_start(out=dbrow[:, :], in_=dbr[:, :]).then_inc(sem["ldw"], 16)
            g.memset(onesb[:, :], 1.0).then_inc(sem["g"], 1)
        p.emit("gpsimd", g_init)
        ldw_all = p.inc("ldw", 16 * (1 + CC + 24 + 4))
        g_done = p.inc("g", 1)

        fin_done = [0] * nq
        x3ld = [0] * nq
        ssm_done = [0] * nq
        res_done = [0] * nq

        for i in range(nq):
            def f(sy, i=i):
                if i > 0:
                    sy.wait_ge(sem["ldh"], 16 * i)
                if i >= 2:
                    sy.wait_ge(sem["v"], fin_done[i - 2])
                sy.dma_start(out=htf[i % 2][:, :].rearrange("p (cc n) -> p cc n", cc=CC),
                             in_=hd.rearrange("(cc p) n -> p cc n", p=128)[:, :, i * 128:(i + 1) * 128],
                             ).then_inc(sem["ldh"], 16)
            p.emit("sync", f)
            ldh_i = p.inc("ldh", 16)

            for cc in range(CC):
                wv_ = [("ldh", ldh_i)] if cc == 0 else []
                if cc == 0 and i == 0:
                    wv_.append(("ldw", ldw_all))
                if cc == 0 and i >= 2:
                    wv_.append(("t", p.marks[("ssmm", i - 2)]))
                fin_done[i] = p.vop(
                    wv_, lambda v, i=i, cc=cc: v.tensor_scalar(
                        hbt[i % 2][:, cc * 128:(cc + 1) * 128],
                        htf[i % 2][:, cc * 128:(cc + 1) * 128],
                        carAs[:, cc:cc + 1], carHs[:, cc:cc + 1], ALU.mult, ALU.add))

            def f(sy, i=i):
                if i > 0:
                    sy.wait_ge(sem["ldx3"], 16 * i)
                if i >= 2:
                    sy.wait_ge(sem["t"], p.marks[("ssmm", i - 2)])
                sy.dma_start(out=x3t[i % 2][:, :].rearrange("p (c n) -> p c n", c=8),
                             in_=x3d.rearrange("(c p) n -> p c n", p=128)[:, :, i * 128:(i + 1) * 128],
                             ).then_inc(sem["ldx3"], 16)
            p.emit("sync", f)
            x3ld[i] = p.inc("ldx3", 16)

            def f(sy, i=i):
                if i > 0:
                    sy.wait_ge(sem["ldx2"], 16 * i)
                if i >= 2:
                    sy.wait_ge(sem["v"], res_done[i - 2])
                sy.dma_start(out=x2t[i % 2][:, :],
                             in_=x2o[i * 128:(i + 1) * 128, :]).then_inc(sem["ldx2"], 16)
            p.emit("sync", f)
            ldx2_i = p.inc("ldx2", 16)

            def f(t, i=i):
                t.wait_ge(sem["v"], fin_done[i])
                t.wait_ge(sem["ldx3"], x3ld[i])
                if i == 0:
                    t.wait_ge(sem["g"], g_done)
                if i >= 1:
                    t.wait_ge(sem["v"], res_done[i - 1])
                for half in range(2):
                    bank = pp[(i % 2) * 2 + half]
                    t.matmul(bank[:, 0:512], onesb[:, :], cbrow[:, half * 512:(half + 1) * 512],
                             start=True, stop=False)
                    t.matmul(bank[:, 0:512], onesb[:, :], dbrow[:, half * 512:(half + 1) * 512],
                             start=False, stop=False)
                    for cc in range(CC):
                        t.matmul(bank[:, 0:512],
                                 hbt[i % 2][:, cc * 128:(cc + 1) * 128],
                                 cwsb[:, cc * 1024 + half * 512: cc * 1024 + (half + 1) * 512],
                                 start=False, stop=False)
                    for c in range(8):
                        ins = t.matmul(bank[:, 0:512],
                                       x3t[i % 2][:, c * 128:(c + 1) * 128],
                                       dwsb[:, c * 1024 + half * 512: c * 1024 + (half + 1) * 512],
                                       start=False, stop=(c == 7))
                ins.then_inc(sem["t"], 1)
            p.emit("tensor", f)
            p.marks[("ssmm", i)] = p.inc("t", 1)

            for half in range(2):
                p.marks[("ssmev", i, half)] = p.vop(
                    [("t", p.marks[("ssmm", i)])] if half == 0 else [],
                    lambda v, i=i, half=half: v.tensor_copy(
                        ssm_sb[i % 2][:, half * 512:(half + 1) * 512],
                        pp[(i % 2) * 2 + half][:, 0:512]))
            ssm_done[i] = p.marks[("ssmev", i, 1)]

            sbf_d = p.vop([], lambda v, i=i: v.tensor_copy(ssmbf[:, :], ssm_sb[i % 2][:, :]))

            def f(t, i=i, sbf_d=sbf_d):
                t.wait_ge(sem["v"], sbf_d)
                if i > 0:
                    t.wait_ge(sem["v"], p.marks[("zfm_ev", i - 1)])
                for k in range(8):
                    ins = t.transpose(tps[:, k * 128:(k + 1) * 128],
                                      ssmbf[:, k * 128:(k + 1) * 128], eyeb[:, :])
                ins.then_inc(sem["t"], 1)
            p.emit("tensor", f)
            str_d = p.inc("t", 1)

            sfm_d = p.vop([("t", str_d)], lambda v: v.tensor_copy(ssmfm[:, :], tps[:, :]))

            def f(t, i=i, sfm_d=sfm_d):
                t.wait_ge(sem["v"], sfm_d)
                if i >= 1:
                    t.wait_ge(sem["v"], ssm_done[i - 1])
                for half in range(2):
                    bank = pp[((i + 1) % 2) * 2 + half]
                    for c in range(8):
                        ins = t.matmul(bank[:, 0:512],
                                       ssmfm[:, c * 128:(c + 1) * 128],
                                       w2sb[:, c * 1024 + half * 512: c * 1024 + (half + 1) * 512],
                                       start=(c == 0), stop=(c == 7))
                ins.then_inc(sem["t"], 1)
            p.emit("tensor", f)
            w2mm = p.inc("t", 1)

            for half in range(2):
                p.marks[("rsilu", i, half)] = p.sop(
                    [("t", w2mm)] if half == 0 else [],
                    lambda s, i=i, half=half: s.activation(
                        r_sb[:, half * 512:(half + 1) * 512],
                        pp[((i + 1) % 2) * 2 + half][:, 0:512], AF.Silu))

            z_d = p.vop([("s", p.marks[("rsilu", i, 1)])],
                        lambda v, i=i: v.tensor_mul(zbf[:, :], ssm_sb[i % 2][:, :], r_sb[:, :]))

            def f(t, i=i, z_d=z_d, sfm_d=sfm_d):
                t.wait_ge(sem["v"], z_d)
                t.wait_ge(sem["v"], sfm_d)
                for k in range(8):
                    ins = t.transpose(tps[:, k * 128:(k + 1) * 128],
                                      zbf[:, k * 128:(k + 1) * 128], eyeb[:, :])
                ins.then_inc(sem["t"], 1)
            p.emit("tensor", f)
            p.marks[("sfm2", i)] = sfm_d
            ztr_d = p.inc("t", 1)

            p.marks[("zfm_ev", i)] = p.vop([("t", ztr_d)],
                                           lambda v: v.tensor_copy(zfm[:, :], tps[:, :]))

            def f(t, i=i):
                t.wait_ge(sem["v"], p.marks[("zfm_ev", i)])
                for half in range(2):
                    bank = pp[((i + 1) % 2) * 2 + half]
                    for c in range(8):
                        ins = t.matmul(bank[:, 0:512],
                                       zfm[:, c * 128:(c + 1) * 128],
                                       w3sb[:, c * 1024 + half * 512: c * 1024 + (half + 1) * 512],
                                       start=(c == 0), stop=(c == 7))
                ins.then_inc(sem["t"], 1)
            p.emit("tensor", f)
            w3mm = p.inc("t", 1)

            for half in range(2):
                wv_ = [("t", w3mm), ("ldx2", ldx2_i)] if half == 0 else []
                if half == 0 and i >= 2:
                    wv_.append(("stx", 16 * (i - 1)))
                p.vop(wv_, lambda v, i=i, half=half: v.tensor_add(
                    outs[i % 2][:, half * 512:(half + 1) * 512],
                    pp[((i + 1) % 2) * 2 + half][:, 0:512],
                    ssm_sb[i % 2][:, half * 512:(half + 1) * 512]))
                res_done[i] = p.vop([], lambda v, i=i, half=half: v.tensor_add(
                    outs[i % 2][:, half * 512:(half + 1) * 512],
                    outs[i % 2][:, half * 512:(half + 1) * 512],
                    x2t[i % 2][:, half * 512:(half + 1) * 512]))

            def f(sy, i=i):
                sy.wait_ge(sem["v"], res_done[i])
                if i > 0:
                    sy.wait_ge(sem["stx"], 16 * i)
                sy.dma_start(out=x3s[i * 128:(i + 1) * 128, :],
                             in_=outs[i % 2][:, :]).then_inc(sem["stx"], 16)
            p.emit("sync", f)
            p.inc("stx", 16)

        def g_final(g):
            g.wait_ge(sem["stx"], 16 * nq)
        p.emit("gpsimd", g_final)




def emit_d(nc, p, sem, dram, ps, nq=NT):
    TOKQ = nq * 128
    NB = 2 * nq
    TOKB = NB * 128
    NGB = NB // 4

    xr1 = dram["xr1"]
    xr2 = dram["xr2"]
    qkvT = dram["qkvT"]
    projT = dram["projT"]
    rwT = dram["rwT"]
    n1r = dram["n1r"]
    n2r = dram["n2r"]
    wgT = dram["wgT"]
    wuT = dram["wuT"]
    wdT = dram["wdT"]
    cosq = dram["cosq"]
    sinq = dram["sinq"]
    cosk = dram["cosk"]
    sink = dram["sink"]
    m128 = dram["m128"]
    cmask = dram["cmask"]
    flagd = dram["flagd"]
    eye = dram["eye"]
    s4 = dram["s4"]
    kd = dram["kd"]
    vd = dram["vd"]
    qd = dram["qd"]
    ud = dram["ud"]
    u2d = dram["u2d"]
    y2d = dram["y2d"]
    hmd = dram["hmd"]

    es = ExitStack()
    with es:
        NA = 32768 * 2 + 8192 + 8192
        aB = es.enter_context(nc.sbuf_tensor("aBd", [128, NA], BF16))
        RA = aB[:, 0:32768]
        RB = aB[:, 32768:65536]
        qkvw = aB[:, 0:24576]
        ckt = aB[:, 24576:24576 + TOKB]
        skt = aB[:, 24576 + TOKB:24576 + 2 * TOKB]
        cqt = aB[:, 24576 + 2 * TOKB:24576 + 2 * TOKB + TOKQ]
        sqt = aB[:, 24576 + 2 * TOKB + TOKQ:24576 + 2 * TOKB + 2 * TOKQ]
        o1 = 24576 + 2 * TOKB + 2 * TOKQ
        y1roll = aB[:, o1:o1 + 4096]
        ktmp = aB[:, o1 + 4096:o1 + 4608]
        y1bf = aB[:, o1 + 4608:o1 + 5632]
        kout = aB[:, o1 + 5632:o1 + 6144]
        khp = aB[:, 0:TOKB]
        qhp = aB[:, TOKB:TOKB + TOKQ]
        vhp = aB[:, TOKB + TOKQ:2 * TOKB + TOKQ]
        probs = aB[:, 2 * TOKB + TOKQ:3 * TOKB + TOKQ]
        probsT = aB[:, 3 * TOKB + TOKQ:4 * TOKB + TOKQ]
        avall = aB[:, 4 * TOKB + TOKQ:4 * TOKB + TOKQ + nq * 1024]
        o2 = 4 * TOKB + TOKQ + nq * 1024
        projw = aB[:, o2:o2 + 8192]
        rww = aB[:, o2 + 8192:o2 + 8256]
        avt = aB[:, o2 + 8256:o2 + 9280]
        avfm = aB[:, o2 + 9280:o2 + 10304]
        y2bf = aB[:, o2 + 10304:o2 + 11328]
        y2fm = aB[:, o2 + 11328:o2 + 12352]
        o3 = o2 + 12352
        khpP = [khp, aB[:, o3:o3 + TOKB]]
        qhpP = [qhp, aB[:, o3 + TOKB:o3 + TOKB + TOKQ]]
        vhpP = [vhp, aB[:, o3 + TOKB + TOKQ:o3 + 2 * TOKB + TOKQ]]
        probsP = [probs, aB[:, o3 + 2 * TOKB + TOKQ:o3 + 3 * TOKB + TOKQ]]
        h_sb = aB[:, 65536:65536 + 8192]
        y2t = [aB[:, 65536 + 8192 + i * 4096:65536 + 8192 + (i + 1) * 4096]
               for i in range(2)]
        hsub = [aB[:, 65536:65536 + 4096], aB[:, 65536 + 4096:65536 + 8192]]

        aF = es.enter_context(nc.sbuf_tensor("aFd", [128, 5120 + max(TOKB, 3072) + 128], F32))
        xet = aF[:, 0:1024]
        ut = aF[:, 1024:2048]
        scr = aF[:, 2048:3072]
        n1b = aF[:, 3072:4096]
        n2b = aF[:, 4096:5120]
        score = aF[:, 5120:5120 + TOKB]
        u2t = [aF[:, 5120 + 1024:5120 + 2048], aF[:, 5120 + 2048:5120 + 3072]]
        otile = [aF[:, 0:1024], aF[:, 1024:2048]]
        gs = aF[:, 2048:2560]
        cmt = aF[:, 5120 + max(TOKB, 3072):5120 + max(TOKB, 3072) + 128]
        scoreP = [score, aF[:, 2048:2048 + TOKB]]
        small = es.enter_context(nc.sbuf_tensor("smalld", [128, 28], F32))
        sum1 = small[:, 0:1]; mean = small[:, 1:2]; ex2 = small[:, 2:3]
        m2 = small[:, 3:4]; var = small[:, 4:5]; lnv = small[:, 5:6]
        rstd = small[:, 6:7]; sumsq = small[:, 7:8]
        negm = small[:, 8:9]; bias1 = small[:, 9:10]
        ssum1 = small[:, 10:11]; ssum2 = small[:, 11:12]; ssum = small[:, 12:13]
        rr2 = small[:, 13:15]
        epsv = small[:, 15:16]; nhalf = small[:, 16:17]; eps6 = small[:, 17:18]
        m1p = small[:, 18:19]; m2p = small[:, 19:20]; negm2 = small[:, 20:21]
        w1g = small[:, 21:22]; w2g = small[:, 22:23]
        rr4 = small[:, 24:28]
        flagt = es.enter_context(nc.sbuf_tensor("flagtd", [128, 1], F32))
        geff = es.enter_context(nc.sbuf_tensor("geffd", [128, NT * 8], F32))
        lgr = es.enter_context(nc.sbuf_tensor("lgrd", [128, 8], F32))
        lg2 = es.enter_context(nc.sbuf_tensor("lg2d", [128, 8], F32))
        lge = es.enter_context(nc.sbuf_tensor("lged", [128, 8], F32))
        oh1 = es.enter_context(nc.sbuf_tensor("oh1d", [128, 8], F32))
        oh2 = es.enter_context(nc.sbuf_tensor("oh2d", [128, 8], F32))
        eyeb = es.enter_context(nc.sbuf_tensor("eyebd", [128, 128], BF16))
        m128b = es.enter_context(nc.sbuf_tensor("m128b", [128, 128], BF16))
        onesb = es.enter_context(nc.sbuf_tensor("onesbd", [1, 128], BF16))
        onesf = es.enter_context(nc.sbuf_tensor("onesfd", [1, 128], F32))

        tpsD = ps["tb0"]
        tpsD2 = ps["tb1"]
        pb = [ps["pf0"], ps["pf1"], ps["pf2"], ps["pf3"], ps["pf4"], ps["pf5"], ps["pf5"]]


        def g_init(g):
            g.dma_start(out=eyeb[:, :], in_=eye[:, :]).then_inc(sem["ldw"], 16)
            g.dma_start(out=m128b[:, :], in_=m128[:, :]).then_inc(sem["ldw"], 16)
            for c in range(8):
                g.dma_start(out=qkvw[:, c * 3072:(c + 1) * 3072],
                            in_=qkvT[c * 128:(c + 1) * 128, :]).then_inc(sem["ldw"], 16)
            g.dma_start(out=ckt, in_=cosk[:, :]).then_inc(sem["ldw"], 16)
            g.dma_start(out=skt, in_=sink[:, :]).then_inc(sem["ldw"], 16)
            g.dma_start(out=cqt, in_=cosq[:, :]).then_inc(sem["ldw"], 16)
            g.dma_start(out=sqt, in_=sinq[:, :]).then_inc(sem["ldw"], 16)
            g.dma_start(out=n1r_row, in_=n1r[:, :]).then_inc(sem["ldw"], 16)
            g.dma_start(out=n2r_row, in_=n2r[:, :]).then_inc(sem["ldw"], 16)
            g.dma_start(out=cmt, in_=cmask[:, :]).then_inc(sem["ldw"], 16)
            g.dma_start(out=flagt[:, :], in_=flagd[:, :]).then_inc(sem["ldw"], 16)
            g.memset(onesb[:, :], 1.0)
            g.memset(onesf[:, :], 1.0)
            g.memset(epsv, 1e-5)
            g.memset(eps6, 1e-6)
            g.memset(nhalf, -0.5).then_inc(sem["g"], 1)
        n1r_row = aF[0:1, 2048:3072]
        n2r_row = aF[0:1, 4096:5120]
        p.emit("gpsimd", g_init)
        ldw_init = p.inc("ldw", 16 * 18)
        g_done = p.inc("g", 1)

        def t_bc(t):
            t.wait_ge(sem["ldw"], ldw_init)
            t.wait_ge(sem["g"], g_done)
            t.matmul(pb[0][:, 0:512], onesf[:, :], n1r_row[:, 0:512], start=True, stop=True)
            t.matmul(pb[1][:, 0:512], onesf[:, :], n1r_row[:, 512:1024], start=True, stop=True)
            t.matmul(pb[2][:, 0:512], onesf[:, :], n2r_row[:, 0:512], start=True, stop=True)
            t.matmul(pb[3][:, 0:512], onesf[:, :], n2r_row[:, 512:1024],
                     start=True, stop=True).then_inc(sem["t"], 1)
        p.emit("tensor", t_bc)
        t_bc_d = p.inc("t", 1)
        p.vop([("t", t_bc_d)], lambda v: v.tensor_copy(n1b[:, 0:512], pb[0][:, 0:512]))
        p.vop([], lambda v: v.tensor_copy(n1b[:, 512:1024], pb[1][:, 0:512]))
        p.vop([], lambda v: v.tensor_copy(n2b[:, 0:512], pb[2][:, 0:512]))
        bc_done = p.vop([], lambda v: v.tensor_copy(n2b[:, 512:1024], pb[3][:, 0:512]))

        ln_done = [0] * NB
        sq_done = [0] * NB
        yfm_done = [0] * NB

        for i in range(NB):
            src_ap = (xr1[i * 128:(i + 1) * 128, :] if i < nq
                      else xr2[(i - nq) * 128:(i - nq + 1) * 128, :])

            def f(sy, i=i, src_ap=src_ap):
                if i > 0:
                    sy.wait_ge(sem["v"], ln_done[i - 1])
                    sy.wait_ge(sem["s"], sq_done[i - 1])
                sy.dma_start(out=xet, in_=src_ap).then_inc(sem["ldx"], 16)
            p.emit("sync", f)
            ldx_i = p.inc("ldx", 16)

            w0 = [("ldx", ldx_i)] + ([("v", bc_done)] if i == 0 else [])
            p.vop(w0, lambda v: v.reduce_sum(sum1, xet, axis=AX.X))
            p.vop([], lambda v: v.tensor_scalar_mul(mean, sum1, 1.0 / D))
            sq1 = p.sop(w0, lambda s: s.activation(scr, xet, AF.Square, accum_out=sumsq))
            p.vop([("s", sq1)], lambda v: v.tensor_scalar_mul(ex2, sumsq, 1.0 / D))
            p.vop([], lambda v: v.tensor_mul(m2, mean, mean))
            var_d = p.vop([], lambda v: v.tensor_sub(var, ex2, m2))
            p.sop([("v", var_d)], lambda s: s.activation(lnv, var, AF.Ln, bias=epsv))
            rstd_d = p.sop([], lambda s: s.activation(rstd, lnv, AF.Exp, scale=nhalf))
            wv_u = [("s", rstd_d)]
            if i >= nq + 1:
                wv_u.append(("stu", 16 * (i - nq)))
            u_d = p.vop(wv_u,
                        lambda v: v.tensor_scalar(ut, xet, mean, rstd, ALU.subtract, ALU.mult))

            if i >= nq:
                def f(sy, i=i, u_d=u_d):
                    sy.wait_ge(sem["v"], u_d)
                    if i > nq:
                        sy.wait_ge(sem["stu"], 16 * (i - nq))
                    sy.dma_start(out=ud[(i - nq) * 128:(i - nq + 1) * 128, :],
                                 in_=ut).then_inc(sem["stu"], 16)
                p.emit("sync", f)
                p.inc("stu", 16)

            sq2 = p.sop([("v", u_d)], lambda s: s.activation(scr, ut, AF.Square, accum_out=sumsq))
            p.vop([("s", sq2)], lambda v: v.tensor_scalar_mul(ex2, sumsq, 1.0 / D))
            p.sop([("v", p.cur("v"))], lambda s: s.activation(lnv, ex2, AF.Ln, bias=eps6))
            rms_d = p.sop([], lambda s: s.activation(rstd, lnv, AF.Exp, scale=nhalf))
            p.vop([("s", rms_d)], lambda v: v.tensor_scalar(scr, ut, rstd, None, ALU.mult))
            ln_done[i] = p.vop([], lambda v: v.tensor_mul(y1bf, scr, n1b))

            def f(t, i=i):
                t.wait_ge(sem["v"], ln_done[i])
                if i > 0:
                    t.wait_ge(sem["v"], yfm_done[i - 1])
                for k in range(8):
                    ins = t.transpose(tpsD[:, k * 128:(k + 1) * 128],
                                      y1bf[:, k * 128:(k + 1) * 128], eyeb[:, :])
                ins.then_inc(sem["t"], 1)
            p.emit("tensor", f)
            ytr = p.inc("t", 1)

            def fv(v, i=i):
                return v.tensor_copy(
                    y1roll.rearrange("p (c n) -> p c n", c=8)[:, :, (i % 4) * 128:(i % 4 + 1) * 128],
                    tpsD[:, :].rearrange("p (c n) -> p c n", c=8))
            wv_ = [("t", ytr)]
            if i >= 4:
                wv_.append(("t", p.marks[("p1mm", i // 4 - 1)]))
            yfm_done[i] = p.vop(wv_, fv)

            if i % 4 != 3:
                continue
            gi = i // 4
            g0 = gi * 512

            for dc in range(8):
                def f(t, gi=gi, dc=dc, i=i):
                    if dc == 0:
                        t.wait_ge(sem["v"], yfm_done[i])
                        if gi == 0:
                            t.wait_ge(sem["v"], bc_done)
                        else:
                            t.wait_ge(sem["v"], p.marks[("pgv", gi - 1)])
                    else:
                        t.wait_ge(sem["v"], p.marks[("kev", gi, dc - 1)])
                    for c in range(8):
                        ins = t.matmul(pb[0][:, 0:512],
                                       qkvw[:, c * 3072 + 1024 + dc * 128: c * 3072 + 1024 + (dc + 1) * 128],
                                       y1roll[:, c * 512:(c + 1) * 512],
                                       start=(c == 0), stop=(c == 7))
                    ins.then_inc(sem["t"], 1)
                p.emit("tensor", f)
                kmm = p.inc("t", 1)

                kev = p.vop([("t", kmm)], lambda v: v.tensor_copy(ktmp, pb[0][:, 0:512]))
                p.marks[("kev", gi, dc)] = kev

                def f(t, kev=kev, gi=gi, dc=dc):
                    t.wait_ge(sem["v"], kev)
                    if gi > 0 or dc > 0:
                        t.wait_ge(sem["v"], p.marks.get(("kr2", gi, dc - 1),
                                                        p.marks.get(("kr2", gi - 1, 7), 0)))
                    for n0 in range(4):
                        ins = t.matmul(pb[1][:, n0 * 128:(n0 + 1) * 128], m128b[:, :],
                                       ktmp[:, n0 * 128:(n0 + 1) * 128],
                                       start=True, stop=True)
                    ins.then_inc(sem["t"], 1)
                p.emit("tensor", f)
                rot = p.inc("t", 1)

                p.vop([("t", rot)],
                      lambda v, g0=g0: v.tensor_mul(score[:, 0:512], pb[1][:, 0:512],
                                                    skt[:, g0:g0 + 512]))
                p.vop([], lambda v, g0=g0: v.tensor_mul(score[:, 512:1024], ktmp,
                                                        ckt[:, g0:g0 + 512]))
                wv_k = [("stk", 16 * (gi * 8 + dc))]
                if dc == 0:
                    wv_k.append(("stv", 16 * gi * 8))
                    if gi - 1 >= NGB // 2:
                        wv_k.append(("stq", 16 * 8 * (gi - NGB // 2)))
                kr2 = p.vop(wv_k, lambda v: v.tensor_add(kout, score[:, 0:512],
                                                         score[:, 512:1024]))
                p.marks[("kr2", gi, dc)] = kr2

                def f(sy, gi=gi, dc=dc, kr2=kr2):
                    sy.wait_ge(sem["v"], kr2)
                    sy.wait_ge(sem["stk"], 16 * (gi * 8 + dc))
                    sy.dma_start(out=kd[dc * 128:(dc + 1) * 128, gi * 512:(gi + 1) * 512],
                                 in_=kout).then_inc(sem["stk"], 16)
                p.emit("sync", f)
                p.inc("stk", 16)

            for tt in range(4):
                for half in range(2):
                    idx = tt * 2 + half

                    def f(t, gi=gi, tt=tt, half=half, idx=idx):
                        if idx >= 2:
                            t.wait_ge(sem["v"], p.marks[("vev2", gi, idx - 2)])
                        elif gi >= 1:
                            t.wait_ge(sem["v"], p.marks[("vev2", gi - 1, 6 + idx)])
                        bank = pb[2 + idx % 2]
                        for c in range(8):
                            ins = t.matmul(bank[:, 0:512],
                                           y1roll[:, c * 512 + tt * 128: c * 512 + (tt + 1) * 128],
                                           qkvw[:, c * 3072 + 2048 + half * 512: c * 3072 + 2048 + (half + 1) * 512],
                                           start=(c == 0), stop=(c == 7))
                        ins.then_inc(sem["t"], 1)
                    p.emit("tensor", f)
                    vmm = p.inc("t", 1)

                    wv_v = [("t", vmm)]
                    if idx == 0:
                        wv_v.append(("stk", 16 * (gi + 1) * 8))
                    else:
                        wv_v.append(("stv", 16 * (gi * 8 + idx)))
                    vev = p.vop(wv_v,
                                lambda v, half=half, idx=idx: v.tensor_copy(
                                    kout, pb[2 + idx % 2][:, 0:512]))
                    p.marks[("vev2", gi, idx)] = vev
                    if idx == 7:
                        p.marks[("vev", gi, 7)] = vev

                    def f(sy, gi=gi, tt=tt, half=half, vev=vev):
                        sy.wait_ge(sem["v"], vev)
                        sy.wait_ge(sem["stv"], 16 * (gi * 8 + tt * 2 + half))
                        sy.dma_start(out=vd[(gi * 4 + tt) * 128:(gi * 4 + tt + 1) * 128,
                                            half * 512:(half + 1) * 512],
                                     in_=kout).then_inc(sem["stv"], 16)
                    p.emit("sync", f)
                    p.inc("stv", 16)

            if i >= nq + 3:
                qgi = gi - NGB // 2
                q0 = qgi * 512
                for dc in range(8):
                    def f(t, dc=dc, qgi=qgi, i=i):
                        if dc == 0:
                            t.wait_ge(sem["v"], p.marks[("vev2", i // 4, 7)])
                        else:
                            t.wait_ge(sem["v"], p.marks[("qev", qgi, dc - 1)])
                        for c in range(8):
                            ins = t.matmul(pb[0][:, 0:512],
                                           qkvw[:, c * 3072 + dc * 128: c * 3072 + (dc + 1) * 128],
                                           y1roll[:, c * 512:(c + 1) * 512],
                                           start=(c == 0), stop=(c == 7))
                        ins.then_inc(sem["t"], 1)
                    p.emit("tensor", f)
                    qmm = p.inc("t", 1)

                    qev = p.vop([("t", qmm)], lambda v: v.tensor_copy(ktmp, pb[0][:, 0:512]))
                    p.marks[("qev", qgi, dc)] = qev

                    def f(t, qev=qev, qgi=qgi, dc=dc):
                        t.wait_ge(sem["v"], qev)
                        if dc > 0:
                            t.wait_ge(sem["v"], p.marks[("qr2", qgi, dc - 1)])
                        else:
                            t.wait_ge(sem["v"], p.marks[("kr2", qgi + NGB // 2, 7)])
                        for n0 in range(4):
                            ins = t.matmul(pb[1][:, n0 * 128:(n0 + 1) * 128], m128b[:, :],
                                           ktmp[:, n0 * 128:(n0 + 1) * 128],
                                           start=True, stop=True)
                        ins.then_inc(sem["t"], 1)
                    p.emit("tensor", f)
                    rot = p.inc("t", 1)

                    p.vop([("t", rot)],
                          lambda v, q0=q0: v.tensor_mul(score[:, 0:512], pb[1][:, 0:512],
                                                        sqt[:, q0:q0 + 512]))
                    p.vop([], lambda v, q0=q0: v.tensor_mul(score[:, 512:1024], ktmp,
                                                            cqt[:, q0:q0 + 512]))
                    wv_q = [("stq", 16 * (qgi * 8 + dc))]
                    if dc == 0:
                        wv_q.append(("stv", 16 * (qgi + NGB // 2 + 1) * 8))
                    qr2 = p.vop(wv_q, lambda v: v.tensor_add(kout, score[:, 0:512],
                                                             score[:, 512:1024]))
                    p.marks[("qr2", qgi, dc)] = qr2

                    def f(sy, qgi=qgi, dc=dc, qr2=qr2):
                        sy.wait_ge(sem["v"], qr2)
                        sy.wait_ge(sem["stq"], 16 * (qgi * 8 + dc))
                        sy.dma_start(out=qd[dc * 128:(dc + 1) * 128, qgi * 512:(qgi + 1) * 512],
                                     in_=kout).then_inc(sem["stq"], 16)
                    p.emit("sync", f)
                    p.inc("stq", 16)

            p.marks[("p1mm", gi)] = p.cur("t")
            p.marks[("pgv", gi)] = p.cur("v")

        p1_v = p.cur("v")
        p1_t = p.cur("t")
        p1_s = p.cur("s")
        nstk = p.cur("stk")
        nstv = p.cur("stv")
        nstq = p.cur("stq")
        nstu = p.cur("stu")

        def f(g):
            g.wait_ge(sem["t"], p1_t)
            for c in range(8):
                g.dma_start(out=projw[:, c * 1024:(c + 1) * 1024],
                            in_=projT[c * 128:(c + 1) * 128, :]).then_inc(sem["ldw3"], 16)
            g.dma_start(out=rww[:, :].rearrange("p (c e) -> p c e", c=8),
                        in_=rwT.rearrange("(c p) e -> p c e", p=128)).then_inc(sem["ldw3"], 16)
        p.emit("gpsimd", f)
        ldproj = p.inc("ldw3", 16 * 9)

        pend_back = [None]

        def flush_pend():
            if pend_back[0] is not None:
                pend_back[0]()
                pend_back[0] = None

        def make_back(hp, qb, h, e2, kl, nkt):
            def emit_back():
                par = h
                for tg in range((nkt + 7) // 8):
                    lo = tg * 8
                    hi = min(lo + 8, nkt)
                    bnk = tpsD if tg % 2 == 0 else tpsD2
                    prev_ptev = p.marks.get(("ptev_par", tg % 2), 0)

                    def f(t, e2=e2, lo=lo, hi=hi, bnk=bnk, par=par,
                          prev_ptev=prev_ptev):
                        t.wait_ge(sem["s"], e2)
                        if prev_ptev:
                            t.wait_ge(sem["v"], prev_ptev)
                        for kc in range(lo, hi):
                            ins = t.transpose(bnk[:, (kc - lo) * 128:(kc - lo + 1) * 128],
                                              probsP[par][:, kc * 128:(kc + 1) * 128],
                                              eyeb[:, :])
                        ins.then_inc(sem["t"], 1)
                    p.emit("tensor", f)
                    ptr = p.inc("t", 1)

                    ptev = p.vop([("t", ptr)],
                                 lambda v, lo=lo, hi=hi, bnk=bnk: v.tensor_copy(
                                     probsT[:, lo * 128:hi * 128],
                                     bnk[:, 0:(hi - lo) * 128]))
                    p.marks[("ptev_par", tg % 2)] = ptev
                lastptev = p.marks[("ptev_par", ((nkt + 7) // 8 - 1) % 2)]

                def f(t, hp=hp, qb=qb, h=h, nkt=nkt, lastptev=lastptev):
                    t.wait_ge(sem["v"], lastptev)
                    if h == 0 and (qb > 0 or hp > 0):
                        pk = p.marks.get(("avev_done", hp, qb - 1) if qb > 0
                                         else ("avev_done", hp - 1, nq - 1))
                        if pk:
                            t.wait_ge(sem["s"], pk)
                    for kc in range(nkt):
                        ins = t.matmul(pb[4][:, h * 64:(h + 1) * 64],
                                       probsT[:, kc * 128:(kc + 1) * 128],
                                       vhpP[hp % 2][:, kc * 128 + h * 64: kc * 128 + h * 64 + 64],
                                       start=(h == 0 and kc == 0),
                                       stop=(h == 1 and kc == nkt - 1))
                    ins.then_inc(sem["t"], 1)
                p.emit("tensor", f)
                p.marks[("avmm", hp, qb, h)] = p.inc("t", 1)

                if h == 1:
                    for hh in range(2):
                        p.marks[("avev", hp, qb, hh)] = p.sop(
                            [("t", p.marks[("avmm", hp, qb, 1)]),
                             ("v", p.marks[("rr_u", hp, qb, 1)])],
                            lambda s, hp=hp, qb=qb, hh=hh: s.activation(
                                avall[:, qb * 1024 + hp * 128 + hh * 64:
                                      qb * 1024 + hp * 128 + (hh + 1) * 64],
                                pb[4][:, hh * 64:(hh + 1) * 64], AF.Identity,
                                scale=rr4[:, (qb % 2) * 2 + hh:(qb % 2) * 2 + hh + 1]))
                    p.marks[("avev_done", hp, qb)] = p.marks[("avev", hp, qb, 1)]
            return emit_back

        for hp in range(8):
            flush_pend()
            lsem = "ldhp" if hp % 2 == 0 else "ldhp1"

            def f(sy, hp=hp, lsem=lsem):
                if hp < 2:
                    sy.wait_ge(sem["stk"], nstk)
                    sy.wait_ge(sem["stq"], nstq)
                    sy.wait_ge(sem["stv"], nstv)
                else:
                    sy.wait_ge(sem["t"], p.marks[("avmm", hp - 2, nq - 1, 1)])
                sy.dma_start(out=khpP[hp % 2],
                             in_=kd[hp * 128:(hp + 1) * 128, :]).then_inc(sem[lsem], 16)
                sy.dma_start(out=qhpP[hp % 2],
                             in_=qd[hp * 128:(hp + 1) * 128, :]).then_inc(sem[lsem], 16)
                sy.dma_start(out=vhpP[hp % 2].rearrange("p (tt n) -> p tt n", tt=NB),
                             in_=vd[:, hp * 128:(hp + 1) * 128]
                             .rearrange("(tt p) n -> p tt n", p=128)).then_inc(sem[lsem], 16)
            p.emit("sync", f)
            ldhp_d = p.inc(lsem, 48)

            for qb in range(nq):
                kl = TOKQ + (qb + 1) * 128
                nkc = (kl + 511) // 512
                nkt = kl // 128
                for h in range(2):
                    base = 64 * h
                    par = h
                    for c5 in range(nkc):
                        n0 = c5 * 512
                        n1 = min(n0 + 512, kl)

                        def f(t, hp=hp, qb=qb, h=h, c5=c5, n0=n0, n1=n1, base=base,
                              ldhp_d=ldhp_d, lsem=lsem):
                            if c5 == 0 and h == 0 and qb == 0:
                                t.wait_ge(sem[lsem], ldhp_d)
                                t.wait_ge(sem["ldw3"], ldproj)
                                if hp == 0:
                                    t.wait_ge(sem["v"], p1_v)
                            if c5 >= 2:
                                mk = p.marks[("scev", hp, qb, h, c5 - 2)]
                                t.wait_ge(sem[mk[0]], mk[1])
                            elif (h > 0 or qb > 0 or hp > 0) and c5 < 2:
                                pk = p.marks.get(("sclast", hp, qb, h))
                                if pk:
                                    for mk in pk:
                                        t.wait_ge(sem[mk[0]], mk[1])
                            t.matmul(pb[c5 % 2][:, 0:n1 - n0],
                                     qhpP[hp % 2][base:base + 64, qb * 128:(qb + 1) * 128],
                                     khpP[hp % 2][base:base + 64, n0:n1],
                                     start=True, stop=True).then_inc(sem["t"], 1)
                        p.emit("tensor", f)
                        smm = p.inc("t", 1)

                        if c5 % 2 == 0:
                            wv_e = [("t", smm)]
                            if c5 == 0:
                                pr = p.marks.get(("rmaxprev",))
                                if pr:
                                    wv_e.append(pr)
                            mk = ("s", p.sop(
                                wv_e,
                                lambda s, c5=c5, n0=n0, n1=n1, par=par: s.activation(
                                    scoreP[par][:, n0:n1], pb[c5 % 2][:, 0:n1 - n0],
                                    AF.Identity)))
                        else:
                            mk = ("v", p.vop(
                                [("t", smm)],
                                lambda v, c5=c5, n0=n0, n1=n1, par=par: v.tensor_copy(
                                    scoreP[par][:, n0:n1], pb[c5 % 2][:, 0:n1 - n0])))
                        p.marks[("scev", hp, qb, h, c5)] = mk
                    p.marks[("sclast", hp, qb, 1 - h) if h == 0 else
                            ("sclast", hp, qb + 1, 0) if qb + 1 < nq else
                            ("sclast", hp + 1, 0, 0)] = (
                        p.marks[("scev", hp, qb, h, nkc - 1)],
                        p.marks[("scev", hp, qb, h, nkc - 2)])

                    evw = [p.marks[("scev", hp, qb, h, c5)] for c5 in range(nkc)]
                    evw = [m for m in evw if m[0] == "s"][-1:] + [
                        m for m in evw if m[0] == "v"][-1:]
                    dm = p.vop(evw, lambda v, kl=kl, par=par: v.tensor_add(
                        scoreP[par][:, kl - 128:kl], scoreP[par][:, kl - 128:kl], cmt))
                    rmax_d = p.vop([], lambda v, kl=kl, par=par: v.reduce_max(
                        negm, scoreP[par][:, 0:kl], axis=AX.X, negate=True))
                    p.marks[("rmaxprev",)] = ("v", rmax_d)
                    ng_d = p.vop([], lambda v: v.tensor_add(bias1, negm, flagt[:, :]))
                    e1 = p.sop([("v", ng_d)],
                               lambda s, par=par: s.activation(
                                   probsP[par][:, 0:TOKQ], scoreP[par][:, 0:TOKQ],
                                   AF.Exp, bias=bias1, accum_out=ssum1))
                    e2 = p.sop([], lambda s, kl=kl, par=par: s.activation(
                        probsP[par][:, TOKQ:kl], scoreP[par][:, TOKQ:kl], AF.Exp, bias=negm,
                        accum_out=ssum2))

                    flush_pend()

                    sm_d = p.vop([("s", e2)], lambda v: v.tensor_add(ssum, ssum1, ssum2))
                    rr_d = p.vop([], lambda v, qb=qb, h=h: v.reciprocal(
                        rr4[:, (qb % 2) * 2 + h:(qb % 2) * 2 + h + 1], ssum))
                    p.marks[("rr_u", hp, qb, h)] = rr_d

                    pend_back[0] = make_back(hp, qb, h, e2, kl, nkt)
        flush_pend()

        for qb in range(nq):
            def f(sy, qb=qb):
                if qb == 0:
                    sy.wait_ge(sem["stu"], nstu)
                else:
                    sy.wait_ge(sem["ldu"], 16 * qb)
                if qb >= 2:
                    sy.wait_ge(sem["v"], p.marks[("u2_d", qb - 2)])
                sy.dma_start(out=ut, in_=ud[qb * 128:(qb + 1) * 128, :]).then_inc(sem["ldu"], 16)
            p.emit("sync", f)
            ldu_d = p.inc("ldu", 16)

            def f(sy, qb=qb):
                if qb == 0:
                    sy.wait_ge(sem["v"], p1_v)
                    sy.wait_ge(sem["s"], p1_s)
                else:
                    sy.wait_ge(sem["ldxo"], 16 * qb)
                if qb >= 2:
                    sy.wait_ge(sem["v"], p.marks[("s4i_d", qb - 2)])
                sy.dma_start(out=xet, in_=xr2[qb * 128:(qb + 1) * 128, :]
                             ).then_inc(sem["ldxo"], 16)
            p.emit("sync", f)
            ldxo_d = p.inc("ldxo", 16)

            ptev_final = max(p.marks.get(("ptev_par", 0), 0),
                             p.marks.get(("ptev_par", 1), 0))

            def f(t, qb=qb, ptev_final=ptev_final):
                t.wait_ge(sem["s"], p.marks[("avev_done", 7, qb)])
                if qb > 0:
                    t.wait_ge(sem["v"], p.marks[("y2fm_d", qb - 1)])
                else:
                    t.wait_ge(sem["v"], ptev_final)
                for k in range(8):
                    ins = t.transpose(tpsD[:, k * 128:(k + 1) * 128],
                                      avall[:, qb * 1024 + k * 128: qb * 1024 + (k + 1) * 128],
                                      eyeb[:, :])
                ins.then_inc(sem["t"], 1)
            p.emit("tensor", f)
            avtr = p.inc("t", 1)
            p.marks[("avfm_d", qb)] = p.vop([("t", avtr)],
                                            lambda v: v.tensor_copy(avfm, tpsD[:, :]))

            def f(t, qb=qb):
                t.wait_ge(sem["v"], p.marks[("avfm_d", qb)])
                if qb > 0:
                    t.wait_ge(sem["v"], p.marks[("u2h_d", qb - 1)])
                for half in range(2):
                    bank = pb[2 + half]
                    for c in range(8):
                        ins = t.matmul(bank[:, 0:512],
                                       avfm[:, c * 128:(c + 1) * 128],
                                       projw[:, c * 1024 + half * 512: c * 1024 + (half + 1) * 512],
                                       start=(c == 0), stop=(c == 7))
                ins.then_inc(sem["t"], 1)
            p.emit("tensor", f)
            pjmm = p.inc("t", 1)

            for half in range(2):
                wv_ = [("t", pjmm), ("ldu", ldu_d)] if half == 0 else []
                if half == 0 and qb >= 2:
                    wv_.append(("s", p.marks[("sq2u", qb - 2)]))
                    wv_.append(("stu2", 16 * (qb - 1)))
                p.marks[("u2h_d", qb)] = p.vop(
                    wv_, lambda v, qb=qb, half=half: v.tensor_add(
                        u2t[qb % 2][:, half * 512:(half + 1) * 512],
                        pb[2 + half][:, 0:512],
                        ut[:, half * 512:(half + 1) * 512]))
            p.marks[("u2_d", qb)] = p.marks[("u2h_d", qb)]
            wv_ = [("ldxo", ldxo_d)]
            if qb > 0:
                wv_.append(("sts4i", 16 * qb))
            s4i = p.vop(wv_, lambda v, qb=qb: v.tensor_add(scr, u2t[qb % 2], xet))
            p.marks[("s4i_d", qb)] = s4i

            def f(sy, qb=qb, s4i=s4i):
                sy.wait_ge(sem["v"], s4i)
                if qb > 0:
                    sy.wait_ge(sem["sts4i"], 16 * qb)
                sy.dma_start(out=s4[qb * 128:(qb + 1) * 128, :], in_=scr).then_inc(sem["sts4i"], 16)
            p.emit("sync", f)
            p.inc("sts4i", 16)

            def f(sy, qb=qb):
                sy.wait_ge(sem["v"], p.marks[("u2_d", qb)])
                if qb > 0:
                    sy.wait_ge(sem["stu2"], 16 * qb)
                sy.dma_start(out=u2d[qb * 128:(qb + 1) * 128, :],
                             in_=u2t[qb % 2][:, :]).then_inc(sem["stu2"], 16)
            p.emit("sync", f)
            p.inc("stu2", 16)

            sq2 = p.sop([("v", p.marks[("u2_d", qb)])],
                        lambda s, qb=qb: s.activation(score[:, 0:1024], u2t[qb % 2],
                                                      AF.Square, accum_out=sumsq))
            p.marks[("sq2u", qb)] = sq2
            p.vop([("s", sq2)], lambda v: v.tensor_scalar_mul(ex2, sumsq, 1.0 / D))
            p.sop([("v", p.cur("v"))], lambda s: s.activation(lnv, ex2, AF.Ln, bias=eps6))
            rms_d = p.sop([], lambda s: s.activation(rstd, lnv, AF.Exp, scale=nhalf))
            p.vop([("s", rms_d)],
                  lambda v, qb=qb: v.tensor_scalar(score[:, 0:1024], u2t[qb % 2], rstd,
                                                   None, ALU.mult))
            y2_d = p.vop([], lambda v: v.tensor_mul(y2bf, score[:, 0:1024], n2b))

            def f(t, qb=qb, y2_d=y2_d):
                t.wait_ge(sem["v"], y2_d)
                t.wait_ge(sem["v"], p.marks[("avfm_d", qb)])
                for k in range(8):
                    ins = t.transpose(tpsD[:, k * 128:(k + 1) * 128],
                                      y2bf[:, k * 128:(k + 1) * 128], eyeb[:, :])
                ins.then_inc(sem["t"], 1)
            p.emit("tensor", f)
            y2tr = p.inc("t", 1)

            y2fm_d = p.vop([("t", y2tr)] + ([("sty2", 16 * qb)] if qb > 0 else []),
                           lambda v: v.tensor_copy(y2fm, tpsD[:, :]))
            p.marks[("y2fm_d", qb)] = y2fm_d

            def f(sy, qb=qb, y2fm_d=y2fm_d):
                sy.wait_ge(sem["v"], y2fm_d)
                sy.dma_start(out=y2d.rearrange("(c p) n -> p c n", p=128)[:, :, qb * 128:(qb + 1) * 128],
                             in_=y2fm.rearrange("p (c n) -> p c n", c=8)).then_inc(sem["sty2"], 16)
            p.emit("sync", f)
            p.inc("sty2", 16)

            def f(t, qb=qb, y2fm_d=y2fm_d):
                t.wait_ge(sem["v"], y2fm_d)
                if qb > 0:
                    t.wait_ge(sem["v"], p.marks[("geff_d", qb - 1)])
                for c in range(8):
                    ins = t.matmul(pb[6][:, 0:8], y2fm[:, c * 128:(c + 1) * 128],
                                   rww[:, c * 8:(c + 1) * 8],
                                   start=(c == 0), stop=(c == 7))
                ins.then_inc(sem["t"], 1)
            p.emit("tensor", f)
            rmm = p.inc("t", 1)

            p.vop([("t", rmm)], lambda v: v.tensor_copy(lgr[:, :], pb[6][:, 0:8]))
            p.vop([], lambda v: v.reduce_max(m1p, lgr[:, :], axis=AX.X))
            p.vop([], lambda v: v.tensor_scalar(oh1[:, :], lgr[:, :], m1p, None, ALU.is_equal))
            p.vop([], lambda v: v.scalar_tensor_tensor(lg2[:, :], oh1[:, :], -1e30,
                                                       lgr[:, :], ALU.mult, ALU.add))
            p.vop([], lambda v: v.reduce_max(m2p, lg2[:, :], axis=AX.X))
            oh2_d = p.vop([], lambda v: v.tensor_scalar(oh2[:, :], lg2[:, :], m2p, None,
                                                        ALU.is_equal))
            ng2 = p.vop([], lambda v: v.tensor_scalar_mul(negm2, m2p, -1.0))
            w1_d = p.sop([("v", ng2)],
                         lambda s: s.activation(w1g, m1p, AF.Sigmoid, bias=negm2))
            w2_d = p.vop([("s", w1_d)], lambda v: v.tensor_scalar(w2g, w1g, -1.0, 1.0,
                                                                  ALU.mult, ALU.add))
            p.vop([], lambda v: v.tensor_scalar(oh1[:, :], oh1[:, :], w1g, None, ALU.mult))
            p.vop([], lambda v: v.tensor_scalar(oh2[:, :], oh2[:, :], w2g, None, ALU.mult))
            p.marks[("geff_d", qb)] = p.vop(
                [], lambda v, qb=qb: v.tensor_add(geff[:, qb * 8:(qb + 1) * 8],
                                                  oh1[:, :], oh2[:, :]))

        p2_v = p.cur("v")
        p2_t = p.cur("t")
        p2_s = p.cur("s")
        nsty2 = p.cur("sty2")
        nstu2 = p.cur("stu2")
        nsts4i = p.cur("sts4i")

        NT3 = TOKQ // 512
        def wgreg(e):
            return RA if e % 2 == 0 else RB

        def wureg(e):
            return RB if e % 2 == 0 else RA

        def emit_wg_load(e):
            def f(g, e=e):
                if e > 0:
                    g.wait_ge(sem["t"], p.marks[("amm", e - 1)])
                else:
                    g.wait_ge(sem["t"], p2_t)
                    g.wait_ge(sem["v"], p2_v)
                    g.wait_ge(sem["s"], p2_s)
                    g.wait_ge(sem["sty2"], nsty2)
                for c in range(8):
                    g.dma_start(out=wgreg(e)[:, c * 4096:(c + 1) * 4096],
                                in_=wgT[e, c * 128:(c + 1) * 128, :]).then_inc(sem["ldwg"], 16)
            p.emit("gpsimd", f)
            return p.inc("ldwg", 16 * 8)

        def emit_wu_load(e):
            def f(g, e=e):
                if e > 0:
                    g.wait_ge(sem["t"], p.marks[("bmm", e - 1)])
                for c in range(8):
                    g.dma_start(out=wureg(e)[:, c * 4096:(c + 1) * 4096],
                                in_=wuT[e, c * 128:(c + 1) * 128, :]).then_inc(sem["ldwu"], 16)
            p.emit("gpsimd", f)
            return p.inc("ldwu", 16 * 8)

        def emit_wd_load(e):
            def f(g, e=e):
                g.wait_ge(sem["t"], p.marks[("amm", e)])
                for j4 in range(8):
                    g.dma_start(
                        out=wgreg(e)[:, j4 * 4096:(j4 + 1) * 4096].rearrange("p (gg d) -> p gg d", gg=4),
                        in_=wdT[e, j4 * 512:(j4 + 1) * 512, :].rearrange("(gg p) d -> p gg d", p=128),
                    ).then_inc(sem["ldwd" if j4 < 4 else "ldwd2"], 16)
            p.emit("gpsimd", f)
            p.inc("ldwd2", 16 * 4)
            return p.inc("ldwd", 16 * 4)

        wg_rdy = [0] * 8
        wu_rdy = [0] * 8
        wd_rdy = [0] * 8
        wg_rdy[0] = emit_wg_load(0)
        wu_rdy[0] = emit_wu_load(0)

        for e in range(8):
            for t_ in range(NT3):
                def f(sy, e=e, t_=t_):
                    j = e * NT3 + t_
                    if j == 0:
                        sy.wait_ge(sem["sty2"], nsty2)
                    else:
                        sy.wait_ge(sem["ldy2"], 16 * j)
                        sy.wait_ge(sem["t"], p.marks[("hmm_t", e, t_ - 1) if t_ > 0 else
                                                     ("hmm_t", e - 1, NT3 - 1)])
                    sy.dma_start(out=y2t[t_ % 2][:, :].rearrange("p (c n) -> p c n", c=8),
                                 in_=y2d.rearrange("(c p) n -> p c n", p=128)
                                 [:, :, t_ * 512:(t_ + 1) * 512]).then_inc(sem["ldy2"], 16)
                p.emit("sync", f)
                ldy2_d = p.inc("ldy2", 16)

                def emit_G(fc, ldy2_d=ldy2_d):
                    gidx = (e * NT3 + t_) * 32 + fc

                    def f(t, e=e, t_=t_, fc=fc, gidx=gidx, ldy2_d=ldy2_d):
                        if fc == 0:
                            t.wait_ge(sem["ldy2"], ldy2_d)
                            t.wait_ge(sem["ldwg"], wg_rdy[e])
                            if e == 0 and t_ == 0:
                                t.wait_ge(sem["v"], p2_v)
                                t.wait_ge(sem["s"], p2_s)
                        if gidx >= 2:
                            t.wait_ge(sem["s"], p.marks[("gsil", gidx - 2)])
                        for c in range(8):
                            ins = t.matmul(pb[gidx % 2][:, 0:512],
                                           wgreg(e)[:, c * 4096 + fc * 128: c * 4096 + (fc + 1) * 128],
                                           y2t[t_ % 2][:, c * 512:(c + 1) * 512],
                                           start=(c == 0), stop=(c == 7))
                        ins.then_inc(sem["t"], 1)
                    p.emit("tensor", f)
                    return p.inc("t", 1)

                def emit_U(fc, ldy2_d=ldy2_d):
                    gidx = (e * NT3 + t_) * 32 + fc

                    def f(t, e=e, t_=t_, fc=fc, gidx=gidx):
                        if fc == 0 and t_ == 0:
                            t.wait_ge(sem["ldwu"], wu_rdy[e])
                        if gidx >= 2:
                            t.wait_ge(sem["v"], p.marks[("hev", gidx - 2)])
                        for c in range(8):
                            ins = t.matmul(pb[2 + gidx % 2][:, 0:512],
                                           wureg(e)[:, c * 4096 + fc * 128: c * 4096 + (fc + 1) * 128],
                                           y2t[t_ % 2][:, c * 512:(c + 1) * 512],
                                           start=(c == 0), stop=(c == 7))
                        ins.then_inc(sem["t"], 1)
                    p.emit("tensor", f)
                    gumm = p.inc("t", 1)
                    if fc == 31:
                        p.marks[("hmm_t", e, t_)] = gumm

                    wv_g = [("t", p.marks[("gmm", gidx)])]
                    if gidx == 0:
                        wv_g.append(("sts4i", nsts4i))
                    if gidx >= 1:
                        wv_g.append(("v", p.marks[("hev", gidx - 1)]))
                    p.marks[("gsil", gidx)] = p.sop(
                        wv_g,
                        lambda s, gidx=gidx: s.activation(
                            gs, pb[gidx % 2][:, 0:512], AF.Silu))
                    wv_h = [("s", p.marks[("gsil", gidx)]), ("t", gumm)]
                    if fc == 0:
                        wv_h.append(("sth", 16 * 2 * (e * NT3 + t_)))
                    elif fc == 16:
                        wv_h.append(("sth", 16 * (2 * (e * NT3 + t_) + 1)))
                    p.marks[("hev", gidx)] = p.vop(
                        wv_h, lambda v, fc=fc, gidx=gidx: v.tensor_mul(
                            h_sb[:, (fc % 16) * 512:((fc % 16) + 1) * 512],
                            gs, pb[2 + gidx % 2][:, 0:512]))
                    if fc == 15 or fc == 31:
                        hf = fc // 16

                        def f(sy, e=e, t_=t_, hf=hf, gidx=gidx):
                            sy.wait_ge(sem["v"], p.marks[("hev", gidx)])
                            sy.wait_ge(sem["sth"], 16 * (2 * (e * NT3 + t_) + hf))
                            sy.dma_start(
                                out=hmd.rearrange("(fc p) n -> p fc n", p=128)
                                [:, hf * 16:(hf + 1) * 16, t_ * 512:(t_ + 1) * 512],
                                in_=h_sb[:, :].rearrange("p (fc n) -> p fc n", fc=16)
                            ).then_inc(sem["sth"], 16)
                        p.emit("sync", f)
                        p.inc("sth", 16)

                for fc in range(32):
                    p.marks[("gmm", (e * NT3 + t_) * 32 + fc)] = emit_G(fc)
                    if fc >= 1:
                        emit_U(fc - 1)
                emit_U(31)
            p.marks[("amm", e)] = p.marks[("hmm_t", e, NT3 - 1)]

            wd_rdy[e] = emit_wd_load(e)
            if e + 1 < 8:
                wg_rdy[e + 1] = emit_wg_load(e + 1)

            NSUB = NT3 * 4
            for t_ in range(NT3):
                for s4i_ in range(4):
                    sub = t_ * 4 + s4i_

                    def f(sy, e=e, t_=t_, s4i_=s4i_, sub=sub, NSUB=NSUB):
                        j = e * NSUB + sub
                        if j > 0:
                            sy.wait_ge(sem["ldh"], 16 * j)
                        if j >= 2:
                            sy.wait_ge(sem["t"], p.marks[("bmm_s", e, sub - 2) if sub >= 2
                                                         else ("bmm_s", e - 1, NSUB - 2 + sub)])
                        sy.wait_ge(sem["sth"], 16 * 2 * NT3 * (e + 1))
                        sy.dma_start(out=hsub[sub % 2][:, :].rearrange("p (fc n) -> p fc n", fc=32),
                                     in_=hmd.rearrange("(fc p) n -> p fc n", p=128)
                                     [:, :, sub * 128:(sub + 1) * 128]).then_inc(sem["ldh"], 16)
                    p.emit("sync", f)
                    ldh_d = p.inc("ldh", 16)

                    def f(t, e=e, sub=sub, ldh_d=ldh_d):
                        t.wait_ge(sem["ldh"], ldh_d)
                        t.wait_ge(sem["ldwd"], wd_rdy[e])
                        if e > 0 or sub >= 1:
                            t.wait_ge(sem["v"], p.marks.get(("oev", e, sub - 1),
                                                            p.marks.get(("oev", e - 1, NSUB - 1), 0)))
                        for half in range(2):
                            bank = pb[4 + half]
                            for fc in range(32):
                                if half == 0 and fc == 16 and sub == 0:
                                    t.wait_ge(sem["ldwd2"], wd_rdy[e])
                                ins = t.matmul(
                                    bank[:, 0:512],
                                    hsub[sub % 2][:, fc * 128:(fc + 1) * 128],
                                    wgreg(e)[:, fc * 1024 + half * 512: fc * 1024 + (half + 1) * 512],
                                    start=(fc == 0), stop=(fc == 31))
                        ins.then_inc(sem["t"], 1)
                    p.emit("tensor", f)
                    p.marks[("bmm_s", e, sub)] = p.inc("t", 1)

                    for half in range(2):
                        wv_ = [("t", p.marks[("bmm_s", e, sub)])] if half == 0 else []
                        if half == 0:
                            ai = e * NSUB + sub
                            if ai >= 2:
                                wv_.append(("acc", 16 * (ai - 1)))
                            if ai == 0:
                                wv_.append(("sts4i", nsts4i))
                                wv_.append(("ldxo", 16 * nq))
                            if ai == 1:
                                wv_.append(("ldu", 16 * nq))
                        p.marks[("oev", e, sub)] = p.vop(
                            wv_, lambda v, e=e, sub=sub, half=half: v.tensor_scalar(
                                otile[sub % 2][:, half * 512:(half + 1) * 512],
                                pb[4 + half][:, 0:512],
                                geff[:, sub * 8 + e: sub * 8 + e + 1], None, ALU.mult))

                    def f(g, e=e, sub=sub, NSUB=NSUB):
                        ai = e * NSUB + sub
                        g.wait_ge(sem["v"], p.marks[("oev", e, sub)])
                        if ai == 0:
                            g.wait_ge(sem["sts4i"], nsts4i)
                        else:
                            g.wait_ge(sem["acc"], 16 * ai)
                        g.dma_start(out=s4[sub * 128:(sub + 1) * 128, :],
                                    in_=otile[sub % 2][:, :],
                                    accum_op=ALU.add).then_inc(sem["acc"], 16)
                    p.emit("gpsimd", f)
                    p.inc("acc", 16)
            p.marks[("bmm", e)] = p.marks[("bmm_s", e, NSUB - 1)]
            if e + 1 < 8:
                wu_rdy[e + 1] = emit_wu_load(e + 1)

        total_acc = p.cur("acc")

        def g_final(g):
            g.wait_ge(sem["acc"], total_acc)
            g.wait_ge(sem["sts4i"], nsts4i)
        p.emit("gpsimd", g_final)



def build_b(nt=NT, e_used=8):
    nc = bass.Bass("TRN2", target_bir_lowering=False)
    dram = {}
    TOKb = nt * 128
    x2 = nc.dram_tensor("x2", [TOKb, D], F32, kind="ExternalInput").ap()
    dram["x2"] = x2
    wr = nc.dram_tensor("wr", [D, 8], F32, kind="ExternalInput").ap()
    dram["wr"] = wr
    br = nc.dram_tensor("br", [1, 8], F32, kind="ExternalInput").ap()
    dram["br"] = br
    lng = nc.dram_tensor("lng", [1, D], F32, kind="ExternalInput").ap()
    dram["lng"] = lng
    lnb = nc.dram_tensor("lnb", [1, D], F32, kind="ExternalInput").ap()
    dram["lnb"] = lnb
    w1 = nc.dram_tensor("w1", [8, D, 4096], F32, kind="ExternalInput").ap()
    dram["w1"] = w1
    b1 = nc.dram_tensor("b1pl", [128, 256], F32, kind="ExternalInput").ap()
    dram["b1pl"] = b1
    w2 = nc.dram_tensor("w2", [8, 4096, D], F32, kind="ExternalInput").ap()
    dram["w2"] = w2
    b2 = nc.dram_tensor("b2", [8, D], F32, kind="ExternalInput").ap()
    dram["b2"] = b2
    eye = nc.dram_tensor("eye", [128, 128], BF16, kind="ExternalInput").ap()
    dram["eye"] = eye
    x3 = nc.dram_tensor("x3", [TOKb, D], F32, kind="ExternalOutput").ap()
    dram["x3"] = x3
    xnd = nc.dram_tensor("xnd", [D, TOKb], BF16).ap()
    dram["xnd"] = xnd
    es = ExitStack()
    with es:
        ps = _mk_ps(nc, es)
        sem = {n: es.enter_context(nc.semaphore(n)) for n in
               ["v", "s", "t", "g"] + ['ldx', 'sti', 'stxn', 'ldi', 'ldw1', 'ldw2', 'ldxn', 'ldxn1', 'acc', 'th', 'to']}
        p = Plan(sem)
        emit_b(nc, p, sem, dram, ps, nt=nt, e_used=e_used)
        _replay(nc, p)
    return nc

def build_a(nq=NT):
    nc = bass.Bass("TRN2", target_bir_lowering=False)
    dram = {}
    NE = nq + 4
    TOKQ = nq * 128
    TOKE = NE * 128
    W = 640
    xe_d = nc.dram_tensor("xe", [TOKE, D], F32, kind="ExternalInput").ap()
    dram["xe"] = xe_d
    wq_d = nc.dram_tensor("wq", [D, D], F32, kind="ExternalInput").ap()
    dram["wq"] = wq_d
    wk_d = nc.dram_tensor("wk", [D, D], F32, kind="ExternalInput").ap()
    dram["wk"] = wk_d
    wv_d = nc.dram_tensor("wv", [D, D], F32, kind="ExternalInput").ap()
    dram["wv"] = wv_d
    wo_d = nc.dram_tensor("wo", [D, D], F32, kind="ExternalInput").ap()
    dram["wo"] = wo_d
    mask_d = nc.dram_tensor("mask", [nq, 128, W], F32, kind="ExternalInput").ap()
    dram["mask"] = mask_d
    eye = nc.dram_tensor("eye", [128, 128], BF16, kind="ExternalInput").ap()
    dram["eye"] = eye
    x1 = nc.dram_tensor("x1", [TOKQ, D], F32, kind="ExternalOutput").ap()
    dram["x1"] = x1
    es = ExitStack()
    with es:
        ps = _mk_ps(nc, es)
        sem = {n: es.enter_context(nc.semaphore(n)) for n in
               ["v", "s", "t", "g"] + ['ldx', 'ldm', 'ldo', 'ldwa', 'ldwo', 'stx']}
        p = Plan(sem)
        emit_a(nc, p, sem, dram, ps, nq=nq)
        _replay(nc, p)
    return nc

def build_c1(nq=NT):
    nc = bass.Bass("TRN2", target_bir_lowering=False)
    dram = {}
    TOKQ = nq * 128
    NTI = nq + 1
    DH = 2048
    CC = DH // 128
    xh = nc.dram_tensor("xh", [NTI * 128, D], F32, kind="ExternalInput").ap()
    dram["xh"] = xh
    w1T = nc.dram_tensor("w1T", [D, D], F32, kind="ExternalInput").ap()
    dram["w1T"] = w1T
    dlT = nc.dram_tensor("dlT", [D, DH], F32, kind="ExternalInput").ap()
    dram["dlT"] = dlT
    awT = nc.dram_tensor("awT", [D, DH], F32, kind="ExternalInput").ap()
    dram["awT"] = awT
    bwT = nc.dram_tensor("bwT", [D, DH], F32, kind="ExternalInput").ap()
    dram["bwT"] = bwT
    cwpl = nc.dram_tensor("cwpl", [128, 24], F32, kind="ExternalInput").ap()
    dram["cwpl"] = cwpl
    cbpl = nc.dram_tensor("cbpl", [128, 8], F32, kind="ExternalInput").ap()
    dram["cbpl"] = cbpl
    abpl = nc.dram_tensor("abpl", [128, CC], F32, kind="ExternalInput").ap()
    dram["abpl"] = abpl
    bbpl = nc.dram_tensor("bbpl", [128, CC], F32, kind="ExternalInput").ap()
    dram["bbpl"] = bbpl
    dbpl = nc.dram_tensor("dbpl", [128, CC], F32, kind="ExternalInput").ap()
    dram["dbpl"] = dbpl
    eye = nc.dram_tensor("eye", [128, 128], BF16, kind="ExternalInput").ap()
    dram["eye"] = eye
    hd = nc.dram_tensor("hd", [DH, TOKQ], F32, kind="ExternalOutput").ap()
    dram["hd"] = hd
    x3d = nc.dram_tensor("x3d", [D, TOKQ], BF16, kind="ExternalOutput").ap()
    dram["x3d"] = x3d
    totd = nc.dram_tensor("totd", [128, 2 * CC], F32, kind="ExternalOutput").ap()
    dram["totd"] = totd
    dram["xhh"] = dram["xh"][nq * 128:nq * 128 + 2, :]
    flh = nc.dram_tensor("flhal", [2, 1], F32, kind="ExternalInput").ap()
    dram["flhal"] = flh
    es = ExitStack()
    with es:
        ps = _mk_ps(nc, es)
        sem = {n: es.enter_context(nc.semaphore(n)) for n in
               ["v", "s", "t", "g"] + ['ldx', 'ldw', 'ldwc', 'sth', 'stx3', 'sttot']}
        p = Plan(sem)
        emit_c1(nc, p, sem, dram, ps, nq=nq)
        _replay(nc, p)
    return nc

def build_c2(nq=NT):
    nc = bass.Bass("TRN2", target_bir_lowering=False)
    dram = {}
    TOKQ = nq * 128
    DH = 2048
    CC = DH // 128
    hd = nc.dram_tensor("hd", [DH, TOKQ], F32, kind="ExternalInput").ap()
    dram["hd"] = hd
    x3d = nc.dram_tensor("x3d", [D, TOKQ], BF16, kind="ExternalInput").ap()
    dram["x3d"] = x3d
    carA = nc.dram_tensor("carA", [128, CC], F32, kind="ExternalInput").ap()
    dram["carA"] = carA
    carH = nc.dram_tensor("carH", [128, CC], F32, kind="ExternalInput").ap()
    dram["carH"] = carH
    x2o = nc.dram_tensor("x2o", [TOKQ, D], F32, kind="ExternalInput").ap()
    dram["x2o"] = x2o
    cwT = nc.dram_tensor("cwT", [DH, D], F32, kind="ExternalInput").ap()
    dram["cwT"] = cwT
    dwT = nc.dram_tensor("dwT", [D, D], F32, kind="ExternalInput").ap()
    dram["dwT"] = dwT
    w2T = nc.dram_tensor("w2T", [D, D], F32, kind="ExternalInput").ap()
    dram["w2T"] = w2T
    w3T = nc.dram_tensor("w3T", [D, D], F32, kind="ExternalInput").ap()
    dram["w3T"] = w3T
    cbr = nc.dram_tensor("cbr", [1, D], F32, kind="ExternalInput").ap()
    dram["cbr"] = cbr
    dbr = nc.dram_tensor("dbr", [1, D], F32, kind="ExternalInput").ap()
    dram["dbr"] = dbr
    eye = nc.dram_tensor("eye", [128, 128], BF16, kind="ExternalInput").ap()
    dram["eye"] = eye
    x3s = nc.dram_tensor("x3s", [TOKQ, D], F32, kind="ExternalOutput").ap()
    dram["x3s"] = x3s
    es = ExitStack()
    with es:
        ps = _mk_ps(nc, es)
        sem = {n: es.enter_context(nc.semaphore(n)) for n in
               ["v", "s", "t", "g"] + ['ldw', 'ldh', 'ldx3', 'ldx2', 'stx']}
        p = Plan(sem)
        emit_c2(nc, p, sem, dram, ps, nq=nq)
        _replay(nc, p)
    return nc

def build_d(nq=NT):
    nc = bass.Bass("TRN2", target_bir_lowering=False)
    dram = {}
    TOKQ = nq * 128
    NB = 2 * nq
    TOKB = NB * 128
    xr = nc.dram_tensor("xr", [TOKB, D], F32, kind="ExternalInput").ap()
    dram["xr"] = xr
    qkvT = nc.dram_tensor("qkvT", [D, 3072], F32, kind="ExternalInput").ap()
    dram["qkvT"] = qkvT
    projT = nc.dram_tensor("projT", [D, D], F32, kind="ExternalInput").ap()
    dram["projT"] = projT
    rwT = nc.dram_tensor("rwT", [D, 8], F32, kind="ExternalInput").ap()
    dram["rwT"] = rwT
    n1r = nc.dram_tensor("n1r", [1, D], F32, kind="ExternalInput").ap()
    dram["n1r"] = n1r
    n2r = nc.dram_tensor("n2r", [1, D], F32, kind="ExternalInput").ap()
    dram["n2r"] = n2r
    wgT = nc.dram_tensor("wgT", [8, D, 4096], F32, kind="ExternalInput").ap()
    dram["wgT"] = wgT
    wuT = nc.dram_tensor("wuT", [8, D, 4096], F32, kind="ExternalInput").ap()
    dram["wuT"] = wuT
    wdT = nc.dram_tensor("wdT", [8, 4096, D], F32, kind="ExternalInput").ap()
    dram["wdT"] = wdT
    cosq = nc.dram_tensor("cosq", [128, TOKQ], F32, kind="ExternalInput").ap()
    dram["cosq"] = cosq
    sinq = nc.dram_tensor("sinq", [128, TOKQ], F32, kind="ExternalInput").ap()
    dram["sinq"] = sinq
    cosk = nc.dram_tensor("cosk", [128, TOKB], F32, kind="ExternalInput").ap()
    dram["cosk"] = cosk
    sink = nc.dram_tensor("sink", [128, TOKB], F32, kind="ExternalInput").ap()
    dram["sink"] = sink
    m128 = nc.dram_tensor("m128", [128, 128], F32, kind="ExternalInput").ap()
    dram["m128"] = m128
    cmask = nc.dram_tensor("cmask", [128, 128], F32, kind="ExternalInput").ap()
    dram["cmask"] = cmask
    flagd = nc.dram_tensor("flagd", [128, 1], F32, kind="ExternalInput").ap()
    dram["flagd"] = flagd
    eye = nc.dram_tensor("eye", [128, 128], BF16, kind="ExternalInput").ap()
    dram["eye"] = eye
    s4 = nc.dram_tensor("s4", [TOKQ, D], F32, kind="ExternalOutput").ap()
    dram["s4"] = s4
    kd = nc.dram_tensor("kd", [D, TOKB], BF16).ap()
    dram["kd"] = kd
    vd = nc.dram_tensor("vd", [TOKB, D], BF16).ap()
    dram["vd"] = vd
    qd = nc.dram_tensor("qd", [D, TOKQ], BF16).ap()
    dram["qd"] = qd
    ud = nc.dram_tensor("ud", [TOKQ, D], F32).ap()
    dram["ud"] = ud
    u2d = nc.dram_tensor("u2d", [TOKQ, D], F32).ap()
    dram["u2d"] = u2d
    y2d = nc.dram_tensor("y2d", [D, TOKQ], BF16).ap()
    dram["y2d"] = y2d
    hmd = nc.dram_tensor("hmd", [4096, TOKQ], BF16).ap()
    dram["hmd"] = hmd
    dram["xr1"] = dram["xr"]
    dram["xr2"] = dram["xr"][nq * 128:2 * nq * 128, :]
    es = ExitStack()
    with es:
        ps = _mk_ps(nc, es)
        sem = {n: es.enter_context(nc.semaphore(n)) for n in
               ["v", "s", "t", "g"] + ['ldx', 'ldw', 'ldwg', 'ldwu', 'ldwd', 'ldwd2', 'ldt', 'stk', 'stv', 'stq', 'stu', 'ldhp', 'ldhp1', 'ldq2', 'stu2', 'sty2', 'sts4i', 'ldu', 'ldxo', 'ldy2', 'sth', 'ldh', 'ldw3', 'acc']}
        p = Plan(sem)
        emit_d(nc, p, sem, dram, ps, nq=nq)
        _replay(nc, p)
    return nc

def build_fused(nq=NT):
    nc = bass.Bass("TRN2", target_bir_lowering=False)
    TOKQ = nq * 128
    NE = nq + 4
    DH = 2048
    CC = DH // 128
    dram = {}

    def ext(name, shape):
        dram[name] = nc.dram_tensor(name, shape, F32, kind="ExternalInput").ap()

    ext("xe", [(NE) * 128, D])
    for n in ("wq", "wk", "wv", "wo"):
        ext(n, [D, D])
    dram["mask"] = nc.dram_tensor("mask", [nq, 128, 640], F32, kind="ExternalInput").ap()
    dram["eye"] = nc.dram_tensor("eye", [128, 128], BF16, kind="ExternalInput").ap()
    ext("wr", [D, 8]); ext("br", [1, 8]); ext("lng", [1, D]); ext("lnb", [1, D])
    ext("w1", [8, D, 4096]); ext("b1pl", [128, 256]); ext("w2", [8, 4096, D]); ext("b2", [8, D])
    ext("w1T", [D, D]); ext("dlT", [D, DH]); ext("awT", [D, DH]); ext("bwT", [D, DH])
    ext("cwpl", [128, 24]); ext("cbpl", [128, 8]); ext("abpl", [128, CC])
    ext("bbpl", [128, CC]); ext("dbpl", [128, CC])
    ext("cwT", [DH, D]); ext("dwT", [D, D]); ext("w2T", [D, D]); ext("w3T", [D, D])
    ext("cbr", [1, D]); ext("dbr", [1, D])
    ext("qkvT", [D, 3072]); ext("projT", [D, D]); ext("rwT", [D, 8])
    ext("n1r", [1, D]); ext("n2r", [1, D])
    ext("wgT", [8, D, 4096]); ext("wuT", [8, D, 4096]); ext("wdT", [8, 4096, D])
    ext("cosq", [128, TOKQ]); ext("sinq", [128, TOKQ])
    ext("cosk", [128, 2 * TOKQ]); ext("sink", [128, 2 * TOKQ])
    ext("m128", [128, 128]); ext("cmask", [128, 128]); ext("flagd", [128, 1])
    ext("flhal", [2, 1]); ext("flcar", [128, 1]); ext("flcar2", [128, 1])

    x1d = nc.dram_tensor("x1d", [TOKQ, D], F32).ap()
    x2d = nc.dram_tensor("x2d", [TOKQ, D], F32).ap()
    x3sd = nc.dram_tensor("x3sd", [TOKQ, D], F32).ap()
    dram["xnd"] = nc.dram_tensor("xnd", [D, TOKQ], BF16).ap()
    dram["hd"] = nc.dram_tensor("hd", [DH, TOKQ], F32).ap()
    dram["x3d"] = nc.dram_tensor("x3d", [D, TOKQ], BF16).ap()
    dram["totd"] = nc.dram_tensor("totd", [128, 2 * CC], F32).ap()
    dram["carA"] = nc.dram_tensor("carAd", [128, CC], F32).ap()
    dram["carH"] = nc.dram_tensor("carHd", [128, CC], F32).ap()
    for n, shape, dt in (("kd", [D, 2 * TOKQ], BF16), ("vd", [2 * TOKQ, D], BF16),
                         ("qd", [D, TOKQ], BF16), ("ud", [TOKQ, D], F32),
                         ("u2d", [TOKQ, D], F32), ("y2d", [D, TOKQ], BF16),
                         ("hmd", [4096, TOKQ], BF16)):
        dram[n] = nc.dram_tensor(n, shape, dt).ap()
    cc1i = nc.dram_tensor("cc1i", [2, D], F32).ap()
    cc1o = nc.dram_tensor("cc1o", [4, D], F32).ap()
    cc2o = nc.dram_tensor("cc2o", [256, 2 * CC], F32).ap()
    cc3o = nc.dram_tensor("cc3o", [2 * TOKQ, D], F32).ap()


    dram["x1"] = x1d
    dram["x2"] = x1d
    dram["x3"] = x2d
    dram["xh"] = x2d
    dram["xhh"] = cc1o[1:3, :]
    dram["x2o"] = x2d
    dram["x3s"] = x3sd
    dram["xr1"] = cc3o
    dram["xr2"] = x3sd
    dram["s4"] = nc.dram_tensor("s4", [TOKQ, D], F32, kind="ExternalOutput").ap()

    RG = [[0, 1], [2, 3], [4, 5], [6, 7]]
    es = ExitStack()
    with es:
        ps = _mk_ps(nc, es)
        shared = {n: es.enter_context(nc.semaphore(n)) for n in
                  ["v", "s", "t", "g", "gl1", "gl2", "gl3", "gl4",
                   "cc1", "cc2", "cc3"]}
        allsem = dict(shared)
        P = Plan(shared)

        carbuf = es.enter_context(nc.sbuf_tensor("carbuf", [128, 2 * CC], F32))
        carout = es.enter_context(nc.sbuf_tensor("carout", [128, 2 * CC], F32))
        flA = es.enter_context(nc.sbuf_tensor("flAf", [128, 1], F32))
        flB = es.enter_context(nc.sbuf_tensor("flBf", [128, 1], F32))

        def g_init(g):
            g.dma_start(out=flA[:, :], in_=dram["flcar"][:, :]).then_inc(shared["gl1"], 16)
            g.dma_start(out=flB[:, :], in_=dram["flcar2"][:, :]).then_inc(shared["gl1"], 16)
        P.emit("gpsimd", g_init)
        gl_init = P.inc("gl1", 32)

        def join():
            waits = [(k, v) for k, v in P.counts.items() if v > 0 and k in allsem]
            for eng in Plan.ENGINES:
                def f(h, waits=tuple(waits)):
                    for n, val in waits:
                        h.wait_ge(allsem[n], val)
                P.emit(eng, f)

        def mksub(prefix, names):
            local = {n: shared[n] for n in ("v", "s", "t", "g")}
            for n in names:
                h = es.enter_context(nc.semaphore(prefix + n))
                local[n] = h
                allsem[prefix + n] = h
            return SubPlan(P, prefix, local)

        pa = mksub("a_", ["ldx", "ldm", "ldo", "ldwa", "ldwo", "stx"])
        emit_a(nc, pa, pa.sem, dram, ps, nq=nq)
        join()

        pb_ = mksub("b_", ["ldx", "sti", "stxn", "ldi", "ldw1", "ldw2", "ldxn",
                           "ldxn1", "acc", "th", "to"])
        emit_b(nc, pb_, pb_.sem, dram, ps, nt=nq, e_used=8)
        join()

        def f(sy):
            sy.dma_start(out=cc1i[0:1, :], in_=x2d[0:1, :]).then_inc(shared["gl2"], 16)
            sy.dma_start(out=cc1i[1:2, :], in_=x2d[TOKQ - 1:TOKQ, :]).then_inc(shared["gl2"], 16)
        P.emit("sync", f)
        gl2a = P.inc("gl2", 32)

        def f(g, gl2a=gl2a):
            g.wait_ge(shared["gl2"], gl2a)
            g.collective_compute("AllGather", ALU.bypass, replica_groups=RG,
                                 ins=[cc1i.opt()], outs=[cc1o.opt()]).then_inc(shared["cc1"])
            g.wait_ge(shared["cc1"], 1)
        P.emit("gpsimd", f)
        P.inc("cc1", 1)
        join()

        pc = mksub("c_", ["ldx", "ldw", "ldwc", "sth", "stx3", "sttot"])
        emit_c1(nc, pc, pc.sem, dram, ps, nq=nq)
        join()

        def f(g):
            g.collective_compute("AllGather", ALU.bypass, replica_groups=RG,
                                 ins=[dram["totd"].opt()], outs=[cc2o.opt()]).then_inc(shared["cc2"])
            g.wait_ge(shared["cc2"], 1)
        P.emit("gpsimd", f)
        P.inc("cc2", 1)

        def f(sy):
            sy.wait_ge(shared["cc2"], 1)
            sy.dma_start(out=carbuf[:, :], in_=cc2o[0:128, :]).then_inc(shared["gl3"], 16)
        P.emit("sync", f)
        gl3a = P.inc("gl3", 16)

        P.vop([("gl3", gl3a)],
              lambda v: v.tensor_scalar(carout[:, 0:CC], carbuf[:, 0:CC], flA[:, 0:1],
                                        flB[:, 0:1], ALU.mult, ALU.add))
        car_d = P.vop([], lambda v: v.tensor_scalar(carout[:, CC:2 * CC],
                                                    carbuf[:, CC:2 * CC], flA[:, 0:1],
                                                    None, ALU.mult))

        def f(sy, car_d=car_d):
            sy.wait_ge(shared["v"], car_d)
            sy.dma_start(out=dram["carA"][:, :], in_=carout[:, 0:CC]).then_inc(shared["gl4"], 16)
            sy.dma_start(out=dram["carH"][:, :], in_=carout[:, CC:2 * CC]).then_inc(shared["gl4"], 16)
        P.emit("sync", f)
        P.inc("gl4", 32)
        join()

        pe = mksub("e_", ["ldw", "ldh", "ldx3", "ldx2", "stx"])
        emit_c2(nc, pe, pe.sem, dram, ps, nq=nq)
        join()

        def f(g):
            g.collective_compute("AllGather", ALU.bypass, replica_groups=RG,
                                 ins=[x3sd.opt()], outs=[cc3o.opt()]).then_inc(shared["cc3"])
            g.wait_ge(shared["cc3"], 1)
        P.emit("gpsimd", f)
        P.inc("cc3", 1)
        join()

        pd = mksub("d_", ["ldx", "ldw", "ldwg", "ldwu", "ldwd", "ldwd2", "ldt",
                          "stk", "stv", "stq", "stu", "ldhp", "ldhp1", "ldq2",
                          "stu2", "sty2", "sts4i", "ldu", "ldxo", "ldy2", "sth",
                          "ldh", "ldw3", "acc"])
        emit_d(nc, pd, pd.sem, dram, ps, nq=nq)

        _replay(nc, P)
    return nc


def fused_in_maps(inputs, nq=NT):
    inp = {k: np.asarray(v) for k, v in inputs.items()}
    x0 = inp["L0_attn_hidden_states"].astype(np.float32)
    TOKQ = nq * 128
    S = 2 * TOKQ
    eye = _eye128()
    cm = np.where(np.arange(128)[None, :] <= np.arange(128)[:, None], 0.0,
                  -1e30).astype(np.float32)
    m128 = rot_matrix()
    maps = []
    for c in range(NCORES):
        b, h = c // 2, c % 2
        s0 = h * TOKQ
        lo, hi = s0 - 256, s0 + TOKQ + 256
        xe = np.zeros((TOKQ + 512, D), np.float32)
        cl, ch = max(lo, 0), min(hi, S)
        xe[cl - lo:ch - lo, :] = x0[b, cl:ch, :]
        qpos = np.arange(s0, s0 + TOKQ)
        kpos = np.concatenate([np.arange(0, TOKQ), np.arange(s0, s0 + TOKQ)])
        cq, sq = rope_tables(qpos, scale=0.125)
        ck, sk = rope_tables(kpos, scale=1.0)
        flag = (np.full((128, 1), -1e30, np.float32) if h == 0
                else np.zeros((128, 1), np.float32))
        flhal = np.array([[1.0 if h == 1 else 0.0],
                          [1.0 if h == 0 else 0.0]], np.float32)
        flcar = np.full((128, 1), 1.0 if h == 1 else 0.0, np.float32)
        maps.append({
            "xe": xe,
            "wq": inp["L0_attn_w_q"].astype(np.float32),
            "wk": inp["L0_attn_w_k"].astype(np.float32),
            "wv": inp["L0_attn_w_v"].astype(np.float32),
            "wo": inp["L0_attn_w_o"].astype(np.float32),
            "mask": a_masks(S=S, nq=nq, s0=s0),
            "eye": eye,
            "wr": inp["L0_ffn_Wr"].astype(np.float32),
            "br": inp["L0_ffn_br"].reshape(1, 8).astype(np.float32),
            "lng": inp["L0_ffn_ln_g"].reshape(1, D).astype(np.float32),
            "lnb": inp["L0_ffn_ln_b"].reshape(1, D).astype(np.float32),
            "w1": inp["L0_ffn_W1"].astype(np.float32),
            "b1pl": _b1pl(inp["L0_ffn_b1"]),
            "w2": inp["L0_ffn_W2"].astype(np.float32),
            "b2": inp["L0_ffn_b2"].astype(np.float32),
            "w1T": np.ascontiguousarray(inp["L1_attn_w1"].T).astype(np.float32),
            "dlT": np.ascontiguousarray(inp["L1_attn_dl_w"].T).astype(np.float32),
            "awT": np.ascontiguousarray(inp["L1_attn_A_w"].T).astype(np.float32),
            "bwT": np.ascontiguousarray(inp["L1_attn_B_w"].T).astype(np.float32),
            "cwpl": inp["L1_attn_conv_w"].reshape(1024, 3).reshape(8, 128, 3)
                    .transpose(1, 0, 2).reshape(128, 24).copy(),
            "cbpl": inp["L1_attn_conv_b"].reshape(8, 128).T.copy(),
            "abpl": inp["L1_attn_A_b"].reshape(16, 128).T.copy(),
            "bbpl": inp["L1_attn_B_b"].reshape(16, 128).T.copy(),
            "dbpl": inp["L1_attn_dl_b"].reshape(16, 128).T.copy(),
            "cwT": np.ascontiguousarray(inp["L1_attn_C_w"].T).astype(np.float32),
            "dwT": np.ascontiguousarray(inp["L1_attn_D_w"].T).astype(np.float32),
            "w2T": np.ascontiguousarray(inp["L1_attn_w2"].T).astype(np.float32),
            "w3T": np.ascontiguousarray(inp["L1_attn_w3"].T).astype(np.float32),
            "cbr": inp["L1_attn_C_b"].reshape(1, D).astype(np.float32),
            "dbr": inp["L1_attn_D_b"].reshape(1, D).astype(np.float32),
            "qkvT": np.ascontiguousarray(inp["L1_ffn_qkv_w"].T).astype(np.float32),
            "projT": np.ascontiguousarray(inp["L1_ffn_proj_w"].T).astype(np.float32),
            "rwT": np.ascontiguousarray(inp["L1_ffn_router_w"].T).astype(np.float32),
            "n1r": inp["L1_ffn_norm1_w"].reshape(1, D).astype(np.float32),
            "n2r": inp["L1_ffn_norm2_w"].reshape(1, D).astype(np.float32),
            "wgT": np.ascontiguousarray(inp["L1_ffn_wg"].transpose(0, 2, 1)).astype(np.float32),
            "wuT": np.ascontiguousarray(inp["L1_ffn_wu"].transpose(0, 2, 1)).astype(np.float32),
            "wdT": np.ascontiguousarray(inp["L1_ffn_wd"].transpose(0, 2, 1)).astype(np.float32),
            "cosq": cq, "sinq": sq, "cosk": ck, "sink": sk,
            "m128": m128, "cmask": cm, "flagd": flag,
            "flhal": flhal, "flcar": flcar, "flcar2": 1.0 - flcar,
        })
    return maps


def kernel_fused(**inputs):
    nc = _get_nc("fused", build_fused)
    maps = fused_in_maps(inputs)
    res = run_bass_kernel_spmd(nc, maps, core_ids=list(range(NCORES)))
    return _gather(res.results, "s4")



_EYE = None


def _eye128():
    global _EYE
    if _EYE is None:
        import ml_dtypes
        _EYE = np.eye(128, dtype=ml_dtypes.bfloat16)
    return _EYE


def _b1pl(b1):
    return np.ascontiguousarray(b1.reshape(8, 32, 128).transpose(2, 0, 1)
                                .reshape(128, 256)).astype(np.float32)


def b_in_maps(x2_full, inp):
    maps = []
    for c in range(NCORES):
        b, h = c // 2, c % 2
        rows = np.ascontiguousarray(x2_full[b, h * TOK:(h + 1) * TOK, :]).astype(np.float32)
        maps.append({
            "x2": rows,
            "wr": inp["L0_ffn_Wr"].astype(np.float32),
            "br": inp["L0_ffn_br"].reshape(1, 8).astype(np.float32),
            "lng": inp["L0_ffn_ln_g"].reshape(1, D).astype(np.float32),
            "lnb": inp["L0_ffn_ln_b"].reshape(1, D).astype(np.float32),
            "w1": inp["L0_ffn_W1"].astype(np.float32),
            "b1pl": _b1pl(inp["L0_ffn_b1"]),
            "w2": inp["L0_ffn_W2"].astype(np.float32),
            "b2": inp["L0_ffn_b2"].astype(np.float32),
            "eye": _eye128(),
        })
    return maps


def rope_tables(pos, scale=1.0):
    inv_freq = 1.0 / (10000.0 ** (np.arange(0, 64, 2, dtype=np.float64) / 64.0))
    freqs = np.outer(pos.astype(np.float64), inv_freq)
    emb = np.concatenate([freqs, freqs], axis=-1)
    cos = (np.cos(emb).T * scale).astype(np.float32)
    sin = (np.sin(emb).T * scale).astype(np.float32)
    return np.vstack([cos, cos]).copy(), np.vstack([sin, sin]).copy()


def rot_matrix():
    M = np.zeros((128, 128), np.float32)
    for b in (0, 64):
        for r in range(32):
            M[b + r, b + 32 + r] = -1.0
            M[b + 32 + r, b + r] = 1.0
    return np.ascontiguousarray(M.T)


def d_in_maps(x3s_full, inp, nq=NT):
    TOKQ = nq * 128
    eye = _eye128()
    cm = np.where(np.arange(128)[None, :] <= np.arange(128)[:, None], 0.0,
                  -1e30).astype(np.float32)
    m128 = rot_matrix()
    maps = []
    for c in range(NCORES):
        b, h = c // 2, c % 2
        if h == 0:
            xr = np.concatenate([x3s_full[b, TOKQ:2 * TOKQ], x3s_full[b, 0:TOKQ]], 0)
            flag = np.full((128, 1), -1e30, np.float32)
            kpos = np.concatenate([np.arange(TOKQ, 2 * TOKQ), np.arange(0, TOKQ)])
            qpos = np.arange(0, TOKQ)
        else:
            xr = x3s_full[b, 0:2 * TOKQ]
            flag = np.zeros((128, 1), np.float32)
            kpos = np.arange(0, 2 * TOKQ)
            qpos = np.arange(TOKQ, 2 * TOKQ)
        cq, sq = rope_tables(qpos, scale=0.125)
        ck, sk = rope_tables(kpos, scale=1.0)
        maps.append({
            "xr": np.ascontiguousarray(xr).astype(np.float32),
            "qkvT": np.ascontiguousarray(inp["L1_ffn_qkv_w"].T).astype(np.float32),
            "projT": np.ascontiguousarray(inp["L1_ffn_proj_w"].T).astype(np.float32),
            "rwT": np.ascontiguousarray(inp["L1_ffn_router_w"].T).astype(np.float32),
            "n1r": inp["L1_ffn_norm1_w"].reshape(1, D).astype(np.float32),
            "n2r": inp["L1_ffn_norm2_w"].reshape(1, D).astype(np.float32),
            "wgT": np.ascontiguousarray(inp["L1_ffn_wg"].transpose(0, 2, 1)).astype(np.float32),
            "wuT": np.ascontiguousarray(inp["L1_ffn_wu"].transpose(0, 2, 1)).astype(np.float32),
            "wdT": np.ascontiguousarray(inp["L1_ffn_wd"].transpose(0, 2, 1)).astype(np.float32),
            "cosq": cq, "sinq": sq, "cosk": ck, "sink": sk,
            "m128": m128, "cmask": cm, "flagd": flag, "eye": eye,
        })
    return maps


def c1_in_maps(x2s_full, inp):
    eye = _eye128()
    maps = []
    S = 4096
    for c in range(NCORES):
        b, h = c // 2, c % 2
        s0 = h * TOK
        xh = np.zeros((TOK + 128, D), np.float32)
        xh[0:TOK] = x2s_full[b, s0:s0 + TOK]
        if s0 - 1 >= 0:
            xh[TOK] = x2s_full[b, s0 - 1]
        if s0 + TOK < S:
            xh[TOK + 1] = x2s_full[b, s0 + TOK]
        maps.append({
            "xh": xh,
            "w1T": np.ascontiguousarray(inp["L1_attn_w1"].T).astype(np.float32),
            "dlT": np.ascontiguousarray(inp["L1_attn_dl_w"].T).astype(np.float32),
            "awT": np.ascontiguousarray(inp["L1_attn_A_w"].T).astype(np.float32),
            "bwT": np.ascontiguousarray(inp["L1_attn_B_w"].T).astype(np.float32),
            "cwpl": inp["L1_attn_conv_w"].reshape(1024, 3).reshape(8, 128, 3)
                    .transpose(1, 0, 2).reshape(128, 24).copy(),
            "cbpl": inp["L1_attn_conv_b"].reshape(8, 128).T.copy(),
            "abpl": inp["L1_attn_A_b"].reshape(16, 128).T.copy(),
            "bbpl": inp["L1_attn_B_b"].reshape(16, 128).T.copy(),
            "dbpl": inp["L1_attn_dl_b"].reshape(16, 128).T.copy(),
            "eye": eye,
            "flhal": np.ones((2, 1), np.float32),
        })
    return maps


def c2_in_maps(x2s_full, inp, c1_results):
    eye = _eye128()
    maps = []
    for c in range(NCORES):
        b, h = c // 2, c % 2
        s0 = h * TOK
        if h == 0:
            carA = np.ones((128, 16), np.float32)
            carH = np.zeros((128, 16), np.float32)
        else:
            tot = c1_results[c - 1]["totd"]
            carA = np.ascontiguousarray(tot[:, 0:16])
            carH = np.ascontiguousarray(tot[:, 16:32])
        maps.append({
            "hd": c1_results[c]["hd"], "x3d": c1_results[c]["x3d"],
            "carA": carA, "carH": carH,
            "x2o": np.ascontiguousarray(x2s_full[b, s0:s0 + TOK]).astype(np.float32),
            "cwT": np.ascontiguousarray(inp["L1_attn_C_w"].T).astype(np.float32),
            "dwT": np.ascontiguousarray(inp["L1_attn_D_w"].T).astype(np.float32),
            "w2T": np.ascontiguousarray(inp["L1_attn_w2"].T).astype(np.float32),
            "w3T": np.ascontiguousarray(inp["L1_attn_w3"].T).astype(np.float32),
            "cbr": inp["L1_attn_C_b"].reshape(1, D).astype(np.float32),
            "dbr": inp["L1_attn_D_b"].reshape(1, D).astype(np.float32),
            "eye": eye,
        })
    return maps


def _gather(results, key):
    out = np.empty((4, 4096, D), np.float32)
    for c in range(NCORES):
        b, h = c // 2, c % 2
        out[b, h * TOK:(h + 1) * TOK, :] = results[c][key]
    return out


_NC_CACHE = {}


def _get_nc(name, builder):
    if name not in _NC_CACHE:
        _NC_CACHE[name] = builder()
    return _NC_CACHE[name]


def kernel(**inputs):
    return kernel_fused(**inputs)


def kernel_unfused(**inputs):
    inp = {k: np.asarray(v) for k, v in inputs.items()}
    x0 = inp["L0_attn_hidden_states"].astype(np.float32)
    cores = list(range(NCORES))

    nc_a = _get_nc("a", build_a)
    res = run_bass_kernel_spmd(nc_a, a_in_maps(x0, inp), core_ids=cores)
    x1 = _gather(res.results, "x1")

    nc_b = _get_nc("b", build_b)
    res = run_bass_kernel_spmd(nc_b, b_in_maps(x1, inp), core_ids=cores)
    x2 = _gather(res.results, "x3")

    nc_c1 = _get_nc("c1", build_c1)
    res1 = run_bass_kernel_spmd(nc_c1, c1_in_maps(x2, inp), core_ids=cores)

    nc_c2 = _get_nc("c2", build_c2)
    res = run_bass_kernel_spmd(nc_c2, c2_in_maps(x2, inp, res1.results), core_ids=cores)
    x3 = _gather(res.results, "x3s")

    nc_d = _get_nc("d", build_d)
    res = run_bass_kernel_spmd(nc_d, d_in_maps(x3, inp), core_ids=cores)
    return _gather(res.results, "s4")
